# version 05
# baseline (speedup 1.0000x reference)
import math
from contextlib import ExitStack

import numpy as np
import ml_dtypes

import concourse.bass as bass
import concourse.mybir as mybir
from concourse.alu_op_type import AluOpType
from concourse.bass_utils import run_bass_kernel_spmd

F32 = mybir.dt.float32
BF16 = mybir.dt.bfloat16
U8 = mybir.dt.uint8
AF = mybir.ActivationFunctionType
OP = AluOpType
BF = ml_dtypes.bfloat16

B, S, D = 2, 4096, 2048
KC = D // 128
BT = 256
NBLK = S // BT
TOK = 1024
HALO = 3
BW = HALO + BT
XW = 4 * BW
NCORE, GROUP = 8, 4
EPS = 1e-5

def blocks_of(g):
    return sorted([g, 7 - g, 8 + g, 15 - g])

BLK_OWNER = {}
for _g in range(GROUP):
    for _s, _b in enumerate(blocks_of(_g)):
        BLK_OWNER[_b] = (_g, _s)


def _u8_view(w_int32):
    return np.ascontiguousarray(
        w_int32.view(np.uint8).reshape(w_int32.shape + (4,))[..., 0])

def _wlay(w, kcn, mcn):
    return np.ascontiguousarray(
        np.asarray(w).reshape(kcn, 128, mcn, 128)
        .transpose(1, 2, 0, 3).reshape(128, -1))

def _rope_tables():
    dh = 128
    inv = 1.0 / (10000.0 ** (np.arange(0, dh, 2, dtype=np.float64) / dh))
    fr = np.outer(np.arange(S, dtype=np.float64), inv)
    cos = np.concatenate([np.cos(fr), np.cos(fr)], axis=1)
    sin = np.concatenate([np.sin(fr), np.sin(fr)], axis=1)
    sinm = sin.copy()
    sinm[:, :64] *= -1.0
    return cos.astype(np.float32), sinm.astype(np.float32)

def core_tokens(core):
    g = core % GROUP
    return np.concatenate([np.arange(b * BT, (b + 1) * BT) for b in blocks_of(g)])

def shard_resid(x):
    return [np.ascontiguousarray(x[c // GROUP][core_tokens(c)].T) for c in range(NCORE)]

def unshard_resid(shards):
    x = np.empty((B, S, D), np.float32)
    for c in range(NCORE):
        x[c // GROUP][core_tokens(c)] = np.asarray(shards[c], np.float32).T
    return x


class Sem:
    __slots__ = ("h", "n", "w")
    def __init__(self, h):
        self.h, self.n, self.w = h, 0, {}

class Ctx:
    SEMS = ("ld", "ld2", "mm", "ep", "va", "vb", "sa", "sb", "ga", "cc",
            "lda", "ldb", "ldc", "ldd", "lde", "ldf", "ste", "stf",
            "bt", "bv", "bs", "bg", "by")

    def __init__(self, nc, block, es):
        self.nc, self.block, self.es = nc, block, es
        self.S = {n: Sem(es.enter_context(nc.semaphore(n))) for n in self.SEMS}
        self._scopes = []
        self._i = 0

    def push(self):
        st = ExitStack()
        self._scopes.append(st)

    def pop(self):
        self._scopes.pop().close()

    def _stack(self):
        return self._scopes[-1] if self._scopes else self.es

    def sbuf(self, name, shape, dt):
        self._i += 1
        return self._stack().enter_context(
            self.nc.sbuf_tensor(f"{name}_{self._i}", shape, dt))

    def psum(self, name, shape, dt=F32):
        self._i += 1
        return self._stack().enter_context(
            self.nc.psum_tensor(f"{name}_{self._i}", shape, dt))

    def dram(self, name, shape, dt):
        self._i += 1
        return self.nc.dram_tensor(f"{name}_{self._i}", shape, dt)

def inc(instr, sem, k=1):
    instr.then_inc(sem.h, k)
    sem.n += k
    return instr

def dinc(instr, sem):
    instr.then_inc(sem.h, 16)
    sem.n += 16
    return instr

def wait(eng, sem, n=None):
    v = sem.n if n is None else n
    k = eng.engine
    if sem.w.get(k, -1) >= v:
        return
    eng.wait_ge(sem.h, v)
    sem.w[k] = v

def xc_plain(kc, t):
    return kc * XW + t

def xc_mamba(kc, slot, t):
    return kc * XW + slot * BW + HALO + t

def emit_consts(ctx, ident_dram):
    ctx.ones_col = ctx.sbuf("ones_col", [128, 1], F32)
    ctx.ones_row = ctx.sbuf("ones_row", [1, 128], F32)
    ctx.ones_row_bf = ctx.sbuf("ones_row_bf", [1, 128], BF16)
    ctx.ident = ctx.sbuf("ident", [128, 128], F32)
    ctx.ident_bf = ctx.sbuf("ident_bf", [128, 128], BF16)
    ctx.ones_col_bf = ctx.sbuf("ones_col_bf", [128, 1], BF16)
    ctx.neg128_row = ctx.sbuf("neg128_row", [1, 128], BF16)
    ld = ctx.S["ld"]
    ldb = ld.n

    @ctx.block.sync
    def _(sync):
        wait(sync, ld)
        dinc(sync.dma_start(out=ctx.ident[:], in_=ident_dram), ld)

    @ctx.block.vector
    def _(vector):
        vector.memset(ctx.ones_col[:], 1.0)
        vector.memset(ctx.ones_row[:], 1.0)
        vector.memset(ctx.ones_col_bf[:], 1.0)
        vector.memset(ctx.neg128_row[:], -128.0)
        wait(vector, ld, ldb + 16)
        vector.tensor_copy(ctx.ident_bf[:], ctx.ident[:])
        inc(vector.memset(ctx.ones_row_bf[:], 1.0), ctx.S["va"], 1)
    ctx.consts_va = ctx.S["va"].n


def emit_convert(ctx, src, dst, kcn, mcn, src_dt):
    CW = 1024
    N = kcn * mcn * 128
    assert N % CW == 0
    n = N // CW
    lda, ldbp = ctx.S["lda"], ctx.S["ldb"]
    va, ga = ctx.S["va"], ctx.S["ga"]
    ste, stf = ctx.S["ste"], ctx.S["stf"]
    lab, lbb, vab, gab, seb, sfb = lda.n, ldbp.n, va.n, ga.n, ste.n, stf.n
    ctx.push()
    bufs = [ctx.sbuf("cvt_in", [128, CW], src_dt) for _ in range(2)]
    obufs = [ctx.sbuf("cvt_out", [128, CW], BF16) for _ in range(2)]

    def conv_done(u):
        return (va, vab + u // 2 + 1) if u % 2 == 0 else (ga, gab + (u + 1) // 2)

    @ctx.block.sync
    def _(sync):
        for u in range(n):
            if u < 2:
                wait(sync, va, vab)
                wait(sync, ga, gab)
            else:
                s_, c_ = conv_done(u - 2)
                wait(sync, s_, c_)
            dinc(sync.dma_start(out=bufs[u % 2][:],
                                in_=src[:, u * CW:(u + 1) * CW]),
                 lda if u % 2 == 0 else ldbp)

    @ctx.block.scalar
    def _(scalar):
        for u in range(n):
            s_, c_ = conv_done(u)
            wait(scalar, s_, c_)
            dinc(scalar.dma_start(out=dst[:, u * CW:(u + 1) * CW],
                                  in_=obufs[u % 2][:]),
                 ste if u % 2 == 0 else stf)

    @ctx.block.vector
    def _(vector):
        for u in range(0, n, 2):
            wait(vector, lda, lab + 16 * (u // 2 + 1))
            wait(vector, ste, seb + 16 * (u // 2) if u >= 2 else seb)
            inc(vector.tensor_copy(obufs[0][:], bufs[0][:]), va, 1)

    @ctx.block.gpsimd
    def _(gpsimd):
        for u in range(1, n, 2):
            wait(gpsimd, ldbp, lbb + 16 * (u // 2 + 1))
            wait(gpsimd, stf, sfb + 16 * (u // 2) if u >= 2 else sfb)
            inc(gpsimd.tensor_copy(obufs[1][:], bufs[1][:]), ga, 1)

    ctx.pop()
    ne = (n + 1) // 2
    nf = n // 2
    return [(ste, seb + 16 * ne), (stf, sfb + 16 * nf)]


def emit_ln(ctx, resid, xhat, mode, dep=()):
    blk = ctx.block
    mm, va, sa, sb, ga = (ctx.S[k] for k in ("mm", "va", "sa", "sb", "ga"))
    mmb, vab, sab, sbb, gab = mm.n, va.n, sa.n, sb.n, ga.n
    if not hasattr(ctx, "_ln"):
        ctx._ln = dict(
            sq=[ctx.sbuf("ln_sq", [128, 256], F32) for _ in range(2)],
            stat=ctx.sbuf("ln_stat", [1, 1024], F32),
            coef=ctx.sbuf("ln_coef", [128, 512], F32),
            tmp=ctx.sbuf("ln_tmp", [128, 2048], F32))
    L = ctx._ln
    sq, stat, coef, tmp = L["sq"], L["stat"], L["coef"], L["tmp"]
    ctx.push()
    ms = [ctx.psum("ln_ms", [1, 256])]
    ss = [ctx.psum("ln_ss", [1, 256])]
    pab = [ctx.psum("ln_ab", [128, 512])]

    def rc(kc, s):
        return kc * 1024 + s * 256

    def oc(kc, s):
        return xc_plain(kc, s * 256) if mode == "plain" else xc_mamba(kc, s, 0)

    @blk.scalar
    def _(scalar):
        for sem, v in dep:
            wait(scalar, sem, v)
        for s in range(4):
            for kc in range(KC):
                u = s * KC + kc
                if u >= 2:
                    s2, k2 = divmod(u - 2, KC)
                    wait(scalar, mm, mmb + s2 * 18 + k2 + 1)
                inc(scalar.activation(sq[u % 2][:],
                                      resid[:, rc(kc, s):rc(kc, s) + 256],
                                      AF.Square), sa, 1)
            wait(scalar, va, vab + 8 * s + 4)
            inc(scalar.activation(stat[:, 768:1024],
                                  stat[:, 512:768],
                                  AF.Sqrt, bias=np.float32(EPS)), sb, 1)
            wait(scalar, mm, mmb + s * 18 + 18)
            if s >= 1:
                wait(scalar, ga, gab + 2 * s)
            inc(scalar.copy(coef[:, 0:512],
                            pab[0][:]), sb, 1)

    @blk.tensor
    def _(tensor):
        for sem, v in dep:
            wait(tensor, sem, v)
        wait(tensor, ctx.S["va"], ctx.consts_va)
        wait(tensor, sb, sbb)
        for s in range(4):
            if s >= 1:
                wait(tensor, va, vab + 8 * (s - 1) + 2)
            for kc in range(KC):
                u = s * KC + kc
                tensor.matmul(ms[0][:], ctx.ones_col[:],
                              resid[:, rc(kc, s):rc(kc, s) + 256],
                              start=(kc == 0), stop=(kc == KC - 1))
                wait(tensor, sa, sab + u + 1)
                inc(tensor.matmul(ss[0][:], ctx.ones_col[:], sq[u % 2][:],
                                  start=(kc == 0), stop=(kc == KC - 1)), mm, 1)
            wait(tensor, va, vab + 8 * s + 6)
            if s >= 1:
                wait(tensor, sb, sbb + 2 * (s - 1) + 2)
            inc(tensor.matmul(pab[0][:, 0:256], ctx.ones_row[:],
                              stat[:, 768:1024],
                              start=True, stop=True), mm, 1)
            inc(tensor.matmul(pab[0][:, 256:512], ctx.ones_row[:],
                              stat[:, 0:256],
                              start=True, stop=True), mm, 1)

    @blk.vector
    def _(vector):
        for sem, v in dep:
            wait(vector, sem, v)
        for s in range(4):
            o = 0
            wait(vector, mm, mmb + s * 18 + 16)
            if s >= 1:
                wait(vector, mm, mmb + (s - 1) * 18 + 18)
            inc(vector.tensor_scalar(stat[:, o:o + 256], ms[0][:],
                                     1.0 / D, None, OP.mult), va, 1)
            inc(vector.tensor_scalar(stat[:, o + 256:o + 512], ss[0][:],
                                     1.0 / D, None, OP.mult), va, 1)
            wait(vector, va, vab + 8 * s + 2)
            inc(vector.scalar_tensor_tensor(stat[:, o + 512:o + 768],
                                            stat[:, o:o + 256], -1.0,
                                            stat[:, o:o + 256],
                                            OP.mult, OP.mult), va, 1)
            wait(vector, va, vab + 8 * s + 3)
            inc(vector.tensor_tensor(stat[:, o + 512:o + 768],
                                     stat[:, o + 256:o + 512],
                                     stat[:, o + 512:o + 768], OP.add), va, 1)
            wait(vector, sb, sbb + 2 * s + 1)
            inc(vector.reciprocal(stat[:, o + 768:o + 1024],
                                  stat[:, o + 768:o + 1024]), va, 1)
            wait(vector, va, vab + 8 * s + 5)
            inc(vector.scalar_tensor_tensor(stat[:, o:o + 256],
                                            stat[:, o:o + 256], -1.0,
                                            stat[:, o + 768:o + 1024],
                                            OP.mult, OP.mult), va, 1)
            wait(vector, sb, sbb + 2 * s + 2)
            a = coef[:, 0:256]
            for h in range(2):
                wait(vector, ga, gab + 2 * s + h)
                for kc in range(h * 8, h * 8 + 8):
                    ins = vector.tensor_tensor(
                        tmp[:, (kc - h * 8) * 256:(kc - h * 8) * 256 + 256],
                        resid[:, rc(kc, s):rc(kc, s) + 256], a, OP.mult)
                inc(ins, va, 1)

    @blk.gpsimd
    def _(gpsimd):
        for sem, v in dep:
            wait(gpsimd, sem, v)
        for s in range(4):
            b = coef[:, 256:512]
            for h in range(2):
                wait(gpsimd, va, vab + 8 * s + 7 + h)
                for kc in range(h * 8, h * 8 + 8):
                    ins = gpsimd.tensor_tensor(
                        xhat[:, oc(kc, s):oc(kc, s) + 256],
                        tmp[:, (kc - h * 8) * 256:(kc - h * 8) * 256 + 256],
                        b, OP.add)
                inc(ins, ga, 1)

    ctx.pop()
    return ga.n


def emit_ffn16(ctx, resid, xhat, w0t, w1t, b0_sb, b1_sb, dep=()):
    blk = ctx.block
    mm, ep, vb = ctx.S["mm"], ctx.S["ep"], ctx.S["vb"]
    l1 = [ctx.S["lda"], ctx.S["ldb"]]
    l2 = [ctx.S["ldc"], ctx.S["ldd"]]
    l1b = [x.n for x in l1]
    l2b = [x.n for x in l2]
    mmb, epb, vbb = mm.n, ep.n, vb.n
    sl1 = [ctx.sbuf("f16_sl1", [128, KC * 128], BF16) for _ in range(2)]
    sl2 = [ctx.sbuf("f16_sl2", [128, 64 * 128], BF16) for _ in range(2)]
    pA = [ctx.psum("f16_pA", [128, 256]) for _ in range(2)]
    pB = [ctx.psum("f16_pB", [128, 256]) for _ in range(2)]
    hbuf = ctx.sbuf("f16_h", [128, 64 * 256], BF16)
    w0_4 = w0t.rearrange("k (m kc c) -> m k kc c", kc=KC, c=128)
    w1_4 = w1t.rearrange("k (m kc c) -> m k kc c", kc=64, c=128)
    NT = 4

    def mma(ti, m):
        return mmb + ti * 80 + m + 1

    def mmbh(ti, mo):
        return mmb + ti * 80 + 64 + mo + 1

    def epa(ti, m):
        return epb + ti * 64 + m + 1

    def epbh(ti, mo):
        return vbb + ti * 16 + mo + 1

    @blk.sync
    def _(sync):
        for sem, v in dep:
            wait(sync, sem, v)
        nl = 0
        for ti in range(NT):
            for m in range(64):
                ua = ti * 64 + m
                if ua < 2:
                    wait(sync, l1[ua % 2])
                else:
                    t2, m2 = divmod(ua - 2, 64)
                    wait(sync, mm, mma(t2, m2))
                dinc(sync.dma_start(
                    out=sl1[ua % 2][:].rearrange("k (kc c) -> k kc c", c=128),
                    in_=w0_4[m]), l1[ua % 2])
            for mo in range(KC):
                ub = ti * KC + mo
                if ub < 2:
                    wait(sync, l2[ub % 2])
                else:
                    t2, m2 = divmod(ub - 2, KC)
                    wait(sync, mm, mmbh(t2, m2))
                dinc(sync.dma_start(
                    out=sl2[ub % 2][:].rearrange("k (kc c) -> k kc c", c=128),
                    in_=w1_4[mo]), l2[ub % 2])

    @blk.tensor
    def _(tensor):
        for sem, v in dep:
            wait(tensor, sem, v)
        for ti in range(NT):
            t0 = ti * 256
            for m in range(64):
                ua = ti * 64 + m
                wait(tensor, l1[ua % 2], l1b[ua % 2] + 16 * (ua // 2 + 1))
                if ua >= 2:
                    t2, m2 = divmod(ua - 2, 64)
                    wait(tensor, ep, epa(t2, m2))
                sl = sl1[ua % 2]
                ps = pA[ua % 2]
                for kc in range(KC):
                    last = tensor.matmul(ps[:], sl[:, kc * 128:(kc + 1) * 128],
                                         xhat[:, xc_plain(kc, t0):xc_plain(kc, t0) + 256],
                                         start=(kc == 0), stop=(kc == KC - 1))
                inc(last, mm, 1)
            wait(tensor, ep, epa(ti, 63))
            for mo in range(KC):
                ub = ti * KC + mo
                wait(tensor, l2[ub % 2], l2b[ub % 2] + 16 * (ub // 2 + 1))
                if ub >= 2:
                    t2, m2 = divmod(ub - 2, KC)
                    wait(tensor, vb, epbh(t2, m2))
                sl = sl2[ub % 2]
                ps = pB[ub % 2]
                for hc in range(64):
                    last = tensor.matmul(ps[:], sl[:, hc * 128:(hc + 1) * 128],
                                         hbuf[:, hc * 256:(hc + 1) * 256],
                                         start=(hc == 0), stop=(hc == 63))
                inc(last, mm, 1)

    @blk.scalar
    def _(scalar):
        for ti in range(NT):
            for m in range(64):
                wait(scalar, mm, mma(ti, m))
                if ti >= 1 and m == 0:
                    wait(scalar, mm, mmbh(ti - 1, KC - 1))
                inc(scalar.activation(hbuf[:, m * 256:(m + 1) * 256],
                                      pA[(ti * 64 + m) % 2][:], AF.Gelu,
                                      bias=b0_sb[:, m:m + 1]), ep, 1)

    @blk.vector
    def _(vector):
        for ti in range(NT):
            t0 = ti * 256
            for mo in range(KC):
                wait(vector, mm, mmbh(ti, mo))
                r = resid[:, mo * 1024 + t0:mo * 1024 + t0 + 256]
                inc(vector.scalar_tensor_tensor(
                    r, pB[(ti * KC + mo) % 2][:], b1_sb[:, mo:mo + 1], r,
                    OP.add, OP.add), vb, 1)

    return dict(mm_end=mmb + NT * 80, done=(ctx.S["vb"], vbb + NT * KC))


def emit_load_colvec(ctx, src_dram, nchunk, name="bias"):
    ld, mm, sb = ctx.S["ld"], ctx.S["mm"], ctx.S["sb"]
    out = ctx.sbuf(name, [128, nchunk], F32)
    stg = ctx.sbuf(name + "_stg", [nchunk, 128], F32)
    ctx.push()
    pt = ctx.psum(name + "_pt", [128, nchunk])
    ldb = ld.n
    sbb0 = ctx.S["sb"].n

    @ctx.block.sync
    def _(sync):
        wait(sync, ld)
        dinc(sync.dma_start(out=stg[:], in_=src_dram.rearrange("(m k) -> m k", k=128)),
             ld)

    @ctx.block.tensor
    def _(tensor):
        wait(tensor, ld, ldb + 16)
        wait(tensor, ctx.S["va"], ctx.consts_va)
        wait(tensor, ctx.S["sb"], sbb0)
        inc(tensor.transpose(pt[:], stg[:], ctx.ident[:nchunk, :nchunk]), mm, 1)

    mmv = ctx.S["mm"].n

    @ctx.block.scalar
    def _(scalar):
        wait(scalar, mm, mmv)
        inc(scalar.copy(out[:], pt[:]), sb, 1)

    ctx.pop()
    return out, (ctx.S["sb"], ctx.S["sb"].n)


class Launch:
    def __init__(self):
        self.nc = bass.Bass("TRN2", target_bir_lowering=False)
        self.es = ExitStack()
        self.block = self.es.enter_context(self.nc.Block())
        self.ctx = Ctx(self.nc, self.block, self.es)

    def input(self, name, shape, dt):
        return self.nc.dram_tensor(name, shape, dt, kind="ExternalInput").ap()

    def output(self, name, shape, dt):
        return self.nc.dram_tensor(name, shape, dt, kind="ExternalOutput").ap()

    def finish(self):
        self.es.close()
        return self.nc

def emit_resid_in(ctx, resid_dram):
    resid = ctx.sbuf("resid", [128, KC * 1024], F32)
    ld = ctx.S["ld"]
    dinc(ctx.block.sync_section_dma(resid, resid_dram), ld) if False else None

    @ctx.block.sync
    def _(sync):
        wait(sync, ctx.S["ld"])
        dinc(sync.dma_start(out=resid[:].rearrange("k (kc t) -> k kc t", t=1024),
                            in_=resid_dram.rearrange("(kc k) t -> k kc t", k=128)),
             ctx.S["ld"])
    return resid, (ctx.S["ld"], ctx.S["ld"].n)

def emit_resid_out(ctx, resid, resid_out_dram, dep):
    so = ctx.S["by"]

    @ctx.block.sync
    def _(sync):
        for sem, v in dep:
            wait(sync, sem, v)
        dinc(sync.dma_start(out=resid_out_dram.rearrange("(kc k) t -> k kc t", k=128),
                            in_=resid[:].rearrange("k (kc t) -> k kc t", t=1024)),
             so)
        wait(sync, so)


def build_ffn16_launch(l0=False):
    L = Launch()
    ctx = L.ctx
    rin = L.input("rin", [D, 1024], F32)
    w0 = L.input("w0t", [128, 64 * KC * 128], F32)
    w1 = L.input("w1t", [128, KC * 64 * 128], F32)
    b0 = L.input("b0", [8192], F32)
    b1 = L.input("b1", [D], F32)
    rout = L.output("rout", [D, 1024], F32)
    w0scr = ctx.dram("w0scr", [128, 64 * KC * 128], BF16)
    w1scr = ctx.dram("w1scr", [128, KC * 64 * 128], BF16)

    ident = L.input("ident", [128, 128], F32)
    emit_consts(ctx, ident)
    resid, dep_r = emit_resid_in(ctx, rin)
    xhat = ctx.sbuf("xhat", [128, KC * XW], BF16)
    ga_ln = emit_ln(ctx, resid, xhat, "plain", dep=[dep_r])
    b0sb, _ = emit_load_colvec(ctx, b0, 64, "b0")
    b1sb, dep_b = emit_load_colvec(ctx, b1, KC, "b1")
    cv1 = emit_convert(ctx, w0, w0scr.ap(), KC, 64, F32)
    cv2 = emit_convert(ctx, w1, w1scr.ap(), 64, KC, F32)
    ffn = emit_ffn16(ctx, resid, xhat, w0scr.ap(), w1scr.ap(), b0sb, b1sb,
                     dep=[(ctx.S["ga"], ga_ln)] + cv1 + cv2 + [dep_b])
    emit_resid_out(ctx, resid, rout, dep=[ffn["done"]])
    return L.finish()


def emit_ffnq(ctx, resid, xhat, gscr, uscr, dscr, gs_sb, us_sb, ds_sb, dep=()):
    blk = ctx.block
    mm, ep, vb = ctx.S["mm"], ctx.S["ep"], ctx.S["vb"]
    lg = [ctx.S["lda"], ctx.S["ldb"]]
    lu = [ctx.S["ldc"], ctx.S["ldd"]]
    ldn = [ctx.S["lde"], ctx.S["ldf"]]
    lgb = [x.n for x in lg]
    lub = [x.n for x in lu]
    ldb_ = [x.n for x in ldn]
    mmb, epb, vbb = mm.n, ep.n, vb.n
    slg = [ctx.sbuf("fq_slg", [128, KC * 128], BF16) for _ in range(2)]
    slu = [ctx.sbuf("fq_slu", [128, KC * 128], BF16) for _ in range(2)]
    sld = [ctx.sbuf("fq_sld", [128, 64 * 128], BF16) for _ in range(2)]
    gss = [ctx.sbuf("fq_gs", [128, 256], BF16) for _ in range(2)]
    uss = [ctx.sbuf("fq_us", [128, 256], BF16) for _ in range(2)]
    dss = [ctx.sbuf("fq_ds", [128, 256], F32) for _ in range(2)]
    hbuf = ctx.sbuf("fq_h", [128, 64 * 256], BF16)
    rsrow = ctx.sbuf("fq_rs", [1, 512], BF16)
    pG = [ctx.psum("fq_pG", [128, 256]) for _ in range(2)]
    pU = [ctx.psum("fq_pU", [128, 256]) for _ in range(2)]
    pD = [ctx.psum("fq_pD", [128, 256]) for _ in range(2)]
    rs = ctx.psum("fq_rsp", [1, 512])
    g4 = gscr.rearrange("k (m kc c) -> m k kc c", kc=KC, c=128)
    u4 = uscr.rearrange("k (m kc c) -> m k kc c", kc=KC, c=128)
    d4 = dscr.rearrange("k (m kc c) -> m k kc c", kc=64, c=128)
    NT = 4

    def mm_rsx(ti):
        return mmb + ti * 82 + 1

    def mm_m(ti, m):
        return mmb + ti * 82 + 1 + m + 1

    def mm_rsh(ti):
        return mmb + ti * 82 + 66

    def mm_mo(ti, mo):
        return mmb + ti * 82 + 66 + mo + 1

    def ep_rsx(ti):
        return epb + ti * 146 + 1

    def ep_m(ti, m):
        return epb + ti * 146 + 1 + 2 * (m + 1)

    def ep_rsh(ti):
        return epb + ti * 146 + 130

    def ep_mo(ti, mo):
        return epb + ti * 146 + 130 + mo + 1

    def vb_m(ti, m):
        return vbb + ti * 80 + m + 1

    def vb_mo(ti, mo):
        return vbb + ti * 80 + 64 + mo + 1

    @blk.sync
    def _(sync):
        for sem, v in dep:
            wait(sync, sem, v)
        for ti in range(NT):
            for m in range(64):
                u = ti * 64 + m
                if u < 2:
                    wait(sync, lg[u % 2])
                    wait(sync, lu[u % 2])
                else:
                    t2, m2 = divmod(u - 2, 64)
                    wait(sync, mm, mm_m(t2, m2))
                dinc(sync.dma_start(
                    out=slg[u % 2][:].rearrange("k (kc c) -> k kc c", c=128),
                    in_=g4[m]), lg[u % 2])
                dinc(sync.dma_start(
                    out=slu[u % 2][:].rearrange("k (kc c) -> k kc c", c=128),
                    in_=u4[m]), lu[u % 2])
            for mo in range(KC):
                u = ti * KC + mo
                if u < 2:
                    wait(sync, ldn[u % 2])
                else:
                    t2, m2 = divmod(u - 2, KC)
                    wait(sync, mm, mm_mo(t2, m2))
                dinc(sync.dma_start(
                    out=sld[u % 2][:].rearrange("k (kc c) -> k kc c", c=128),
                    in_=d4[mo]), ldn[u % 2])

    @blk.tensor
    def _(tensor):
        for sem, v in dep:
            wait(tensor, sem, v)
        for ti in range(NT):
            t0 = ti * 256
            if ti >= 1:
                wait(tensor, ep, ep_rsh(ti - 1))
            for kc in range(KC):
                last = tensor.matmul(rs[:, 0:256], ctx.ones_col_bf[:],
                                     xhat[:, xc_plain(kc, t0):xc_plain(kc, t0) + 256],
                                     start=(kc == 0), stop=(kc == KC - 1))
            inc(last, mm, 1)
            wait(tensor, ep, ep_rsx(ti))
            for m in range(64):
                u = ti * 64 + m
                wait(tensor, lg[u % 2], lgb[u % 2] + 16 * (u // 2 + 1))
                wait(tensor, lu[u % 2], lub[u % 2] + 16 * (u // 2 + 1))
                if u >= 2:
                    t2, m2 = divmod(u - 2, 64)
                    wait(tensor, ep, ep_m(t2, m2))
                for kc in range(KC):
                    tensor.matmul(pG[u % 2][:], slg[u % 2][:, kc * 128:(kc + 1) * 128],
                                  xhat[:, xc_plain(kc, t0):xc_plain(kc, t0) + 256],
                                  start=(kc == 0), stop=False)
                tensor.matmul(pG[u % 2][:], ctx.neg128_row[:], rsrow[:, 0:256],
                              start=False, stop=True)
                for kc in range(KC):
                    tensor.matmul(pU[u % 2][:], slu[u % 2][:, kc * 128:(kc + 1) * 128],
                                  xhat[:, xc_plain(kc, t0):xc_plain(kc, t0) + 256],
                                  start=(kc == 0), stop=False)
                inc(tensor.matmul(pU[u % 2][:], ctx.neg128_row[:], rsrow[:, 0:256],
                                  start=False, stop=True), mm, 1)
            wait(tensor, vb, vb_m(ti, 63))
            for hc in range(64):
                last = tensor.matmul(rs[:, 256:512], ctx.ones_col_bf[:],
                                     hbuf[:, hc * 256:(hc + 1) * 256],
                                     start=(hc == 0), stop=(hc == 63))
            inc(last, mm, 1)
            wait(tensor, ep, ep_rsh(ti))
            for mo in range(KC):
                u = ti * KC + mo
                wait(tensor, ldn[u % 2], ldb_[u % 2] + 16 * (u // 2 + 1))
                if u >= 2:
                    t2, m2 = divmod(u - 2, KC)
                    wait(tensor, vb, vb_mo(t2, m2))
                for hc in range(64):
                    tensor.matmul(pD[u % 2][:], sld[u % 2][:, hc * 128:(hc + 1) * 128],
                                  hbuf[:, hc * 256:(hc + 1) * 256],
                                  start=(hc == 0), stop=False)
                inc(tensor.matmul(pD[u % 2][:], ctx.neg128_row[:], rsrow[:, 256:512],
                                  start=False, stop=True), mm, 1)

    @blk.scalar
    def _(scalar):
        for ti in range(NT):
            wait(scalar, mm, mm_rsx(ti))
            inc(scalar.copy(rsrow[:, 0:256], rs[:, 0:256]), ep, 1)
            for m in range(64):
                u = ti * 64 + m
                wait(scalar, mm, mm_m(ti, m))
                if ti >= 1 and m == 0:
                    wait(scalar, mm, mm_mo(ti - 1, KC - 1))
                if u >= 2:
                    t2, m2 = divmod(u - 2, 64)
                    wait(scalar, vb, vb_m(t2, m2))
                inc(scalar.activation(gss[u % 2][:], pG[u % 2][:], AF.Silu,
                                      scale=gs_sb[:, m:m + 1]), ep, 1)
                inc(scalar.activation(uss[u % 2][:], pU[u % 2][:], AF.Copy,
                                      scale=us_sb[:, m:m + 1]), ep, 1)
            wait(scalar, mm, mm_rsh(ti))
            inc(scalar.copy(rsrow[:, 256:512], rs[:, 256:512]), ep, 1)
            for mo in range(KC):
                u = ti * KC + mo
                wait(scalar, mm, mm_mo(ti, mo))
                if u >= 2:
                    t2, m2 = divmod(u - 2, KC)
                    wait(scalar, vb, vb_mo(t2, m2))
                inc(scalar.activation(dss[u % 2][:], pD[u % 2][:], AF.Copy,
                                      scale=ds_sb[:, mo:mo + 1]), ep, 1)

    @blk.vector
    def _(vector):
        for ti in range(NT):
            t0 = ti * 256
            for m in range(64):
                u = ti * 64 + m
                wait(vector, ep, ep_m(ti, m))
                inc(vector.tensor_tensor(hbuf[:, m * 256:(m + 1) * 256],
                                         gss[u % 2][:], uss[u % 2][:],
                                         OP.mult), vb, 1)
            for mo in range(KC):
                u = ti * KC + mo
                wait(vector, ep, ep_mo(ti, mo))
                r = resid[:, mo * 1024 + t0:mo * 1024 + t0 + 256]
                inc(vector.tensor_tensor(r, r, dss[u % 2][:], OP.add), vb, 1)

    return dict(done=(vb, vbb + NT * 80))

def build_ffnq_launch():
    L = Launch()
    ctx = L.ctx
    rin = L.input("rin", [D, 1024], F32)
    gw = L.input("gwt", [128, 64 * KC * 128], U8)
    uw = L.input("uwt", [128, 64 * KC * 128], U8)
    dw = L.input("dwt", [128, KC * 64 * 128], U8)
    gsc = L.input("gsc", [8192], F32)
    usc = L.input("usc", [8192], F32)
    dsc = L.input("dsc", [D], F32)
    ident = L.input("ident", [128, 128], F32)
    rout = L.output("rout", [D, 1024], F32)
    gscr = ctx.dram("gscr", [128, 64 * KC * 128], BF16)
    uscr = ctx.dram("uscr", [128, 64 * KC * 128], BF16)
    dscr = ctx.dram("dscr", [128, KC * 64 * 128], BF16)

    emit_consts(ctx, ident)
    resid, dep_r = emit_resid_in(ctx, rin)
    xhat = ctx.sbuf("xhat", [128, KC * XW], BF16)
    ga_ln = emit_ln(ctx, resid, xhat, "plain", dep=[dep_r])
    gs_sb, _ = emit_load_colvec(ctx, gsc, 64, "gs")
    us_sb, _ = emit_load_colvec(ctx, usc, 64, "us")
    ds_sb, dep_s = emit_load_colvec(ctx, dsc, KC, "ds")
    cv1 = emit_convert(ctx, gw, gscr.ap(), KC, 64, U8)
    cv2 = emit_convert(ctx, uw, uscr.ap(), KC, 64, U8)
    cv3 = emit_convert(ctx, dw, dscr.ap(), 64, KC, U8)
    q = emit_ffnq(ctx, resid, xhat, gscr.ap(), uscr.ap(), dscr.ap(),
                  gs_sb, us_sb, ds_sb,
                  dep=[(ctx.S["ga"], ga_ln)] + cv1 + cv2 + cv3 + [dep_s])
    emit_resid_out(ctx, resid, rout, dep=[q["done"]])
    return L.finish()


def attn_cfg(variant):
    if variant == "a":
        return dict(nkv=4, kvd=512, KR=516, vD=512, rope=True, bias=False)
    return dict(nkv=16, kvd=2048, KR=2064, vD=2048, rope=False, bias=True)

def emit_rope(ctx, vector, buf, col, tcol, tn, cos, sinm, t1, t2, va):
    wait(vector, va)
    inc(vector.tensor_tensor(t1[:, :tn], buf[:, col:col + tn],
                             sinm[:, tcol:tcol + tn], OP.mult), va, 1)
    wait(vector, va)
    inc(vector.tensor_copy(t2[0:64, :tn], t1[64:128, :tn]), va, 1)
    inc(vector.tensor_copy(t2[64:128, :tn], t1[0:64, :tn]), va, 1)
    wait(vector, va)
    inc(vector.tensor_tensor(t1[:, :tn], buf[:, col:col + tn],
                             cos[:, tcol:tcol + tn], OP.mult), va, 1)
    wait(vector, va)
    inc(vector.tensor_tensor(buf[:, col:col + tn], t1[:, :tn], t2[:, :tn],
                             OP.subtract), va, 1)

def emit_attn_kv(ctx, variant, xhat, wk_scr, wv_scr, bk_sb, bv_row, cos, sinm,
                 kpack, vpack, dep=()):
    cfg = attn_cfg(variant)
    nkv, kvd, vD, KR = cfg["nkv"], cfg["kvd"], cfg["vD"], cfg["KR"]
    nkc = kvd // 128
    blk = ctx.block
    mm, ep, va, sa, ga = (ctx.S[k] for k in ("mm", "ep", "va", "sa", "ga"))
    so = [ctx.S["ste"], ctx.S["stf"]]
    l2 = ctx.S["ld2"]
    lw = [ctx.S["lda"], ctx.S["ldb"]]
    lwb = [x.n for x in lw]
    sob = [x.n for x in so]
    l2b = l2.n
    ctx.push()
    ktb = [ctx.sbuf("akv_kt", [128, 1024], BF16) for _ in range(2)]
    vstg = [ctx.sbuf("akv_vs", [128, 512], BF16) for _ in range(2)]
    knr = ctx.sbuf("akv_knr", [1, 1024], BF16)
    vslab = ctx.sbuf("akv_vsl", [128, KC * 512], BF16)
    slab = [ctx.sbuf("akv_sl", [128, KC * 128], BF16) for _ in range(2)]
    tmp = ctx._ln["tmp"]
    t1, t2 = tmp[:, 0:512], tmp[:, 512:1024]
    sqb = [tmp[:, 1024:1536], tmp[:, 1536:2048]]
    pk = [ctx.psum("akv_pk", [128, 512]) for _ in range(2)]
    pv = [ctx.psum("akv_pv", [128, 512]) for _ in range(2)]
    pn = ctx.psum("akv_pn", [1, 512])
    w4 = wk_scr.rearrange("k (m kc c) -> m k kc c", kc=KC, c=128)
    v4 = wv_scr.rearrange("k (m kc c) -> m kc k c", kc=KC, c=128)

    for m in range(nkc):
        @blk.sync
        def _(sync, m=m):
            for sem, v in dep:
                wait(sync, sem, v)
            if m < 2:
                wait(sync, lw[m % 2])
            else:
                wait(sync, mm)
            dinc(sync.dma_start(
                out=slab[m % 2][:].rearrange("k (kc c) -> k kc c", c=128),
                in_=w4[m]), lw[m % 2])

        for tt in range(2):
            @blk.tensor
            def _(tensor, m=m, tt=tt):
                for sem, v in dep:
                    wait(tensor, sem, v)
                wait(tensor, lw[m % 2], lwb[m % 2] + 16 * (m // 2 + 1))
                if m * 2 + tt >= 2:
                    wait(tensor, ep)
                for kc in range(KC):
                    last = tensor.matmul(
                        pk[tt][:], slab[m % 2][:, kc * 128:(kc + 1) * 128],
                        xhat[:, xc_plain(kc, tt * 512):xc_plain(kc, tt * 512) + 512],
                        start=(kc == 0), stop=(kc == KC - 1))
                inc(last, mm, 1)

            @blk.scalar
            def _(scalar, m=m, tt=tt):
                wait(scalar, mm)
                if m >= 2 and tt == 0:
                    wait(scalar, so[m % 2], sob[m % 2] + 16 * (m // 2 - 1 + 1))
                c0 = tt * 512
                if cfg["bias"]:
                    ins = scalar.activation(ktb[m % 2][:, c0:c0 + 512],
                                            pk[tt][:], AF.Identity,
                                            bias=bk_sb[:, m:m + 1])
                else:
                    ins = scalar.copy(ktb[m % 2][:, c0:c0 + 512], pk[tt][:])
                inc(ins, ep, 1)

        if cfg["rope"]:
            @blk.vector
            def _(vector, m=m):
                wait(vector, ep)
                emit_rope(ctx, vector, ktb[m % 2], 0, 0, 512, cos, sinm, t1, t2, va)
                emit_rope(ctx, vector, ktb[m % 2], 512, 512, 512, cos, sinm,
                          t1, t2, va)

        for tt in range(2):
            @blk.scalar
            def _(scalar, m=m, tt=tt):
                if cfg["rope"]:
                    wait(scalar, va)
                else:
                    wait(scalar, ep)
                inc(scalar.activation(sqb[tt][:], ktb[m % 2][:, tt * 512:tt * 512 + 512],
                                      AF.Square), sa, 1)

            @blk.tensor
            def _(tensor, tt=tt):
                wait(tensor, sa)
                if tt == 0:
                    wait(tensor, ep)
                inc(tensor.matmul(pn[:], ctx.ones_col[:], sqb[tt][:],
                                  start=True, stop=True), mm, 1)

            @blk.scalar
            def _(scalar, m=m, tt=tt):
                wait(scalar, mm)
                if m >= 1 and tt == 0:
                    wait(scalar, l2, l2b + 16 * m)
                inc(scalar.copy(knr[:, tt * 512:tt * 512 + 512], pn[:]), ep, 1)

        @blk.scalar
        def _(scalar, m=m):
            wait(scalar, ep)
            dinc(scalar.dma_start(out=kpack[kvd + m:kvd + m + 1, :], in_=knr[:]),
                 l2)
            dinc(scalar.dma_start(out=kpack[m * 128:(m + 1) * 128, :],
                                  in_=ktb[m % 2][:]), so[m % 2])

    for dv in range(vD // 512):
        @blk.sync
        def _(sync, dv=dv):
            if dv >= 1:
                wait(sync, mm)
            for mi in range(4):
                dinc(sync.dma_start(
                    out=vslab[:].rearrange("k (kc m c) -> k kc m c",
                                           m=4, c=128)[:, :, mi],
                    in_=v4[dv * 4 + mi].rearrange("kc k c -> k kc c")),
                    ctx.S["ldc"])

        for tt in range(8):
            u = dv * 8 + tt

            @blk.tensor
            def _(tensor, u=u, tt=tt, dv=dv):
                if tt == 0:
                    wait(tensor, ctx.S["ldc"])
                if u >= 2:
                    wait(tensor, ep)
                for kc in range(KC):
                    tensor.matmul(
                        pv[u % 2][:],
                        xhat[:, xc_plain(kc, tt * 128):xc_plain(kc, tt * 128) + 128],
                        vslab[:, kc * 512:(kc + 1) * 512],
                        start=(kc == 0), stop=False)
                inc(tensor.matmul(pv[u % 2][:], ctx.ones_row_bf[:],
                                  bv_row[:, dv * 512:(dv + 1) * 512],
                                  start=False, stop=True), mm, 1)

            @blk.scalar
            def _(scalar, u=u, tt=tt, dv=dv):
                wait(scalar, mm)
                if u >= 2:
                    wait(scalar, so[u % 2], sob[u % 2] + 16 * (nkc // 2 + u // 2 - 1 + 1))
                inc(scalar.copy(vstg[u % 2][:], pv[u % 2][:]), ep, 1)
                wait(scalar, ep)
                dinc(scalar.dma_start(
                    out=vpack[tt * 128:(tt + 1) * 128, dv * 512:(dv + 1) * 512],
                    in_=vstg[u % 2][:]), so[u % 2])

    @blk.gpsimd
    def _(gpsimd):
        wait(gpsimd, ep)
        wait(gpsimd, so[0])
        wait(gpsimd, so[1])
        wait(gpsimd, l2)
        inc(gpsimd.engine_nop(), ga, 1)

    ctx.pop()
    return [(ga, ga.n)]

def emit_attn_core(ctx, variant, resid, xhat, wq_scr, wo_scr, bq_sb, bo_sb,
                   cos, sinm, kag, vag, masks, eq, dep=()):
    cfg = attn_cfg(variant)
    nkv, kvd, vD, KR = cfg["nkv"], cfg["kvd"], cfg["vD"], cfg["KR"]
    blk = ctx.block
    mm, ep, va, vb, sa, sb, ga = (ctx.S[k] for k in
                                  ("mm", "ep", "va", "vb", "sa", "sb", "ga"))
    lq = [ctx.S["lda"], ctx.S["ldb"]]
    lk = [ctx.S["lde"], ctx.S["ldf"]]
    lv = [ctx.S["ste"], ctx.S["stf"]]
    lm = ctx.S["ldc"]
    ld = ctx.S["ld"]
    ctx.push()
    qt = ctx.sbuf("at_qt", [128, 16 * 1024], BF16)
    qn2 = ctx.sbuf("at_qn2", [1, 128], F32)
    negBs = ctx.sbuf("at_negB", [128, 32], F32)
    brow = ctx.sbuf("at_brow", [1, 96], F32)
    knl = ctx.sbuf("at_knl", [nkv, 1024], BF16)
    kmx = ctx.sbuf("at_kmx", [nkv, 2], F32)
    PT = ctx.sbuf("at_ptb", [128, 1024], BF16)
    kblk = [ctx.sbuf("at_kb", [128, nkv * 256], BF16) for _ in range(2)]
    vblk = [ctx.sbuf("at_vb", [128, 2 * vD], BF16) for _ in range(2)]
    mbuf = ctx.sbuf("at_mb", [128, 4 * 512], BF16)
    ao = ctx.sbuf("at_ao", [128, 2048], BF16)
    rsv = ctx.sbuf("at_rsv", [16, 128], F32)
    rec = ctx.sbuf("at_rec", [128, 16], F32)
    slab = [ctx.sbuf("at_sl", [128, KC * 128], BF16) for _ in range(2)]
    tmp = ctx._ln["tmp"]
    t1, t2 = tmp[:, 0:512], tmp[:, 512:1024]
    sqv = tmp[:, 1024:1536]
    ctx.push()
    pq = [ctx.psum("at_pq", [128, 512]) for _ in range(2)]
    pn = ctx.psum("at_pn", [1, 512])
    w4q = wq_scr.rearrange("k (m kc c) -> m k kc c", kc=KC, c=128)
    lqb = [x.n for x in lq]

    for u in range(32):
        m, tt = divmod(u, 2)
        if tt == 0:
            @blk.sync
            def _(sync, m=m):
                if m < 2:
                    wait(sync, lq[m % 2])
                else:
                    wait(sync, mm)
                dinc(sync.dma_start(
                    out=slab[m % 2][:].rearrange("k (kc c) -> k kc c", c=128),
                    in_=w4q[m]), lq[m % 2])

        @blk.tensor
        def _(tensor, u=u, m=m, tt=tt):
            for sem, v in dep:
                wait(tensor, sem, v)
            wait(tensor, lq[m % 2], lqb[m % 2] + 16 * (m // 2 + 1))
            if u >= 2:
                wait(tensor, ep)
            for kc in range(KC):
                last = tensor.matmul(
                    pq[u % 2][:], slab[m % 2][:, kc * 128:(kc + 1) * 128],
                    xhat[:, xc_plain(kc, tt * 512):xc_plain(kc, tt * 512) + 512],
                    start=(kc == 0), stop=(kc == KC - 1))
            inc(last, mm, 1)

        @blk.scalar
        def _(scalar, u=u, m=m, tt=tt):
            wait(scalar, mm)
            c0 = m * 1024 + tt * 512
            if cfg["bias"]:
                ins = scalar.activation(qt[:, c0:c0 + 512], pq[u % 2][:],
                                        AF.Identity, bias=bq_sb[:, m:m + 1])
            else:
                ins = scalar.copy(qt[:, c0:c0 + 512], pq[u % 2][:])
            inc(ins, ep, 1)

        if cfg["rope"]:
            @blk.vector
            def _(vector, m=m, tt=tt):
                wait(vector, ep)
                emit_rope(ctx, vector, qt, m * 1024 + tt * 512, tt * 512, 512,
                          cos, sinm, t1, t2, va)

        @blk.scalar
        def _(scalar, m=m, tt=tt):
            if cfg["rope"]:
                wait(scalar, va)
            else:
                wait(scalar, ep)
            wait(scalar, vb)
            inc(scalar.activation(sqv[:], qt[:, m * 1024 + tt * 512:
                                             m * 1024 + tt * 512 + 512],
                                  AF.Square), sa, 1)

        @blk.tensor
        def _(tensor):
            wait(tensor, sa)
            inc(tensor.matmul(pn[:], ctx.ones_col[:], sqv[:],
                              start=True, stop=True), mm, 1)

        @blk.vector
        def _(vector, m=m, tt=tt):
            wait(vector, mm)
            qhq, qi = divmod(m, 4)
            for h in range(4):
                st = tt * 4 + h
                inc(vector.reduce_max(
                    qn2[:, qhq * 32 + st * 4 + qi:qhq * 32 + st * 4 + qi + 1],
                    pn[:, h * 128:h * 128 + 128], axis=mybir.AxisListType.X),
                    vb, 1)

    for r in range(GROUP):
        @blk.sync
        def _(sync, r=r):
            wait(sync, ld)
            if r >= 1:
                wait(sync, vb)
            dinc(sync.dma_start(out=knl[:, :],
                                in_=kag[r * KR + kvd:r * KR + kvd + nkv, :]), ld)

        @blk.vector
        def _(vector, r=r):
            wait(vector, ld)
            inc(vector.reduce_max(kmx[:, 1:2] if r else kmx[:, 0:1], knl[:, :],
                                  axis=mybir.AxisListType.X), vb, 1)
            if r:
                wait(vector, vb)
                inc(vector.tensor_tensor(kmx[:, 0:1], kmx[:, 0:1], kmx[:, 1:2],
                                         OP.max), vb, 1)

    @blk.tensor
    def _(tensor):
        wait(tensor, vb)
        wait(tensor, ep)
        inc(tensor.transpose(pn[:, :nkv], kmx[:, 0:1], ctx.ident[:nkv, :nkv]),
            mm, 1)

    @blk.scalar
    def _(scalar):
        wait(scalar, mm)
        inc(scalar.copy(brow[:, 0:nkv], pn[:, :nkv]), sb, 1)

    @blk.vector
    def _(vector):
        wait(vector, sb)
        if variant == "a":
            inc(vector.tensor_copy(brow[:, 16:20], brow[:, 0:4]), vb, 1)
        else:
            inc(vector.reduce_max(brow[:, 16:20],
                                  brow[:, 0:16].rearrange("o (q i) -> o q i", i=4),
                                  axis=mybir.AxisListType.X), vb, 1)
        for q in range(4):
            wait(vector, vb)
            inc(vector.tensor_scalar(brow[:, 24 + q * 8:24 + q * 8 + 8],
                                     ctx.ones_row[:, 0:8], brow[:, 16 + q:17 + q],
                                     None, OP.mult), vb, 1)
        inc(vector.reduce_max(brow[:, 56:88],
                              qn2[:].rearrange("o (g i) -> o g i", i=4),
                              axis=mybir.AxisListType.X), vb, 1)
        wait(vector, vb)
        inc(vector.tensor_tensor(brow[:, 56:88], brow[:, 56:88], brow[:, 24:56],
                                 OP.mult), vb, 1)

    @blk.scalar
    def _(scalar):
        wait(scalar, vb)
        inc(scalar.activation(brow[:, 56:88], brow[:, 56:88], AF.Sqrt), sb, 1)

    @blk.vector
    def _(vector):
        wait(vector, sb)
        inc(vector.tensor_scalar(brow[:, 56:88], brow[:, 56:88],
                                 float(-1.01 / math.sqrt(128.0)), None,
                                 OP.mult), vb, 1)

    @blk.tensor
    def _(tensor):
        wait(tensor, vb)
        inc(tensor.matmul(pq[0][:, 0:32], ctx.ones_row[:], brow[:, 56:88],
                          start=True, stop=True), mm, 1)

    @blk.scalar
    def _(scalar):
        wait(scalar, mm)
        inc(scalar.copy(negBs[:], pq[0][:, 0:32]), sb, 1)

    ctx.pop()
    OT = ctx.psum("at_OT", [128, 2048])
    RSp = ctx.psum("at_RS", [16, 128])
    stq = [ctx.psum("at_stq", [128, 512]) for _ in range(2)]
    ptp = ctx.psum("at_ptp", [128, 128], BF16)
    sc128 = np.float32(1.0 / math.sqrt(128.0))

    def kvh_of(qh):
        return qh // 4 if variant == "a" else qh

    vis = [(s, qh2, j) for s in range(4) for qh2 in range(2)
           for j in range(4 * (s + 1))]

    def kv_dma(sync, s, qh2, j, vi):
        r, sl = BLK_OWNER[j]
        scol = sl * 256
        dinc(sync.dma_start(
            out=kblk[vi % 2][:].rearrange("k (h t) -> k h t", t=256),
            in_=kag[r * KR:r * KR + kvd, scol:scol + 256]
                .rearrange("(h k) t -> k h t", k=128)), lk[vi % 2])
        dinc(sync.dma_start(
            out=vblk[vi % 2][:].rearrange("k (h d) -> k h d", d=vD),
            in_=vag[r * 1024 + scol:r * 1024 + scol + 256, :]
                .rearrange("(h k) d -> k h d", k=128)), lv[vi % 2])

    lkb = [x.n for x in lk]
    lvb = [x.n for x in lv]
    nkv_dma = [0, 0]

    for vi, (s, qh2, j) in enumerate(vis):
        ns = 4 * (s + 1)
        qoff = s * 256 + qh2 * 128
        newtile = (j == 0)

        @blk.sync
        def _(sync, s=s, qh2=qh2, j=j, vi=vi, ns=ns, newtile=newtile):
            if newtile:
                for jj in range(min(2, ns)):
                    if vi + jj < 2:
                        wait(sync, lk[(vi + jj) % 2])
                        wait(sync, lv[(vi + jj) % 2])
                    else:
                        wait(sync, mm)
                    kv_dma(sync, s, qh2, jj, vi + jj)
                wait(sync, mm)
                for jr in range(2):
                    for kh in range(2):
                        dinc(sync.dma_start(
                            out=mbuf[:, (jr * 2 + kh) * 512:(jr * 2 + kh) * 512 + 512],
                            in_=masks[s, qh2, jr, kh]), lm)
            elif j + 1 < ns:
                wait(sync, mm)
                kv_dma(sync, s, qh2, j + 1, vi + 1)
            if j == ns - 2:
                for jr in range(2, 4):
                    for kh in range(2):
                        dinc(sync.dma_start(
                            out=mbuf[:, ((jr - 2) * 2 + kh) * 512:
                                     ((jr - 2) * 2 + kh) * 512 + 512],
                            in_=masks[s, qh2, jr, kh]), lm)

        for quad in range(4):
            @blk.tensor
            def _(tensor, s=s, qh2=qh2, j=j, vi=vi, ns=ns, quad=quad, qoff=qoff):
                if quad == 0:
                    nkv_dma[0] += 1
                    wait(tensor, lk[vi % 2], lkb[vi % 2] + 16 * ((vi // 2) + 1))
                    wait(tensor, lv[vi % 2], lvb[vi % 2] + 16 * ((vi // 2) + 1))
                    if j >= ns - 4:
                        wait(tensor, lm)
                if vi == 0 and quad == 0:
                    wait(tensor, ep)
                    wait(tensor, va)
                    wait(tensor, sb)
                    wait(tensor, vb)
                if quad >= 1 or vi > 0:
                    wait(tensor, sa)
                if j == 0 and quad == 0:
                    wait(tensor, vb)
                last = None
                for kh in range(2):
                    masked = j >= ns - 4
                    for qi in range(4):
                        qh = quad * 4 + qi
                        last = tensor.matmul(
                            stq[kh][:, qi * 128:qi * 128 + 128],
                            kblk[vi % 2][:, kvh_of(qh) * 256 + kh * 128:
                                         kvh_of(qh) * 256 + kh * 128 + 128],
                            qt[:, qh * 1024 + qoff:qh * 1024 + qoff + 128],
                            start=(qi == 0), stop=(qi == 3 and not masked))
                        if masked:
                            jr = (j - (ns - 4)) % 2
                            mc = (jr * 2 + kh) * 512 + qi * 128
                            last = tensor.matmul(
                                stq[kh][:, qi * 128:qi * 128 + 128],
                                ctx.ident_bf[:], mbuf[:, mc:mc + 128],
                                start=False, stop=(qi == 3))
                inc(last, mm, 1)

            @blk.scalar
            def _(scalar, s=s, qh2=qh2, quad=quad, vi=vi):
                wait(scalar, mm)
                if vi == 0 and quad == 0:
                    wait(scalar, sb)
                nb = quad * 8 + s * 2 + qh2
                for kh in range(2):
                    ins = scalar.activation(PT[:, kh * 512:kh * 512 + 512],
                                            stq[kh][:], AF.Exp,
                                            bias=negBs[:, nb:nb + 1],
                                            scale=sc128)
                inc(ins, sa, 1)

            @blk.tensor
            def _(tensor, s=s, qh2=qh2, j=j, vi=vi, quad=quad, ns=ns):
                wait(tensor, sa)
                if j == 0 and quad == 0:
                    wait(tensor, sb)
                first = (j == 0)
                lastv = (j == ns - 1)
                for kh in range(2):
                    for qi in range(4):
                        qh = quad * 4 + qi
                        tensor.matmul(
                            OT[:, qh * 128:qh * 128 + 128],
                            PT[:, kh * 512 + qi * 128:kh * 512 + qi * 128 + 128],
                            vblk[vi % 2][:, kh * vD + kvh_of(qh) * 128:
                                         kh * vD + kvh_of(qh) * 128 + 128],
                            start=(first and kh == 0 and qi == 0),
                            stop=(lastv and kh == 1 and qi == 3))
                        last = tensor.matmul(
                            RSp[:], eq[:, qh * 16:qh * 16 + 16],
                            PT[:, kh * 512 + qi * 128:kh * 512 + qi * 128 + 128],
                            start=(first and kh == 0 and qi == 0 and quad == 0),
                            stop=(lastv and kh == 1 and qi == 3 and quad == 3))
                inc(last, mm, 1)

        if j == ns - 1:
            @blk.scalar
            def _(scalar):
                wait(scalar, mm)
                inc(scalar.copy(rsv[:], RSp[:]), sb, 1)

            @blk.tensor
            def _(tensor):
                wait(tensor, sb)
                wait(tensor, sa)
                inc(tensor.transpose(stq[0][:, 0:16], rsv[:],
                                     ctx.ident[:16, :16]), mm, 1)

            @blk.vector
            def _(vector):
                wait(vector, mm)
                inc(vector.reciprocal(rec[:], stq[0][:, 0:16]), vb, 1)
                wait(vector, vb)
                for dvc in range(16):
                    inc(vector.tensor_scalar(ao[:, dvc * 128:dvc * 128 + 128],
                                             OT[:, dvc * 128:dvc * 128 + 128],
                                             rec[:, dvc:dvc + 1], None,
                                             OP.mult), vb, 1)

            for dvc in range(16):
                @blk.tensor
                def _(tensor, dvc=dvc):
                    wait(tensor, vb)
                    wait(tensor, ep)
                    inc(tensor.transpose(ptp[:], ao[:, dvc * 128:dvc * 128 + 128],
                                         ctx.ident_bf[:]), mm, 1)

                @blk.scalar
                def _(scalar, s=s, qh2=qh2, dvc=dvc):
                    wait(scalar, mm)
                    st = s * 2 + qh2
                    inc(scalar.copy(xhat[:, xc_plain(dvc, st * 128):
                                        xc_plain(dvc, st * 128) + 128],
                                    ptp[:]), ep, 1)

    w4o = wo_scr.rearrange("k (m kc c) -> m k kc c", kc=KC, c=128)
    for u in range(32):
        m, tt = divmod(u, 2)
        if tt == 0:
            @blk.sync
            def _(sync, m=m):
                wait(sync, mm)
                dinc(sync.dma_start(
                    out=slab[m % 2][:].rearrange("k (kc c) -> k kc c", c=128),
                    in_=w4o[m]), lq[m % 2])

        @blk.tensor
        def _(tensor, u=u, m=m, tt=tt):
            wait(tensor, lq[m % 2])
            if u == 0:
                wait(tensor, ep)
                wait(tensor, vb)
            if u >= 2:
                wait(tensor, vb)
            for kc in range(KC):
                last = tensor.matmul(
                    stq[u % 2][:], slab[m % 2][:, kc * 128:(kc + 1) * 128],
                    xhat[:, xc_plain(kc, tt * 512):xc_plain(kc, tt * 512) + 512],
                    start=(kc == 0), stop=(kc == KC - 1))
            inc(last, mm, 1)

        @blk.vector
        def _(vector, u=u, m=m, tt=tt):
            wait(vector, mm)
            r = resid[:, m * 1024 + tt * 512:m * 1024 + tt * 512 + 512]
            sc = bo_sb[:, m:m + 1] if bo_sb is not None else 0.0
            inc(vector.scalar_tensor_tensor(r, stq[u % 2][:], sc, r,
                                            OP.add, OP.add), vb, 1)

    ctx.pop()
    return [(ctx.S["vb"], ctx.S["vb"].n)]

def emit_load_row(ctx, src_dram, n, name="row", dt=BF16):
    ld, va = ctx.S["ld"], ctx.S["va"]
    stg = ctx.sbuf(name + "_stg", [1, n], F32)
    out = ctx.sbuf(name, [1, n], dt)

    @ctx.block.sync
    def _(sync):
        wait(sync, ld)
        dinc(sync.dma_start(out=stg[:], in_=src_dram.rearrange("(o n) -> o n", o=1)),
             ld)

    @ctx.block.vector
    def _(vector):
        wait(vector, ctx.S["ld"])
        inc(vector.tensor_copy(out[:], stg[:]), va, 1)

    return out, (va, va.n)

def emit_load_tile(ctx, src_dram, shape, name, dt):
    t = ctx.sbuf(name, shape, dt)
    ld = ctx.S["ld"]

    @ctx.block.sync
    def _(sync):
        wait(sync, ld)
        dinc(sync.dma_start(out=t[:], in_=src_dram), ld)
    return t, (ld, ld.n)

def host_masks(core):
    g = core % GROUP
    m = np.zeros((4, 2, 4, 2, 128, 512), np.float32)
    for s in range(4):
        qb = blocks_of(g)[s]
        for qh2 in range(2):
            for jr in range(4):
                j = 4 * s + jr
                for kh in range(2):
                    kpos = j * 256 + kh * 128 + np.arange(128)[:, None]
                    qpos = qb * 256 + qh2 * 128 + np.arange(128)[None, :]
                    t = np.where(kpos <= qpos, 0.0, -30000.0)
                    m[s, qh2, jr, kh] = np.tile(t, (1, 4))
    return m.astype(BF)

def host_eq():
    e = np.zeros((128, 256), np.float32)
    for qh in range(16):
        e[:, qh * 16 + qh] = 1.0
    return e.astype(BF)

def build_attn_kv_launch(variant):
    cfg = attn_cfg(variant)
    kvd, vD, KR = cfg["kvd"], cfg["vD"], cfg["KR"]
    L = Launch()
    ctx = L.ctx
    rin = L.input("rin", [D, 1024], F32)
    wk = L.input("wk", [128, (kvd // 128) * KC * 128], F32)
    wv = L.input("wv", [128, (vD // 128) * KC * 128], F32)
    ident = L.input("ident", [128, 128], F32)
    kpack = L.output("kpack", [KR, 1024], BF16)
    vpack = L.output("vpack", [1024, vD], BF16)
    kscr = ctx.dram("kscr", [128, (kvd // 128) * KC * 128], BF16)
    vscr = ctx.dram("vscr", [128, (vD // 128) * KC * 128], BF16)
    emit_consts(ctx, ident)
    resid, dep_r = emit_resid_in(ctx, rin)
    xhat = ctx.sbuf("xhat", [128, KC * XW], BF16)
    ga_ln = emit_ln(ctx, resid, xhat, "plain", dep=[dep_r])
    dep = [(ctx.S["ga"], ga_ln)]
    if cfg["bias"]:
        bkv = L.input("bk", [kvd], F32)
        bvv = L.input("bv", [vD], F32)
        bk_sb, _ = emit_load_colvec(ctx, bkv, kvd // 128, "bk")
        bv_row, dep_bv = emit_load_row(ctx, bvv, vD, "bv")
        dep.append(dep_bv)
        cos = sinm = None
    else:
        bk_sb = None
        bv_row = ctx.sbuf("bv0", [1, vD], BF16)
        cosd = L.input("cos", [128, 1024], F32)
        sind = L.input("sinm", [128, 1024], F32)
        cos, dep_c = emit_load_tile(ctx, cosd, [128, 1024], "cos", F32)
        sinm, dep_s = emit_load_tile(ctx, sind, [128, 1024], "sinm", F32)
        dep += [dep_c, dep_s]

        @ctx.block.vector
        def _(vector):
            inc(vector.memset(bv_row[:], 0.0), ctx.S["va"], 1)
        dep.append((ctx.S["va"], ctx.S["va"].n))
    cv1 = emit_convert(ctx, wk, kscr.ap(), KC, kvd // 128, F32)
    cv2 = emit_convert(ctx, wv, vscr.ap(), KC, vD // 128, F32)
    fin = emit_attn_kv(ctx, variant, xhat, kscr.ap(), vscr.ap(), bk_sb, bv_row,
                       cos, sinm, kpack, vpack, dep=dep + cv1 + cv2)

    @ctx.block.sync
    def _(sync):
        for sem, v in fin:
            wait(sync, sem, v)
    return L.finish()

def build_attn_core_launch(variant):
    cfg = attn_cfg(variant)
    kvd, vD, KR = cfg["kvd"], cfg["vD"], cfg["KR"]
    L = Launch()
    ctx = L.ctx
    rin = L.input("rin", [D, 1024], F32)
    wq = L.input("wq", [128, KC * KC * 128], F32)
    wo = L.input("wo", [128, KC * KC * 128], F32)
    kag = L.input("kag", [4 * KR, 1024], BF16)
    vag = L.input("vag", [4096, vD], BF16)
    masks = L.input("masks", [4, 2, 4, 2, 128, 512], BF16)
    eqc = L.input("eqc", [128, 256], BF16)
    ident = L.input("ident", [128, 128], F32)
    rout = L.output("rout", [D, 1024], F32)
    qscr = ctx.dram("qscr", [128, KC * KC * 128], BF16)
    oscr = ctx.dram("oscr", [128, KC * KC * 128], BF16)
    emit_consts(ctx, ident)
    resid, dep_r = emit_resid_in(ctx, rin)
    xhat = ctx.sbuf("xhat", [128, KC * XW], BF16)
    ga_ln = emit_ln(ctx, resid, xhat, "plain", dep=[dep_r])
    dep = [(ctx.S["ga"], ga_ln)]
    eq, dep_e = emit_load_tile(ctx, eqc, [128, 256], "eq", BF16)
    dep.append(dep_e)
    if cfg["bias"]:
        bqv = L.input("bq", [D], F32)
        bov = L.input("bo", [D], F32)
        bq_sb, _ = emit_load_colvec(ctx, bqv, KC, "bq")
        bo_sb, dep_b = emit_load_colvec(ctx, bov, KC, "bo")
        dep.append(dep_b)
        cos = sinm = None
    else:
        bq_sb = bo_sb = None
        cosd = L.input("cos", [128, 1024], F32)
        sind = L.input("sinm", [128, 1024], F32)
        cos, dep_c = emit_load_tile(ctx, cosd, [128, 1024], "cos", F32)
        sinm, dep_s = emit_load_tile(ctx, sind, [128, 1024], "sinm", F32)
        dep += [dep_c, dep_s]
    cv1 = emit_convert(ctx, wq, qscr.ap(), KC, KC, F32)
    cv2 = emit_convert(ctx, wo, oscr.ap(), KC, KC, F32)
    fin = emit_attn_core(ctx, variant, resid, xhat, qscr.ap(), oscr.ap(),
                         bq_sb, bo_sb, cos, sinm, kag, vag, masks, eq,
                         dep=dep + cv1 + cv2)
    emit_resid_out(ctx, resid, rout, dep=fin)
    return L.finish()


def mamba_cfg(variant):
    return dict(in_bias=(variant == "a"), conv_bias=(variant == "b"),
                out_bias=(variant == "a"), gate_first=(variant == "a"))

MMC = 67
NIC = 32

def emit_mamba_m1(ctx, xhat, hpack, dep=()):
    blk = ctx.block
    mm, ep, sb = ctx.S["mm"], ctx.S["ep"], ctx.S["sb"]
    l2 = ctx.S["ld2"]
    ctx.push()
    hst = ctx.sbuf("m1_hst", [3, 4 * 2048], BF16)
    pt = ctx.psum("m1_pt", [3, 128], BF16)
    for s in range(4):
        for kc in range(KC):
            @blk.tensor
            def _(tensor, s=s, kc=kc):
                for sem, v in dep:
                    wait(tensor, sem, v)
                wait(tensor, ctx.S["va"], ctx.consts_va)
                wait(tensor, ep)
                inc(tensor.transpose(
                    pt[:], xhat[:, kc * XW + s * BW + BW - 3:kc * XW + s * BW + BW],
                    ctx.ident_bf[:]), mm, 1)

            @blk.scalar
            def _(scalar, s=s, kc=kc):
                wait(scalar, mm)
                inc(scalar.copy(hst[0:3, s * 2048 + kc * 128:s * 2048 + (kc + 1) * 128],
                                pt[0:3, :]), ep, 1)

    @blk.scalar
    def _(scalar):
        wait(scalar, ep)
        for s4 in range(4):
            dinc(scalar.dma_start(out=hpack[s4 * 3:(s4 + 1) * 3, :],
                                  in_=hst[:, s4 * 2048:(s4 + 1) * 2048]), l2)
        wait(scalar, l2)
    ctx.pop()
    return [(l2, l2.n)]

def emit_halo_select(ctx, xhat, hgat, hsel, dep=()):
    blk = ctx.block
    mm, ep, va = ctx.S["mm"], ctx.S["ep"], ctx.S["va"]
    ld = ctx.S["ld"]
    ctx.push()
    hg = ctx.sbuf("hs_hg", [48, 2048], BF16)
    hs = ctx.sbuf("hs_sel", [48, 12], BF16)
    htok = ctx.sbuf("hs_tok", [12, 2048], BF16)
    pg = ctx.psum("hs_pg", [12, 512])
    ptr = ctx.psum("hs_ptr", [128, 12], BF16)

    @blk.sync
    def _(sync):
        for sem, v in dep:
            wait(sync, sem, v)
        wait(sync, ld)
        dinc(sync.dma_start(out=hg[:], in_=hgat), ld)
        wait(sync, ld)
        dinc(sync.dma_start(out=hs[:], in_=hsel), ld)

    for t4 in range(4):
        @blk.tensor
        def _(tensor, t4=t4):
            wait(tensor, ld)
            if t4 >= 1:
                wait(tensor, ep)
            inc(tensor.matmul(pg[:], hs[:], hg[:, t4 * 512:(t4 + 1) * 512],
                              start=True, stop=True), mm, 1)

        @blk.scalar
        def _(scalar, t4=t4):
            wait(scalar, mm)
            inc(scalar.copy(htok[:, t4 * 512:(t4 + 1) * 512], pg[:]), ep, 1)

    for kc in range(KC):
        @blk.tensor
        def _(tensor, kc=kc):
            wait(tensor, ep)
            if kc >= 1:
                wait(tensor, ctx.S["sa"])
            inc(tensor.transpose(ptr[:, :12], htok[:, kc * 128:(kc + 1) * 128],
                                 ctx.ident_bf[:12, :12]), mm, 1)

        @blk.scalar
        def _(scalar, kc=kc):
            wait(scalar, mm)
            for s in range(4):
                ins = scalar.copy(xhat[:, kc * XW + s * BW:kc * XW + s * BW + 3],
                                  ptr[:, s * 3:s * 3 + 3])
            inc(ins, ctx.S["sa"], 1)
    ctx.pop()
    return [(ctx.S["sa"], ctx.S["sa"].n)]

def emit_mamba_m2(ctx, variant, xhat, wi_scr, bin_sb, cw_sb, cb_sb, dtb_sb,
                  acol, dbc, rsb, mtri, rfr2, gate_sp, yp_sp, c_sp, sd_sp,
                  spack, ctile, dep=()):
    cfg = mamba_cfg(variant)
    blk = ctx.block
    mm, ep, va, vb, sa, sb, ga = (ctx.S[k] for k in
                                  ("mm", "ep", "va", "vb", "sa", "sb", "ga"))
    lw = [ctx.S["lda"], ctx.S["ldb"]]
    so = [ctx.S["ste"], ctx.S["stf"]]
    l2 = ctx.S["ld2"]
    lwb = [x.n for x in lw]
    l2b = l2.n
    ctx.push()
    slab = [ctx.sbuf("m2_sl", [128, KC * 128], BF16)]
    xbc = ctx.sbuf("m2_xbc", [128, 34 * BW], BF16)
    xcs = ctx.sbuf("m2_xcs", [128, 34 * 256], BF16)
    gst = [ctx.sbuf("m2_gst", [128, 256], BF16)]
    cacc = ctx.sbuf("m2_cacc", [128, 256], F32)
    sstg = cacc[:].bitcast(BF16)
    dtf = ctx._ln["tmp"][64:128, 0:BW]
    dw = ctx._ln["tmp"]
    cumt = ctx.sbuf("m2_cumt", [128, 128], F32)
    fbf = ctx.sbuf("m2_fbf", [64, 256], BF16)
    ddbf = ctx.sbuf("m2_ddbf", [64, 256], BF16)
    crow = ctx.sbuf("m2_crow", [1, 512], F32)
    ebuf = ctx.sbuf("m2_e", [128, 256], BF16)
    sdt = ctx.sbuf("m2_sdt", [128, 128], BF16)
    xdfm = ctx.sbuf("m2_xdfm", [128, 256], BF16)
    xdt = ctx.sbuf("m2_xdt", [128, 2 * 4096], BF16)
    ybuf = ctx.sbuf("m2_yb", [128, 2 * 4096], BF16)
    btk = ctx.sbuf("m2_btk", [128, 2 * 128], BF16)
    gbuf = ctx.sbuf("m2_g", [128, 2 * 256], BF16)
    stg = [ctx.sbuf("m2_stg", [128, 512], BF16)]
    ytmp = ctx._ln["coef"][:, 256:320]
    p1 = [ctx.psum("m2_p1", [128, 512]) for _ in range(2)]
    p2 = ctx.psum("m2_p2", [128, 256])
    p3 = ctx.psum("m2_p3", [128, 256])
    pt = ctx.psum("m2_pt", [128, 128], BF16)
    ptf = ctx.psum("m2_ptf", [128, 128])
    p4 = ctx.psum("m2_p4", [128, 512])
    w4 = wi_scr.rearrange("k (m kc c) -> m k kc c", kc=KC, c=128)

    def dsl(i):
        return dw[0:64, i * 256:(i + 1) * 256]

    ct_done = {}

    for s in range(4):
        xo = s * BW
        for mc in range(MMC):
            u2 = s * MMC + mc

            @blk.sync
            def _(sync, mc=mc, u2=u2):
                for sem, v in dep:
                    wait(sync, sem, v)
                if u2 == 0:
                    wait(sync, lw[0])
                else:
                    wait(sync, mm)
                dinc(sync.dma_start(
                    out=slab[0][:].rearrange("k (kc c) -> k kc c", c=128),
                    in_=w4[mc]), lw[0])

            @blk.tensor
            def _(tensor, mc=mc, u2=u2, xo=xo):
                for sem, v in dep:
                    wait(tensor, sem, v)
                wait(tensor, lw[0], lwb[0] + 16 * (u2 + 1))
                wait(tensor, ep)
                wait(tensor, sa)
                for kc in range(KC):
                    last = tensor.matmul(
                        p1[u2 % 2][:, :BW], slab[0][:, kc * 128:(kc + 1) * 128],
                        xhat[:, kc * XW + xo:kc * XW + xo + BW],
                        start=(kc == 0), stop=(kc == KC - 1))
                inc(last, mm, 1)

            @blk.scalar
            def _(scalar, s=s, mc=mc, u2=u2):
                wait(scalar, mm)
                wait(scalar, va)
                ps = p1[u2 % 2]
                if mc < 32:
                    wait(scalar, so[0])
                    if cfg["in_bias"]:
                        ins = scalar.activation(gst[0][:], ps[:, 3:BW],
                                                AF.Identity, bias=bin_sb[:, mc:mc + 1])
                    else:
                        ins = scalar.copy(gst[0][:], ps[:, 3:BW])
                    inc(ins, ep, 1)
                    wait(scalar, ep)
                    dinc(scalar.dma_start(
                        out=gate_sp[mc * 128:(mc + 1) * 128, s * 256:(s + 1) * 256],
                        in_=gst[0][:]), so[0])
                elif mc < 66:
                    c0 = (mc - 32) * BW
                    if cfg["in_bias"]:
                        ins = scalar.activation(xbc[:, c0:c0 + BW], ps[:, :BW],
                                                AF.Identity, bias=bin_sb[:, mc:mc + 1])
                    else:
                        ins = scalar.copy(xbc[:, c0:c0 + BW], ps[:, :BW])
                    inc(ins, ep, 1)
                else:
                    if cfg["in_bias"]:
                        ins = scalar.activation(dtf, ps[0:64, :BW], AF.Identity,
                                                bias=bin_sb[0:64, mc:mc + 1])
                    else:
                        ins = scalar.copy(dtf, ps[0:64, :BW])
                    inc(ins, ep, 1)

        for kc in range(34):
            @blk.vector
            def _(vector, kc=kc):
                wait(vector, ep)
                wait(vector, sa)
                if kc == 0:
                    wait(vector, mm)
                inc(vector.tensor_scalar(cacc[:], xbc[:, kc * BW:kc * BW + 256],
                                         cw_sb[:, kc * 4:kc * 4 + 1], None,
                                         OP.mult), va, 1)
                for k in range(1, 4):
                    wait(vector, va)
                    inc(vector.scalar_tensor_tensor(
                        cacc[:], xbc[:, kc * BW + k:kc * BW + k + 256],
                        cw_sb[:, kc * 4 + k:kc * 4 + k + 1], cacc[:],
                        OP.mult, OP.add), va, 1)

            @blk.scalar
            def _(scalar, kc=kc):
                wait(scalar, va)
                wait(scalar, vb)
                wait(scalar, l2)
                if cfg["conv_bias"]:
                    ins = scalar.activation(xcs[:, kc * 256:(kc + 1) * 256],
                                            cacc[:], AF.Silu,
                                            bias=cb_sb[:, kc:kc + 1])
                else:
                    ins = scalar.activation(xcs[:, kc * 256:(kc + 1) * 256],
                                            cacc[:], AF.Silu)
                inc(ins, sa, 1)

        @blk.vector
        def _(vector):
            wait(vector, ep)
            wait(vector, sa)
            inc(vector.tensor_scalar(dsl(0), dtf[:, 3:BW], dtb_sb[64:128, 0:1],
                                     None, OP.add), va, 1)

        @blk.scalar
        def _(scalar):
            wait(scalar, va)
            inc(scalar.activation(dsl(1), dsl(0), AF.Abs), sa, 1)
            wait(scalar, sa)
            inc(scalar.activation(dsl(2), dsl(1), AF.Exp, scale=np.float32(-1.0)),
                sa, 1)
            wait(scalar, sa)
            inc(scalar.activation(dsl(3), dsl(2), AF.Ln, bias=np.float32(1.0)),
                sa, 1)

        @blk.vector
        def _(vector):
            wait(vector, sa)
            inc(vector.tensor_scalar(dsl(4), dsl(0), 0.0, None, OP.max), va, 1)
            wait(vector, va)
            inc(vector.tensor_tensor(dsl(5), dsl(4), dsl(3), OP.add), va, 1)
            wait(vector, va)
            inc(vector.tensor_scalar(dsl(6), dsl(5), acol[0:64, 0:1], None,
                                     OP.mult), va, 1)
            wait(vector, va)
            inc(vector.tensor_tensor_scan(dsl(7), dsl(6), dsl(6), 0.0,
                                          OP.add, OP.bypass), va, 1)

        for jh in range(2):
            @blk.tensor
            def _(tensor, jh=jh):
                wait(tensor, va)
                wait(tensor, sb)
                inc(tensor.transpose(ptf[:, 0:64], dsl(7)[:, jh * 128:jh * 128 + 128],
                                     ctx.ident[:64, :64]), mm, 1)

            @blk.scalar
            def _(scalar, jh=jh):
                wait(scalar, mm)
                inc(scalar.mul(cumt[:, jh * 64:jh * 64 + 64], ptf[:, 0:64], -1.0),
                    sb, 1)


        @blk.scalar
        def _(scalar, s=s):
            wait(scalar, va)
            wait(scalar, l2)
            dinc(scalar.dma_start(out=ctile, in_=dsl(7)), l2)
            ct_done[s] = l2.n
            inc(scalar.activation(dsl(1), dsl(7), AF.Exp, scale=np.float32(-1.0),
                                  bias=dsl(7)[:, 255:256]), sa, 1)
            wait(scalar, sa)
            inc(scalar.activation(dsl(3), dsl(7), AF.Exp), sa, 1)

        @blk.vector
        def _(vector):
            wait(vector, sa)
            inc(vector.tensor_copy(fbf[:], dsl(5)), va, 1)
            wait(vector, va)
            inc(vector.tensor_copy(ddbf[:], dsl(1)), va, 1)

        for jh in range(2):
            @blk.tensor
            def _(tensor, jh=jh):
                wait(tensor, va)
                wait(tensor, ep)
                inc(tensor.transpose(ptf[:, 0:64], dsl(3)[:, jh * 128:jh * 128 + 128],
                                     ctx.ident[:64, :64]), mm, 1)

            @blk.scalar
            def _(scalar, jh=jh):
                wait(scalar, mm)
                inc(scalar.copy(sdt[:, jh * 64:jh * 64 + 64], ptf[:, 0:64]), ep, 1)
        @blk.scalar
        def _(scalar, s=s):
            wait(scalar, ep)
            for jh in range(2):
                wait(scalar, l2)
                dinc(scalar.dma_start(
                    out=sd_sp[s * 256 + jh * 128:s * 256 + jh * 128 + 128, :],
                    in_=sdt[:, jh * 64:jh * 64 + 64]), l2)

        for ci in range(32):
            @blk.tensor
            def _(tensor, ci=ci):
                wait(tensor, va)
                wait(tensor, sa)
                if ci >= 1:
                    wait(tensor, vb)
                inc(tensor.matmul(p2[:], rsb[:, ci * 128:(ci + 1) * 128], fbf[:],
                                  start=True, stop=True), mm, 1)

            @blk.vector
            def _(vector, ci=ci):
                wait(vector, mm)
                inc(vector.tensor_tensor(xdfm[:], xcs[:, ci * 256:(ci + 1) * 256],
                                         p2[:], OP.mult), vb, 1)

            for jh in range(2):
                @blk.tensor
                def _(tensor, ci=ci, jh=jh):
                    wait(tensor, vb)
                    wait(tensor, ep)
                    inc(tensor.transpose(pt[:], xdfm[:, jh * 128:jh * 128 + 128],
                                         ctx.ident_bf[:]), mm, 1)

                @blk.scalar
                def _(scalar, ci=ci, jh=jh):
                    wait(scalar, mm)
                    inc(scalar.copy(xdt[:, jh * 4096 + ci * 128:jh * 4096 + ci * 128 + 128],
                                    pt[:]), ep, 1)

        for jh in range(2):
            @blk.tensor
            def _(tensor, jh=jh):
                wait(tensor, ep)
                wait(tensor, sa)
                inc(tensor.transpose(pt[:], xcs[:, 32 * 256 + jh * 128:32 * 256 + jh * 128 + 128],
                                     ctx.ident_bf[:]), mm, 1)

            @blk.scalar
            def _(scalar, jh=jh):
                wait(scalar, mm)
                inc(scalar.copy(btk[:, jh * 128:(jh + 1) * 128], pt[:]), ep, 1)

        @blk.scalar
        def _(scalar, s=s):
            wait(scalar, sa)
            wait(scalar, l2)
            dinc(scalar.dma_start(out=c_sp[s * 128:(s + 1) * 128, :],
                                  in_=xcs[:, 33 * 256:34 * 256]), l2)

        for jh in range(2):
            @blk.tensor
            def _(tensor, jh=jh):
                wait(tensor, sa)
                if jh >= 1:
                    wait(tensor, ep)
                inc(tensor.matmul(p3[:], xcs[:, 32 * 256 + jh * 128:32 * 256 + jh * 128 + 128],
                                  xcs[:, 33 * 256:34 * 256],
                                  start=True, stop=True), mm, 1)

            @blk.scalar
            def _(scalar, jh=jh):
                wait(scalar, mm)
                inc(scalar.copy(gbuf[:, jh * 256:(jh + 1) * 256], p3[:]), ep, 1)

        for ti in range(8):
            for jh in range(2):
                @blk.tensor
                def _(tensor, ti=ti, jh=jh):
                    wait(tensor, ep)
                    wait(tensor, vb)
                    inc(tensor.matmul(
                        p4[:], ddbf[:, jh * 128:(jh + 1) * 128],
                        rfr2[:, ti * 512:(ti + 1) * 512],
                        start=True, stop=True), mm, 1)

                @blk.vector
                def _(vector, ti=ti, jh=jh):
                    wait(vector, mm)
                    if ti + jh > 0:
                        wait(vector, ctx.S["mm"])
                    inc(vector.tensor_tensor(
                        sstg, xdt[:, jh * 4096 + ti * 512:jh * 4096 + (ti + 1) * 512],
                        p4[:], OP.mult), vb, 1)

                @blk.tensor
                def _(tensor, ti=ti, jh=jh, s=s):
                    wait(tensor, vb)
                    if ti >= 2 and jh == 0:
                        wait(tensor, sb)
                    inc(tensor.matmul(
                        p1[ti % 2][:], btk[:, jh * 128:(jh + 1) * 128],
                        sstg, start=(jh == 0), stop=(jh == 1)), mm, 1)

            @blk.scalar
            def _(scalar, ti=ti, s=s):
                wait(scalar, mm)
                wait(scalar, so[0])
                inc(scalar.copy(stg[0][:], p1[ti % 2][:]), sb, 1)
                wait(scalar, sb)
                dinc(scalar.dma_start(
                    out=spack[s * 129:s * 129 + 128, ti * 512:(ti + 1) * 512],
                    in_=stg[0][:]), so[0])

        @blk.tensor
        def _(tensor):
            wait(tensor, sb)
            inc(tensor.transpose(ptf[0:1, 0:64], dsl(7)[:, 255:256],
                                 ctx.ident[:64, :64]), mm, 1)

        @blk.scalar
        def _(scalar, s=s):
            wait(scalar, mm)
            wait(scalar, so[0])
            inc(scalar.copy(stg[0][0:1, 0:64].bitcast(BF16) if False
                            else stg[0][0:1, 0:64], ptf[0:1, 0:64]), sb, 1)
            wait(scalar, sb)
            wait(scalar, l2)
            dinc(scalar.dma_start(out=spack[s * 129 + 128:s * 129 + 129, 0:64],
                                  in_=stg[0][0:1, 0:64]), l2)

        for h in range(64):
            if h % 2 == 0:
                @blk.scalar
                def _(scalar, s=s, h=h):
                    if s + h > 0:
                        wait(scalar, mm)
                    wait(scalar, l2)
                    wait(scalar, l2, ct_done[s])
                    dinc(scalar.dma_start(
                        out=crow[:],
                        in_=ctile.rearrange("a b -> (a b)")[h * 256:(h + 2) * 256]
                            .rearrange("(o n) -> o n", o=1)), l2)

            for jh in range(2):
                @blk.tensor
                def _(tensor, h=h, jh=jh):
                    wait(tensor, l2)
                    wait(tensor, vb)
                    wait(tensor, sa)
                    tensor.matmul(p2[:] if jh == 0 else p3[:], ctx.ones_row[:],
                                  crow[:, (h % 2) * 256:(h % 2) * 256 + 256],
                                  start=True, stop=False)
                    inc(tensor.matmul(p2[:] if jh == 0 else p3[:], ctx.ident_bf[:],
                                      mtri[:, jh * 256:(jh + 1) * 256],
                                      start=False, stop=True), mm, 1)

                @blk.scalar
                def _(scalar, h=h, jh=jh):
                    wait(scalar, mm)
                    wait(scalar, vb)
                    inc(scalar.activation(ebuf[:],
                                          p2[:] if jh == 0 else p3[:], AF.Exp,
                                          bias=cumt[:, jh * 64 + h:jh * 64 + h + 1]),
                        sa, 1)

                @blk.vector
                def _(vector, h=h, jh=jh):
                    wait(vector, sa)
                    inc(vector.tensor_tensor(ebuf[:],
                                             gbuf[:, jh * 256:(jh + 1) * 256],
                                             ebuf[:], OP.mult), vb, 1)

                @blk.tensor
                def _(tensor, h=h, jh=jh):
                    wait(tensor, vb)
                    wait(tensor, sa)
                    if jh == 0 and h == 0:
                        wait(tensor, sb)
                    for ih in range(2):
                        last = tensor.matmul(
                            p1[0][:, ih * 64:ih * 64 + 64],
                            ebuf[:, ih * 128:ih * 128 + 128],
                            xdt[:, jh * 4096 + h * 64:jh * 4096 + h * 64 + 64],
                            start=(jh == 0 and ih == 0),
                            stop=(jh == 1 and ih == 1))
                    inc(last, mm, 1)

            for ih in range(2):
                @blk.tensor
                def _(tensor, h=h, ih=ih):
                    wait(tensor, vb)
                    ci, half = divmod(h, 2)
                    inc(tensor.transpose(
                        pt[:, 0:64],
                        xcs[half * 64:half * 64 + 64,
                            ci * 256 + ih * 128:ci * 256 + ih * 128 + 128],
                        ctx.ident_bf[half * 64:half * 64 + 64,
                                     half * 64:half * 64 + 64]), mm, 1)

                @blk.scalar
                def _(scalar, h=h, ih=ih):
                    wait(scalar, mm)
                    wait(scalar, vb)
                    inc(scalar.copy(ytmp[:], p1[0][:, ih * 64:ih * 64 + 64]), sa, 1)

                @blk.vector
                def _(vector, h=h, ih=ih, s=s):
                    wait(vector, mm)
                    wait(vector, sa)
                    if h == 0 and ih == 0:
                        wait(vector, so[0])
                        wait(vector, so[1])
                    inc(vector.scalar_tensor_tensor(
                        ybuf[:, ih * 4096 + h * 64:ih * 4096 + h * 64 + 64],
                        pt[:, 0:64], dbc[:, h:h + 1], ytmp[:],
                        OP.mult, OP.add), vb, 1)

        @blk.scalar
        def _(scalar, s=s):
            wait(scalar, vb)
            for ih in range(2):
                dinc(scalar.dma_start(
                    out=yp_sp[s * 256 + ih * 128:s * 256 + ih * 128 + 128, :],
                    in_=ybuf[:, ih * 4096:(ih + 1) * 4096]), so[ih])

    @blk.gpsimd
    def _(gpsimd):
        wait(gpsimd, ep)
        wait(gpsimd, so[0])
        wait(gpsimd, so[1])
        wait(gpsimd, l2)
        inc(gpsimd.engine_nop(), ctx.S["cc"], 1)

    ctx.pop()
    return [(ctx.S["cc"], ctx.S["cc"].n)]

def emit_mamba_m3(ctx, variant, resid, sag, pmb, zmk, rfree, cperm, gate_sp,
                  yp_sp, c_sp, sd_sp, wo_scr, nw_sb, bo_sb, dep=(), xscr=None):
    cfg = mamba_cfg(variant)
    blk = ctx.block
    mm, ep, va, vb, sa, sb, ga = (ctx.S[k] for k in
                                  ("mm", "ep", "va", "vb", "sa", "sb", "ga"))
    lw = [ctx.S["lda"], ctx.S["ldb"]]
    l2 = ctx.S["ld2"]
    ld = ctx.S["ld"]
    ctx.push()
    alm = ctx.sbuf("m3_alm", [16, 64], BF16)
    alt = ctx.sbuf("m3_alt", [64, 40], F32)
    wrep = ctx.sbuf("m3_wrep", [64, 128], BF16)
    o64 = ctx.sbuf("m3_o64", [64, 128], F32)
    wbc = ctx.sbuf("m3_wbc", [128, 4096], BF16)
    sjb = [ctx.sbuf("m3_sj", [128, 4096], BF16) for _ in range(2)]
    P = ctx.sbuf("m3_P", [128, 4096], F32)
    Pb = ctx.sbuf("m3_Pb", [128, 4096], BF16)
    tmb = ctx.sbuf("m3_tmb", [128, 4096], F32)
    ytk = [ctx.sbuf("m3_ytk", [128, 4096], BF16) for _ in range(2)]
    sdc = ctx.sbuf("m3_sdc", [128, 2 * 64], BF16)
    cfm = ctx.sbuf("m3_cfm", [128, 256], BF16)
    ycf = ctx.sbuf("m3_ycf", [128, 256], BF16)
    gfm = ctx.sbuf("m3_gfm", [128, 256], BF16)
    gsil = ctx.sbuf("m3_gsil", [128, 256], BF16)
    sqf = ctx.sbuf("m3_sqf", [128, 256], F32)
    ynrm = (ctx.sbuf("m3_ynrm", [128, 32 * 256], BF16) if xscr is None
            else xscr[:, 0:32 * 256])
    rowst = ctx.sbuf("m3_rows", [1, 512], F32)
    rbc = ctx.sbuf("m3_rbc", [128, 256], F32)
    slab = [(xscr[:, 8192:12288] if xscr is not None
             else ctx.sbuf("m3_sl", [128, 32 * 128], BF16)),
            ctx.sbuf("m3_sl", [128, 32 * 128], BF16)]
    pA = ctx.psum("m3_pA", [128, 512])
    pB = [ctx.psum("m3_pB", [128, 256]) for _ in range(2)]
    pt = ctx.psum("m3_pt", [128, 128], BF16)
    ptf = ctx.psum("m3_ptf", [128, 128])
    pr = ctx.psum("m3_pr", [1, 512])
    lwb = [x.n for x in lw]
    w4o = wo_scr.rearrange("k (m kc c) -> m k kc c", kc=NIC, c=128)

    @blk.sync
    def _(sync):
        for sem, v in dep:
            wait(sync, sem, v)
        wait(sync, ld)
        dinc(sync.dma_start(
            out=alm[:],
            in_=sag.rearrange("(rc n) d -> rc n d", n=129)[:, 128, 0:64]), ld)

    @blk.tensor
    def _(tensor):
        for sem, v in dep:
            wait(tensor, sem, v)
        wait(tensor, ld)
        wait(tensor, ctx.S["va"], ctx.consts_va)
        inc(tensor.matmul(ptf[0:16, 0:64], cperm[:], alm[:],
                          start=True, stop=True), mm, 1)

    @blk.scalar
    def _(scalar):
        wait(scalar, mm)
        inc(scalar.copy(wrep[0:16, 0:64].bitcast(F32) if False
                        else wrep[0:16, 0:64], ptf[0:16, 0:64]), sb, 1)

    @blk.tensor
    def _(tensor):
        wait(tensor, sb)
        inc(tensor.transpose(pt[0:64, 0:16], wrep[0:16, 0:64],
                             ctx.ident_bf[:16, :16]), mm, 1)

    @blk.scalar
    def _(scalar):
        wait(scalar, mm)
        inc(scalar.copy(alt[:, 0:16], pt[0:64, 0:16]), sb, 1)

    @blk.vector
    def _(vector):
        wait(vector, sb)
        inc(vector.tensor_tensor_scan(alt[:, 16:32], alt[:, 0:16], alt[:, 0:16],
                                      0.0, OP.add, OP.bypass), va, 1)
        wait(vector, va)
        inc(vector.memset(o64[:], 1.0), va, 1)

    for s in range(4):
        nj = 4 * (s + 1)

        @blk.vector
        def _(vector, s=s):
            wait(vector, va)
            if s >= 1:
                wait(vector, vb)
            inc(vector.tensor_tensor(alt[:, 32 + 0:32 + 16] if False
                                     else tmb[0:64, 0:16],
                                     alt[:, 16:32], zmk[:, s * 16:(s + 1) * 16],
                                     OP.mult), va, 1)
            wait(vector, va)
            inc(vector.tensor_reduce(alt[:, 32:33], tmb[0:64, 0:16],
                                     axis=mybir.AxisListType.X, op=OP.add), va, 1)
            wait(vector, va)
            inc(vector.memset(P[:], 0.0), va, 1)

        for m in range(nj):
            @blk.sync
            def _(sync, s=s, m=m):
                if s + m > 0:
                    wait(sync, vb)
                else:
                    wait(sync, lw[m % 2])
                r, sl = BLK_OWNER[m]
                dinc(sync.dma_start(out=sjb[m % 2][:],
                                    in_=sag[r * 516 + sl * 129:r * 516 + sl * 129 + 128, :]),
                     lw[m % 2])

            @blk.vector
            def _(vector, s=s, m=m):
                wait(vector, va)
                if m >= 1:
                    wait(vector, vb)
                inc(vector.tensor_tensor(alt[:, 33:34], alt[:, 32:33],
                                         alt[:, 16 + m:17 + m], OP.subtract),
                    vb, 1)

            @blk.scalar
            def _(scalar, s=s, m=m):
                wait(scalar, vb)
                inc(scalar.activation(alt[:, 34:35], alt[:, 33:34], AF.Exp,
                                      bias=pmb[:, s * 16 + m:s * 16 + m + 1]),
                    sa, 1)

            @blk.vector
            def _(vector, s=s, m=m):
                wait(vector, sa)
                inc(vector.tensor_scalar(wrep[:], o64[:], alt[:, 34:35], None,
                                         OP.mult), vb, 1)

            for t8 in range(8):
                @blk.tensor
                def _(tensor, t8=t8):
                    wait(tensor, vb)
                    if t8 >= 1:
                        wait(tensor, ep)
                    inc(tensor.matmul(pA[:], wrep[:],
                                      rfree[:, t8 * 512:(t8 + 1) * 512],
                                      start=True, stop=True), mm, 1)

                @blk.scalar
                def _(scalar, t8=t8):
                    wait(scalar, mm)
                    inc(scalar.copy(wbc[:, t8 * 512:(t8 + 1) * 512], pA[:]), ep, 1)

            @blk.vector
            def _(vector, s=s, m=m):
                wait(vector, ep)
                wait(vector, lw[m % 2], lwb[m % 2] + 16 * ((s * (s + 1) * 2 + m) // 2 + 1)
                     if False else None)
                wait(vector, lw[m % 2])
                inc(vector.tensor_tensor(tmb[:], sjb[m % 2][:], wbc[:], OP.mult),
                    vb, 1)
                wait(vector, vb)
                inc(vector.tensor_tensor(P[:], P[:], tmb[:], OP.add), vb, 1)

        @blk.vector
        def _(vector, s=s):
            wait(vector, vb)
            inc(vector.tensor_copy(Pb[:], P[:]), vb, 1)

        @blk.sync
        def _(sync, s=s):
            wait(sync, vb)
            if s >= 1:
                wait(sync, ga)
            for ih in range(2):
                wait(sync, ld)
                dinc(sync.dma_start(
                    out=ytk[ih][:],
                    in_=yp_sp[s * 256 + ih * 128:s * 256 + ih * 128 + 128, :]), ld)
                wait(sync, ld)
                dinc(sync.dma_start(
                    out=sdc[:, ih * 64:(ih + 1) * 64],
                    in_=sd_sp[s * 256 + ih * 128:s * 256 + ih * 128 + 128, :]), ld)
            wait(sync, ld)
            dinc(sync.dma_start(out=cfm[:], in_=c_sp[s * 128:(s + 1) * 128, :]), ld)

        for ih in range(2):
            for h in range(64):
                @blk.tensor
                def _(tensor, s=s, ih=ih, h=h):
                    wait(tensor, ld)
                    wait(tensor, vb)
                    if ih + h > 0:
                        wait(tensor, ga)
                    inc(tensor.matmul(pB[(ih * 64 + h) % 2][:, 0:64],
                                      cfm[:, ih * 128:(ih + 1) * 128],
                                      Pb[:, h * 64:(h + 1) * 64],
                                      start=True, stop=True), mm, 1)

                @blk.vector
                def _(vector, s=s, ih=ih, h=h):
                    wait(vector, mm)
                    inc(vector.scalar_tensor_tensor(
                        ytk[ih][:, h * 64:(h + 1) * 64],
                        pB[(ih * 64 + h) % 2][:, 0:64], sdc[:, ih * 64 + h:ih * 64 + h + 1],
                        ytk[ih][:, h * 64:(h + 1) * 64], OP.mult, OP.add), ga, 1)

        for ci in range(NIC):
            @blk.sync
            def _(sync, s=s, ci=ci):
                wait(sync, ga)
                wait(sync, mm)
                if ci >= 1:
                    wait(sync, sa)
                dinc(sync.dma_start(
                    out=gfm[:],
                    in_=gate_sp[ci * 128:(ci + 1) * 128, s * 256:(s + 1) * 256]),
                    l2)

            for ih in range(2):
                @blk.tensor
                def _(tensor, s=s, ci=ci, ih=ih):
                    wait(tensor, ga)
                    wait(tensor, ep)
                    inc(tensor.transpose(pt[:], ytk[ih][:, ci * 128:(ci + 1) * 128],
                                         ctx.ident_bf[:]), mm, 1)

                @blk.scalar
                def _(scalar, ci=ci, ih=ih):
                    wait(scalar, mm)
                    if ih == 0:
                        wait(scalar, vb)
                    inc(scalar.copy(ycf[:, ih * 128:(ih + 1) * 128], pt[:]), ep, 1)

            @blk.scalar
            def _(scalar, s=s, ci=ci):
                wait(scalar, l2)
                inc(scalar.activation(gsil[:], gfm[:], AF.Silu), sa, 1)

            @blk.vector
            def _(vector, s=s, ci=ci):
                wait(vector, ep)
                wait(vector, sa)
                if cfg["gate_first"]:
                    inc(vector.tensor_tensor(ynrm[:, ci * 256:(ci + 1) * 256],
                                             ycf[:], gsil[:], OP.mult), vb, 1)
                else:
                    inc(vector.tensor_copy(ynrm[:, ci * 256:(ci + 1) * 256],
                                           ycf[:]), vb, 1)
                wait(vector, vb)
                inc(vector.tensor_tensor(sqf[:], ynrm[:, ci * 256:(ci + 1) * 256],
                                         ynrm[:, ci * 256:(ci + 1) * 256],
                                         OP.mult), vb, 1)

            @blk.tensor
            def _(tensor, s=s, ci=ci):
                wait(tensor, vb)
                inc(tensor.matmul(pr[:, 0:256], ctx.ones_col[:], sqf[:],
                                  start=(ci == 0), stop=(ci == NIC - 1)), mm, 1)

        @blk.vector
        def _(vector, s=s):
            wait(vector, mm)
            inc(vector.tensor_scalar(rowst[:, 0:256], pr[:, 0:256], 1.0 / 4096.0,
                                     None, OP.mult), vb, 1)

        @blk.scalar
        def _(scalar, s=s):
            wait(scalar, vb)
            inc(scalar.activation(rowst[:, 256:512], rowst[:, 0:256], AF.Sqrt,
                                  bias=np.float32(EPS)), sa, 1)

        @blk.vector
        def _(vector, s=s):
            wait(vector, sa)
            inc(vector.reciprocal(rowst[:, 256:512], rowst[:, 256:512]), vb, 1)

        @blk.tensor
        def _(tensor, s=s):
            wait(tensor, vb)
            inc(tensor.matmul(pA[:, 0:256], ctx.ones_row[:], rowst[:, 256:512],
                              start=True, stop=True), mm, 1)

        @blk.scalar
        def _(scalar, s=s):
            wait(scalar, mm)
            inc(scalar.copy(rbc[:], pA[:, 0:256]), ep, 1)

        for ci in range(NIC):
            if not cfg["gate_first"]:
                @blk.sync
                def _(sync, s=s, ci=ci):
                    wait(sync, ep)
                    wait(sync, sa)
                    dinc(sync.dma_start(
                        out=gfm[:],
                        in_=gate_sp[ci * 128:(ci + 1) * 128, s * 256:(s + 1) * 256]),
                        l2)

                @blk.scalar
                def _(scalar, s=s, ci=ci):
                    wait(scalar, l2)
                    wait(scalar, vb)
                    inc(scalar.activation(gsil[:], gfm[:], AF.Silu), sa, 1)

            @blk.vector
            def _(vector, s=s, ci=ci):
                wait(vector, ep)
                c0 = ci * 256
                if cfg["gate_first"]:
                    inc(vector.scalar_tensor_tensor(
                        ynrm[:, c0:c0 + 256], ynrm[:, c0:c0 + 256],
                        nw_sb[:, ci:ci + 1], rbc[:], OP.mult, OP.mult), vb, 1)
                else:
                    wait(vector, sa)
                    inc(vector.scalar_tensor_tensor(
                        ynrm[:, c0:c0 + 256], ynrm[:, c0:c0 + 256],
                        nw_sb[:, ci:ci + 1], rbc[:], OP.mult, OP.mult), vb, 1)
                    wait(vector, vb)
                    inc(vector.tensor_tensor(ynrm[:, c0:c0 + 256],
                                             ynrm[:, c0:c0 + 256], gsil[:],
                                             OP.mult), vb, 1)

        for mo in range(KC):
            u3 = s * KC + mo

            @blk.sync
            def _(sync, s=s, mo=mo, u3=u3):
                wait(sync, vb)
                if u3 < 2:
                    wait(sync, lw[u3 % 2])
                else:
                    wait(sync, mm)
                dinc(sync.dma_start(
                    out=slab[u3 % 2][:].rearrange("k (kc c) -> k kc c", c=128),
                    in_=w4o[mo]), lw[u3 % 2])

            @blk.tensor
            def _(tensor, s=s, mo=mo, u3=u3):
                wait(tensor, lw[u3 % 2])
                wait(tensor, vb)
                if u3 >= 2:
                    wait(tensor, ga)
                for kc in range(NIC):
                    last = tensor.matmul(
                        pB[u3 % 2][:], slab[u3 % 2][:, kc * 128:(kc + 1) * 128],
                        ynrm[:, kc * 256:(kc + 1) * 256],
                        start=(kc == 0), stop=(kc == NIC - 1))
                inc(last, mm, 1)

            @blk.vector
            def _(vector, s=s, mo=mo, u3=u3):
                wait(vector, mm)
                r = resid[:, mo * 1024 + s * 256:mo * 1024 + s * 256 + 256]
                sc = bo_sb[:, mo:mo + 1] if bo_sb is not None else 0.0
                inc(vector.scalar_tensor_tensor(r, pB[u3 % 2][:], sc, r,
                                                OP.add, OP.add), ga, 1)

    ctx.pop()
    return [(ctx.S["ga"], ctx.S["ga"].n)]

def host_hsel(core):
    g = core % GROUP
    sel = np.zeros((48, 12), np.float32)
    for s, b in enumerate(blocks_of(g)):
        if b == 0:
            continue
        r, sl = BLK_OWNER[b - 1]
        for i in range(3):
            sel[r * 12 + sl * 3 + i, s * 3 + i] = 1.0
    return sel.astype(BF)

def host_pmb(core):
    g = core % GROUP
    m = np.full((64, 64), -1e30, np.float32)
    for s, b in enumerate(blocks_of(g)):
        for j in range(16):
            if j < b:
                m[:, s * 16 + j] = 0.0
    return m

def host_zmk(core):
    g = core % GROUP
    m = np.zeros((64, 64), np.float32)
    for s, b in enumerate(blocks_of(g)):
        if b >= 1:
            m[:, s * 16 + (b - 1)] = 1.0
    return m

def host_rfree():
    r = np.zeros((64, 4096), np.float32)
    for h in range(64):
        r[h, h * 64:(h + 1) * 64] = 1.0
    return r.astype(BF)

def host_cperm():
    p = np.zeros((16, 16), np.float32)
    for m in range(16):
        r, sl = BLK_OWNER[m]
        p[r * 4 + sl, m] = 1.0
    return p.astype(BF)

def host_rall():
    r = np.zeros((64, 32 * 128), np.float32)
    for ci in range(32):
        for q in range(128):
            r[2 * ci + q // 64, ci * 128 + q] = 1.0
    return r.astype(BF)

def host_mtri():
    m = np.zeros((128, 2 * 256), np.float32)
    for jh in range(2):
        j = jh * 128 + np.arange(128)[:, None]
        i = np.arange(256)[None, :]
        m[:, jh * 256:(jh + 1) * 256] = np.where(j > i, -1e30, 0.0)
    return m.astype(BF)

def build_m1_launch():
    L = Launch()
    ctx = L.ctx
    rin = L.input("rin", [D, 1024], F32)
    ident = L.input("ident", [128, 128], F32)
    hpack = L.output("hpack", [12, D], BF16)
    emit_consts(ctx, ident)
    resid, dep_r = emit_resid_in(ctx, rin)
    xhat = ctx.sbuf("xhat", [128, KC * XW], BF16)
    ga_ln = emit_ln(ctx, resid, xhat, "mamba", dep=[dep_r])
    fin = emit_mamba_m1(ctx, xhat, hpack, dep=[(ctx.S["ga"], ga_ln)])

    @ctx.block.sync
    def _(sync):
        for sem, v in fin:
            wait(sync, sem, v)
    return L.finish()

def build_m2_launch(variant):
    cfg = mamba_cfg(variant)
    L = Launch()
    ctx = L.ctx
    rin = L.input("rin", [D, 1024], F32)
    wi = L.input("wi", [128, MMC * KC * 128], F32)
    cw = L.input("cwt", [4, 4352], F32)
    alog = L.input("alog", [128], F32)
    dtb = L.input("dtb", [128], F32)
    dvec = L.input("dvec", [64], F32)
    hgat = L.input("hgat", [48, D], BF16)
    hsel = L.input("hsel", [48, 12], BF16)
    rall = L.input("rall", [64, 32 * 128], BF16)
    mtri = L.input("mtri", [128, 512], BF16)
    ident = L.input("ident", [128, 128], F32)
    gate_sp = L.output("gate_sp", [4096, 1024], BF16)
    yp_sp = L.output("yp_sp", [1024, 4096], BF16)
    c_sp = L.output("c_sp", [512, 256], BF16)
    sd_sp = L.output("sd_sp", [1024, 64], BF16)
    spack = L.output("spack", [4 * 129, 4096], BF16)
    wiscr = ctx.dram("wiscr", [128, MMC * KC * 128], BF16)
    ctile = ctx.dram("ctile", [64, 256], F32)
    emit_consts(ctx, ident)
    resid, dep_r = emit_resid_in(ctx, rin)
    xhat = ctx.sbuf("xhat", [128, KC * XW], BF16)
    ga_ln = emit_ln(ctx, resid, xhat, "mamba", dep=[dep_r])
    dep = [(ctx.S["ga"], ga_ln)]
    if cfg["in_bias"]:
        binv = L.input("bin", [MMC * 128], F32)
        bin_sb, _ = emit_load_colvec(ctx, binv, MMC, "bin")
    else:
        bin_sb = None
    cwl = []
    for k in range(4):
        c_sb, dep_c = emit_load_colvec(ctx, cw[k], 34, f"cw{k}")
        cwl.append(c_sb)
    cb_sb = None
    if cfg["conv_bias"]:
        cbv = L.input("cb", [4352], F32)
        cb_sb, _ = emit_load_colvec(ctx, cbv, 34, "cb")
    acol_in, _ = emit_load_colvec(ctx, alog, 1, "alogc")
    dtb_sb, _ = emit_load_colvec(ctx, dtb, 1, "dtbc")
    drow, dep_d = emit_load_row(ctx, dvec, 64, "drow", F32)
    rsb, dep_rs = emit_load_tile(ctx, rall, [64, 32 * 128], "rsb", BF16)
    rfr2 = rsb
    mtr = ctx._ln["coef"][:].bitcast(BF16)[:, 0:512]

    @ctx.block.sync
    def _(sync):
        wait(sync, ctx.S["ga"], ga_ln)
        wait(sync, ctx.S["ld"])
        dinc(sync.dma_start(out=mtr, in_=mtri), ctx.S["ld"])
    dep_mt = (ctx.S["ld"], ctx.S["ld"].n)
    dep += [dep_d, dep_rs, dep_mt]
    acol = ctx.sbuf("acol", [128, 1], F32)
    dbc = ctx.sbuf("dbc", [128, 64], F32)
    pdb = ctx.psum("pdb", [128, 64])

    @ctx.block.scalar
    def _(scalar):
        wait(scalar, ctx.S["sb"])
        inc(scalar.activation(acol[:], acol_in[:, 0:1], AF.Exp,
                              scale=np.float32(1.0)), ctx.S["sa"], 1)

    @ctx.block.vector
    def _(vector):
        wait(vector, ctx.S["sa"])
        inc(vector.tensor_scalar(acol[:], acol[:], -1.0, None, OP.mult),
            ctx.S["va"], 1)

    @ctx.block.tensor
    def _(tensor):
        for sem, v in [dep_d]:
            wait(tensor, sem, v)
        wait(tensor, ctx.S["va"], ctx.consts_va)
        wait(tensor, ctx.S["sb"])
        inc(tensor.matmul(pdb[:], ctx.ones_row[:], drow[:],
                          start=True, stop=True), ctx.S["mm"], 1)

    @ctx.block.scalar
    def _(scalar):
        wait(scalar, ctx.S["mm"])
        inc(scalar.copy(dbc[:], pdb[:]), ctx.S["sb"], 1)
    dep += [(ctx.S["va"], ctx.S["va"].n), (ctx.S["sb"], ctx.S["sb"].n)]

    cwp = ctx.sbuf("cwp", [128, 136], F32)

    @ctx.block.vector
    def _(vector):
        wait(vector, ctx.S["sb"])
        for k in range(4):
            ins = vector.tensor_copy(
                cwp[:].rearrange("p (kc t) -> p kc t", t=4)[:, :, k], cwl[k][:, 0:34])
        inc(ins, ctx.S["va"], 1)
    dep += [(ctx.S["va"], ctx.S["va"].n)]

    cv = emit_convert(ctx, wi, wiscr.ap(), KC, MMC, F32)
    dep += emit_halo_select(ctx, xhat, hgat, hsel, dep=list(dep) + cv + [(ctx.S['va'], ctx.S['va'].n), (ctx.S['ga'], ctx.S['ga'].n)])
    fin = emit_mamba_m2(ctx, variant, xhat, wiscr.ap(), bin_sb, cwp, cb_sb,
                        dtb_sb, acol, dbc, rsb, mtr, rfr2, gate_sp, yp_sp, c_sp,
                        sd_sp, spack, ctile.ap(), dep=dep + cv)

    @ctx.block.sync
    def _(sync):
        for sem, v in fin:
            wait(sync, sem, v)
    return L.finish()

def build_m3_launch(variant):
    cfg = mamba_cfg(variant)
    L = Launch()
    ctx = L.ctx
    rin = L.input("rin", [D, 1024], F32)
    wo = L.input("wo", [128, KC * NIC * 128], F32)
    nw = L.input("nw", [4096], F32)
    sag = L.input("sag", [4 * 516, 4096], BF16)
    gate_sp = L.input("gate_sp", [4096, 1024], BF16)
    yp_sp = L.input("yp_sp", [1024, 4096], BF16)
    c_sp = L.input("c_sp", [512, 256], BF16)
    sd_sp = L.input("sd_sp", [1024, 64], BF16)
    pmbi = L.input("pmb", [64, 64], F32)
    zmki = L.input("zmk", [64, 64], F32)
    rfri = L.input("rfree", [64, 4096], BF16)
    ident = L.input("ident", [128, 128], F32)
    rout = L.output("rout", [D, 1024], F32)
    woscr = ctx.dram("woscr", [128, KC * NIC * 128], BF16)
    emit_consts(ctx, ident)
    resid, dep_r = emit_resid_in(ctx, rin)
    dep = [dep_r]
    nw_sb, _ = emit_load_colvec(ctx, nw, NIC, "nw")
    if cfg["out_bias"]:
        bov = L.input("bo", [D], F32)
        bo_sb, dep_b = emit_load_colvec(ctx, bov, KC, "bo")
        dep.append(dep_b)
    else:
        bo_sb = None
    cpi = L.input("cperm", [16, 16], BF16)
    cpe, dep0 = emit_load_tile(ctx, cpi, [16, 16], "cpes", BF16)
    dep.append(dep0)
    pmb, dep1 = emit_load_tile(ctx, pmbi, [64, 64], "pmbs", F32)
    zmk, dep2 = emit_load_tile(ctx, zmki, [64, 64], "zmks", F32)
    rfr, dep3 = emit_load_tile(ctx, rfri, [64, 4096], "rfrs", BF16)
    dep += [dep1, dep2, dep3]
    cv = emit_convert(ctx, wo, woscr.ap(), NIC, KC, F32)
    fin = emit_mamba_m3(ctx, variant, resid, sag, pmb, zmk, rfr, cpe, gate_sp,
                        yp_sp, c_sp, sd_sp, woscr.ap(), nw_sb, bo_sb,
                        dep=dep + cv)
    emit_resid_out(ctx, resid, rout, dep=fin)
    return L.finish()


def _barrier(ctx, deps):
    deps = [(sem, v) for sem, v in deps]

    for name in ("sync", "tensor", "scalar", "vector", "gpsimd"):
        @getattr(ctx.block, name)
        def _(eng):
            for sem, v in deps:
                wait(eng, sem, v)

def _emit_ag(ctx, src, dst, deps):
    cc = ctx.S["cc"]

    @ctx.block.gpsimd
    def _(g):
        for sem, v in deps:
            wait(g, sem, v)
        g.collective_compute("AllGather", AluOpType.bypass,
                             replica_groups=[[0, 1, 2, 3], [4, 5, 6, 7]],
                             ins=[src.ap().opt()],
                             outs=[dst.ap().opt()]).then_inc(cc.h)
        cc.n += 1
    return (cc, cc.n)

def build_mega():
    L = Launch()
    ctx = L.ctx
    rin = L.input("rin", [D, 1024], F32)
    ident = L.input("ident", [128, 128], F32)
    cosd = L.input("cos", [128, 1024], F32)
    sind = L.input("sinm", [128, 1024], F32)
    masks = L.input("masks", [4, 2, 4, 2, 128, 512], BF16)
    eqc = L.input("eqc", [128, 256], BF16)
    wq_a = L.input("wq_a", [128, KC * KC * 128], F32)
    wk_a = L.input("wk_a", [128, 4 * KC * 128], F32)
    wv_a = L.input("wv_a", [128, 4 * KC * 128], F32)
    wo_a = L.input("wo_a", [128, KC * KC * 128], F32)
    gw = L.input("gwt", [128, 64 * KC * 128], U8)
    uw = L.input("uwt", [128, 64 * KC * 128], U8)
    dwq = L.input("dwt", [128, KC * 64 * 128], U8)
    gsc = L.input("gsc", [8192], F32)
    usc = L.input("usc", [8192], F32)
    dsc = L.input("dsc", [D], F32)
    wq_b = L.input("wq_b", [128, KC * KC * 128], F32)
    wk_b = L.input("wk_b", [128, KC * KC * 128], F32)
    wv_b = L.input("wv_b", [128, KC * KC * 128], F32)
    wo_b = L.input("wo_b", [128, KC * KC * 128], F32)
    bq_b = L.input("bq_b", [D], F32)
    bk_b = L.input("bk_b", [2048], F32)
    bv_b = L.input("bv_b", [2048], F32)
    bo_b = L.input("bo_b", [D], F32)
    w0 = L.input("w0t", [128, 64 * KC * 128], F32)
    w1 = L.input("w1t", [128, KC * 64 * 128], F32)
    b0 = L.input("b0", [8192], F32)
    b1 = L.input("b1", [D], F32)
    hseli = L.input("hsel", [48, 12], BF16)
    ralli = L.input("rall", [64, 32 * 128], BF16)
    mtrii = L.input("mtri", [128, 512], BF16)
    pmbi = L.input("pmb", [64, 64], F32)
    zmki = L.input("zmk", [64, 64], F32)
    cpi = L.input("cperm", [16, 16], BF16)
    mam_in = {}
    for v in ("a", "b"):
        mam_in[v] = dict(
            wi=L.input(f"wi_{v}", [128, MMC * KC * 128], F32),
            cw=L.input(f"cwt_{v}", [4, 4352], F32),
            alog=L.input(f"alog_{v}", [128], F32),
            dtb=L.input(f"dtb_{v}", [128], F32),
            dvec=L.input(f"dvec_{v}", [64], F32),
            nw=L.input(f"nw_{v}", [4096], F32),
            wo=L.input(f"wom_{v}", [128, KC * NIC * 128], F32))
    mam_in["a"]["bin"] = L.input("bin_a", [MMC * 128], F32)
    mam_in["a"]["bo"] = L.input("bom_a", [D], F32)
    mam_in["b"]["cb"] = L.input("cb_b", [4352], F32)
    rout = L.output("rout", [D, 1024], F32)

    emit_consts(ctx, ident)
    resid, dep_r = emit_resid_in(ctx, rin)
    xhat = ctx.sbuf("xhat", [128, KC * XW], BF16)

    def _ln_alloc():
        ctx._ln = dict(
            sq=[ctx.sbuf("ln_sq", [128, 256], F32) for _ in range(2)],
            stat=ctx.sbuf("ln_stat", [1, 1024], F32),
            coef=ctx.sbuf("ln_coef", [128, 512], F32),
            tmp=ctx.sbuf("ln_tmp", [128, 2048], F32))

    def _ln_free():
        del ctx._ln
    _barrier(ctx, [dep_r])

    def attn_phase(variant, wq, wk, wv, wo, bq=None, bk=None, bv=None, bo=None):
        cfg = attn_cfg(variant)
        kvd, vD, KR = cfg["kvd"], cfg["vD"], cfg["KR"]
        kpack_d = ctx.dram(f"kp_{variant}", [KR, 1024], BF16)
        vpack_d = ctx.dram(f"vp_{variant}", [1024, vD], BF16)
        kag_d = ctx.dram(f"kag_{variant}", [4 * KR, 1024], BF16)
        vag_d = ctx.dram(f"vag_{variant}", [4096, vD], BF16)
        kscr = ctx.dram(f"kscr_{variant}", [128, (kvd // 128) * KC * 128], BF16)
        vscr = ctx.dram(f"vscr_{variant}", [128, (vD // 128) * KC * 128], BF16)
        qscr = ctx.dram(f"qscr_{variant}", [128, KC * KC * 128], BF16)
        oscr = ctx.dram(f"oscr_{variant}", [128, KC * KC * 128], BF16)
        ctx.push()
        _ln_alloc()
        ctx.push()
        ga_ln = emit_ln(ctx, resid, xhat, "plain", dep=[])
        dep = [(ctx.S["ga"], ga_ln)]
        if cfg["bias"]:
            bk_sb, _ = emit_load_colvec(ctx, bk, kvd // 128, "bk")
            bv_row, dep_bv = emit_load_row(ctx, bv, vD, "bv")
            dep.append(dep_bv)
            cos1 = sinm1 = None
        else:
            bk_sb = None
            bv_row = ctx.sbuf("bv0", [1, vD], BF16)
            cos1, dep_c = emit_load_tile(ctx, cosd, [128, 1024], "cos", F32)
            sinm1, dep_s = emit_load_tile(ctx, sind, [128, 1024], "sinm", F32)
            dep += [dep_c, dep_s]

            @ctx.block.vector
            def _(vector):
                inc(vector.memset(bv_row[:], 0.0), ctx.S["va"], 1)
            dep.append((ctx.S["va"], ctx.S["va"].n))
        cv1 = emit_convert(ctx, wk, kscr.ap(), KC, kvd // 128, F32)
        cv2 = emit_convert(ctx, wv, vscr.ap(), KC, vD // 128, F32)
        fin_kv = emit_attn_kv(ctx, variant, xhat, kscr.ap(), vscr.ap(), bk_sb,
                              bv_row, cos1, sinm1, kpack_d.ap(), vpack_d.ap(),
                              dep=dep + cv1 + cv2)
        agk = _emit_ag(ctx, kpack_d, kag_d, fin_kv)
        agv = _emit_ag(ctx, vpack_d, vag_d, [])
        _barrier(ctx, list(fin_kv) + [agk, agv, (ctx.S["ga"], ga_ln)])
        ctx.pop()
        ctx.push()
        dep2 = []
        eq, dep_e = emit_load_tile(ctx, eqc, [128, 256], "eq", BF16)
        dep2.append(dep_e)
        if cfg["bias"]:
            bq_sb, _ = emit_load_colvec(ctx, bq, KC, "bq")
            bo_sb, dep_b = emit_load_colvec(ctx, bo, KC, "bo")
            dep2.append(dep_b)
            cos2 = sinm2 = None
        else:
            bq_sb = bo_sb = None
            cos2, dep_c2 = emit_load_tile(ctx, cosd, [128, 1024], "cos", F32)
            sinm2, dep_s2 = emit_load_tile(ctx, sind, [128, 1024], "sinm", F32)
            dep2 += [dep_c2, dep_s2]
        cv3 = emit_convert(ctx, wq, qscr.ap(), KC, KC, F32)
        cv4 = emit_convert(ctx, wo, oscr.ap(), KC, KC, F32)
        fin_core = emit_attn_core(ctx, variant, resid, xhat, qscr.ap(),
                                  oscr.ap(), bq_sb, bo_sb, cos2, sinm2,
                                  kag_d.ap(), vag_d.ap(), masks, eq,
                                  dep=dep2 + cv3 + cv4)
        _barrier(ctx, fin_core)
        ctx.pop()
        ctx.pop()
        _ln_free()

    gscr = ctx.dram("gscr", [128, 64 * KC * 128], BF16)
    uscr = ctx.dram("uscr", [128, 64 * KC * 128], BF16)
    dscr = ctx.dram("dscr", [128, KC * 64 * 128], BF16)

    def ffnq_phase(first):
        ctx.push()
        _ln_alloc()
        ga_ln = emit_ln(ctx, resid, xhat, "plain", dep=[])
        dep = [(ctx.S["ga"], ga_ln)]
        gs_sb, _ = emit_load_colvec(ctx, gsc, 64, "gs")
        us_sb, _ = emit_load_colvec(ctx, usc, 64, "us")
        ds_sb, dep_s = emit_load_colvec(ctx, dsc, KC, "ds")
        dep.append(dep_s)
        cvs = []
        if first:
            cvs += emit_convert(ctx, gw, gscr.ap(), KC, 64, U8)
            cvs += emit_convert(ctx, uw, uscr.ap(), KC, 64, U8)
            cvs += emit_convert(ctx, dwq, dscr.ap(), 64, KC, U8)
        q = emit_ffnq(ctx, resid, xhat, gscr.ap(), uscr.ap(), dscr.ap(),
                      gs_sb, us_sb, ds_sb, dep=dep + cvs)
        _barrier(ctx, [q["done"]])
        ctx.pop()
        _ln_free()

    w0scr = ctx.dram("w0scr", [128, 64 * KC * 128], BF16)
    w1scr = ctx.dram("w1scr", [128, KC * 64 * 128], BF16)

    def ffn16_phase(first):
        ctx.push()
        _ln_alloc()
        ga_ln = emit_ln(ctx, resid, xhat, "plain", dep=[])
        dep = [(ctx.S["ga"], ga_ln)]
        b0sb, _ = emit_load_colvec(ctx, b0, 64, "b0")
        b1sb, dep_b = emit_load_colvec(ctx, b1, KC, "b1")
        dep.append(dep_b)
        cvs = []
        if first:
            cvs += emit_convert(ctx, w0, w0scr.ap(), KC, 64, F32)
            cvs += emit_convert(ctx, w1, w1scr.ap(), 64, KC, F32)
        ffn = emit_ffn16(ctx, resid, xhat, w0scr.ap(), w1scr.ap(), b0sb, b1sb,
                         dep=dep + cvs)
        _barrier(ctx, [ffn["done"]])
        ctx.pop()
        _ln_free()
        return ffn

    def mamba_phase(variant):
        cfgm = mamba_cfg(variant)
        iv = mam_in[variant]
        hpack_d = ctx.dram(f"hp_{variant}", [12, D], BF16)
        hgat_d = ctx.dram(f"hg_{variant}", [48, D], BF16)
        spack_d = ctx.dram(f"spk_{variant}", [4 * 129, 4096], BF16)
        sag_d = ctx.dram(f"sag_{variant}", [4 * 516, 4096], BF16)
        gate_d = ctx.dram(f"gate_{variant}", [4096, 1024], BF16)
        yp_d = ctx.dram(f"yp_{variant}", [1024, 4096], BF16)
        csp_d = ctx.dram(f"csp_{variant}", [512, 256], BF16)
        sd_d = ctx.dram(f"sd_{variant}", [1024, 64], BF16)
        wiscr = ctx.dram(f"wiscr_{variant}", [128, MMC * KC * 128], BF16)
        ctile = ctx.dram(f"ctile_{variant}", [64, 256], F32)
        woscr = ctx.dram(f"woscr_{variant}", [128, KC * NIC * 128], BF16)
        ctx.push()
        _ln_alloc()
        ctx.push()
        ga_ln = emit_ln(ctx, resid, xhat, "mamba", dep=[])
        fin1 = emit_mamba_m1(ctx, xhat, hpack_d.ap(),
                             dep=[(ctx.S["ga"], ga_ln)])
        ag1 = _emit_ag(ctx, hpack_d, hgat_d, fin1)
        _barrier(ctx, list(fin1) + [ag1, (ctx.S["ga"], ga_ln)])
        ctx.pop()
        ctx.push()
        dep = []
        if cfgm["in_bias"]:
            bin_sb, _ = emit_load_colvec(ctx, iv["bin"], MMC, "bin")
        else:
            bin_sb = None
        cwl = []
        for k in range(4):
            c_sb, _ = emit_load_colvec(ctx, iv["cw"][k], 34, f"cw{k}")
            cwl.append(c_sb)
        cb_sb = None
        if cfgm["conv_bias"]:
            cb_sb, _ = emit_load_colvec(ctx, iv["cb"], 34, "cb")
        acol_in, _ = emit_load_colvec(ctx, iv["alog"], 1, "alogc")
        dtb_sb, _ = emit_load_colvec(ctx, iv["dtb"], 1, "dtbc")
        drow, dep_d = emit_load_row(ctx, iv["dvec"], 64, "drow", F32)
        rsb, dep_rs = emit_load_tile(ctx, ralli, [64, 32 * 128], "rsb", BF16)
        rfr2 = rsb
        mtr = ctx._ln["coef"][:].bitcast(BF16)[:, 0:512]

        @ctx.block.sync
        def _(sync):
            wait(sync, ctx.S["ga"], ga_ln)
            wait(sync, ctx.S["ld"])
            dinc(sync.dma_start(out=mtr, in_=mtrii), ctx.S["ld"])
        dep_mt = (ctx.S["ld"], ctx.S["ld"].n)
        dep += [dep_d, dep_rs, dep_mt]
        acol = ctx.sbuf("acol", [128, 1], F32)
        dbc = ctx.sbuf("dbc", [128, 64], F32)
        pdb = ctx.psum("pdb", [128, 64])

        @ctx.block.scalar
        def _(scalar):
            wait(scalar, ctx.S["sb"])
            inc(scalar.activation(acol[:], acol_in[:, 0:1], AF.Exp,
                                  scale=np.float32(1.0)), ctx.S["sa"], 1)

        @ctx.block.vector
        def _(vector):
            wait(vector, ctx.S["sa"])
            inc(vector.tensor_scalar(acol[:], acol[:], -1.0, None, OP.mult),
                ctx.S["va"], 1)

        @ctx.block.tensor
        def _(tensor):
            for sem, v in [dep_d]:
                wait(tensor, sem, v)
            wait(tensor, ctx.S["va"], ctx.consts_va)
            wait(tensor, ctx.S["sb"])
            inc(tensor.matmul(pdb[:], ctx.ones_row[:], drow[:],
                              start=True, stop=True), ctx.S["mm"], 1)

        @ctx.block.scalar
        def _(scalar):
            wait(scalar, ctx.S["mm"])
            inc(scalar.copy(dbc[:], pdb[:]), ctx.S["sb"], 1)
        dep += [(ctx.S["va"], ctx.S["va"].n), (ctx.S["sb"], ctx.S["sb"].n)]

        cwp = ctx.sbuf("cwp", [128, 136], F32)

        @ctx.block.vector
        def _(vector):
            wait(vector, ctx.S["sb"])
            for k in range(4):
                ins = vector.tensor_copy(
                    cwp[:].rearrange("p (kc t) -> p kc t", t=4)[:, :, k],
                    cwl[k][:, 0:34])
            inc(ins, ctx.S["va"], 1)
        dep += [(ctx.S["va"], ctx.S["va"].n)]

        cv = emit_convert(ctx, iv["wi"], wiscr.ap(), KC, MMC, F32)
        dep += emit_halo_select(ctx, xhat, hgat_d.ap(), hseli,
                                dep=list(dep) + cv +
                                [(ctx.S["va"], ctx.S["va"].n),
                                 (ctx.S["ga"], ctx.S["ga"].n)])
        fin2 = emit_mamba_m2(ctx, variant, xhat, wiscr.ap(), bin_sb, cwp,
                             cb_sb, dtb_sb, acol, dbc, rsb, mtr, rfr2,
                             gate_d.ap(), yp_d.ap(), csp_d.ap(), sd_d.ap(),
                             spack_d.ap(), ctile.ap(), dep=dep + cv)
        ag2 = _emit_ag(ctx, spack_d, sag_d, fin2)
        _barrier(ctx, list(fin2) + [ag2])
        ctx.pop()
        ctx.pop()
        _ln_free()
        ctx.push()
        dep3 = []
        nw_sb, _ = emit_load_colvec(ctx, iv["nw"], NIC, "nw")
        if cfgm["out_bias"]:
            bo_sb, dep_b = emit_load_colvec(ctx, iv["bo"], KC, "bo")
            dep3.append(dep_b)
        else:
            bo_sb = None
        cpe, dep0 = emit_load_tile(ctx, cpi, [16, 16], "cpes", BF16)
        pmb, dep1 = emit_load_tile(ctx, pmbi, [64, 64], "pmbs", F32)
        zmk, dep2_ = emit_load_tile(ctx, zmki, [64, 64], "zmks", F32)
        rfr, dep3_ = emit_load_tile(ctx, ralli, [64, 4096], "rfrs", BF16)
        dep3 += [dep0, dep1, dep2_, dep3_]
        cv2 = emit_convert(ctx, iv["wo"], woscr.ap(), NIC, KC, F32)
        fin3 = emit_mamba_m3(ctx, variant, resid, sag_d.ap(), pmb, zmk, rfr,
                             cpe, gate_d.ap(), yp_d.ap(), csp_d.ap(),
                             sd_d.ap(), woscr.ap(), nw_sb, bo_sb,
                             dep=dep3 + cv2, xscr=xhat)
        _barrier(ctx, fin3)
        ctx.pop()
        return fin3

    attn_phase("a", wq_a, wk_a, wv_a, wo_a)
    ffnq_phase(True)
    attn_phase("b", wq_b, wk_b, wv_b, wo_b, bq=bq_b, bk=bk_b, bv=bv_b,
               bo=bo_b)
    ffnq_phase(False)
    mamba_phase("a")
    f6 = ffn16_phase(True)
    mamba_phase("b")
    f8 = ffn16_phase(False)
    emit_resid_out(ctx, resid, rout, dep=[f8["done"]])
    return L.finish()


def _run(nc, in_maps):
    res = run_bass_kernel_spmd(nc, in_maps, core_ids=list(range(NCORE)))
    return res.results

def kernel_unfused(**inp):
    ident = np.eye(128, dtype=np.float32)
    cos_t, sinm_t = _rope_tables()
    eqc = host_eq()
    x = np.asarray(inp["L0_attn_a_x"], np.float32)
    shards = shard_resid(x)
    percore = [dict(
        masks=host_masks(c), hsel=host_hsel(c), pmb=host_pmb(c),
        zmk=host_zmk(c),
        cos=np.ascontiguousarray(cos_t[core_tokens(c)].T),
        sinm=np.ascontiguousarray(sinm_t[core_tokens(c)].T),
    ) for c in range(NCORE)]
    rall, mtri, cperm = host_rall(), host_mtri(), host_cperm()

    def attn(variant, wq, wk, wv, wo, bq=None, bk=None, bv=None, bo=None):
        nonlocal shards
        nc1 = build_attn_kv_launch(variant)
        m1 = []
        for c in range(NCORE):
            d = dict(rin=shards[c], wk=wk, wv=wv, ident=ident)
            if variant == "b":
                d.update(bk=bk, bv=bv)
            else:
                d.update(cos=percore[c]["cos"], sinm=percore[c]["sinm"])
            m1.append(d)
        r1 = _run(nc1, m1)
        nc2 = build_attn_core_launch(variant)
        m2 = []
        for c in range(NCORE):
            g0 = (c // GROUP) * GROUP
            d = dict(rin=shards[c], wq=wq, wo=wo,
                     kag=np.concatenate([r1[i]["kpack"] for i in range(g0, g0 + 4)], 0),
                     vag=np.concatenate([r1[i]["vpack"] for i in range(g0, g0 + 4)], 0),
                     masks=percore[c]["masks"], eqc=eqc, ident=ident)
            if variant == "b":
                d.update(bq=bq, bo=bo)
            else:
                d.update(cos=percore[c]["cos"], sinm=percore[c]["sinm"])
            m2.append(d)
        r2 = _run(nc2, m2)
        shards = [r2[c]["rout"] for c in range(NCORE)]

    def ffnq(gw, uw, dw, gs, us, ds):
        nonlocal shards
        nc = build_ffnq_launch()
        m = [dict(rin=shards[c], gwt=gw, uwt=uw, dwt=dw, gsc=gs, usc=us,
                  dsc=ds, ident=ident) for c in range(NCORE)]
        r = _run(nc, m)
        shards = [r[c]["rout"] for c in range(NCORE)]

    def ffn16(w0t, w1t, b0, b1):
        nonlocal shards
        nc = build_ffn16_launch()
        m = [dict(rin=shards[c], w0t=w0t, w1t=w1t, b0=b0, b1=b1, ident=ident)
             for c in range(NCORE)]
        r = _run(nc, m)
        shards = [r[c]["rout"] for c in range(NCORE)]

    def mamba(variant, wi, cwt, dtb, alog, dvec, wo, nw, bin_=None, cb=None,
              bo=None):
        nonlocal shards
        nc1 = build_m1_launch()
        r1 = _run(nc1, [dict(rin=shards[c], ident=ident) for c in range(NCORE)])
        nc2 = build_m2_launch(variant)
        m2 = []
        for c in range(NCORE):
            g0 = (c // GROUP) * GROUP
            d = dict(rin=shards[c], wi=wi, cwt=cwt, alog=alog, dtb=dtb,
                     dvec=dvec,
                     hgat=np.concatenate([r1[i]["hpack"] for i in range(g0, g0 + 4)], 0),
                     hsel=percore[c]["hsel"], rall=rall, mtri=mtri, ident=ident)
            if variant == "a":
                d["bin"] = bin_
            else:
                d["cb"] = cb
            m2.append(d)
        r2 = _run(nc2, m2)
        nc3 = build_m3_launch(variant)
        m3 = []
        for c in range(NCORE):
            g0 = (c // GROUP) * GROUP
            d = dict(rin=shards[c], wo=wo, nw=nw,
                     sag=np.concatenate([r2[i]["spack"] for i in range(g0, g0 + 4)], 0),
                     gate_sp=r2[c]["gate_sp"], yp_sp=r2[c]["yp_sp"],
                     c_sp=r2[c]["c_sp"], sd_sp=r2[c]["sd_sp"],
                     pmb=percore[c]["pmb"], zmk=percore[c]["zmk"],
                     rfree=rall, cperm=cperm, ident=ident)
            if variant == "a":
                d["bo"] = bo
            m3.append(d)
        r3 = _run(nc3, m3)
        shards = [r3[c]["rout"] for c in range(NCORE)]

    f32 = lambda k: np.asarray(inp[k], np.float32)
    attn("a", _wlay(f32("L0_attn_a_Wq"), KC, KC),
         _wlay(f32("L0_attn_a_Wk"), KC, 4), _wlay(f32("L0_attn_a_Wv"), KC, 4),
         _wlay(f32("L0_attn_a_Wo"), KC, KC))
    gw = _wlay(_u8_view(np.asarray(inp["L0_ffn_gate_w"])).T, KC, 64)
    uw = _wlay(_u8_view(np.asarray(inp["L0_ffn_up_w"])).T, KC, 64)
    dw = _wlay(_u8_view(np.asarray(inp["L0_ffn_down_w"])).T, 64, KC)
    ffnq(gw, uw, dw, f32("L0_ffn_gate_scale"), f32("L0_ffn_up_scale"),
         f32("L0_ffn_down_scale"))
    wqkv = f32("L0_attn_b_w_qkv")
    bqkv = f32("L0_attn_b_b_qkv")
    attn("b", _wlay(wqkv[:, 0:2048], KC, KC),
         _wlay(wqkv[:, 2048:4096], KC, KC),
         _wlay(wqkv[:, 4096:6144], KC, KC),
         _wlay(f32("L0_attn_b_w_out"), KC, KC),
         bq=np.ascontiguousarray(bqkv[0:2048]),
         bk=np.ascontiguousarray(bqkv[2048:4096]),
         bv=np.ascontiguousarray(bqkv[4096:6144]), bo=f32("L0_attn_b_b_out"))
    ffnq(gw, uw, dw, f32("L0_ffn_gate_scale"), f32("L0_ffn_up_scale"),
         f32("L0_ffn_down_scale"))
    wia = np.zeros((D, MMC * 128), np.float32)
    wia[:, :8512] = f32("L1_attn_a_in_proj_w").T
    bia = np.zeros(MMC * 128, np.float32)
    bia[:8512] = f32("L1_attn_a_in_proj_b")
    pad128 = lambda v: np.concatenate([v, np.zeros(64, np.float32)])
    pad128hi = lambda v: np.concatenate([np.zeros(64, np.float32), v])
    mamba("a", _wlay(wia, KC, MMC),
          np.ascontiguousarray(f32("L1_attn_a_conv_w").T),
          pad128hi(f32("L1_attn_a_dt_bias")), pad128(f32("L1_attn_a_A_log")),
          f32("L1_attn_a_D"), _wlay(f32("L1_attn_a_out_proj_w").T, NIC, KC),
          f32("L1_attn_a_norm_w"), bin_=bia, bo=f32("L1_attn_a_out_proj_b"))
    w0t = _wlay(f32("L1_ffn_W0").T, KC, 64)
    w1t = _wlay(f32("L1_ffn_W1").T, 64, KC)
    ffn16(w0t, w1t, f32("L1_ffn_bias0"), f32("L1_ffn_bias1"))
    wib = np.zeros((D, MMC * 128), np.float32)
    wib[:, :8512] = f32("L1_attn_b_in_proj_w")
    mamba("b", _wlay(wib, KC, MMC),
          np.ascontiguousarray(f32("L1_attn_b_conv_w").T),
          pad128hi(f32("L1_attn_b_dt_bias")), pad128(f32("L1_attn_b_A_log")),
          f32("L1_attn_b_D"), _wlay(f32("L1_attn_b_out_proj_w"), NIC, KC),
          f32("L1_attn_b_norm_weight"), cb=f32("L1_attn_b_conv_b"))
    ffn16(w0t, w1t, f32("L1_ffn_bias0"), f32("L1_ffn_bias1"))
    return unshard_resid(shards)

def kernel(**inp):
    f32 = lambda k: np.asarray(inp[k], np.float32)
    ident = np.eye(128, dtype=np.float32)
    cos_t, sinm_t = _rope_tables()
    x = f32("L0_attn_a_x")
    shards = shard_resid(x)
    wqkv = f32("L0_attn_b_w_qkv")
    bqkv = f32("L0_attn_b_b_qkv")
    wia = np.zeros((D, MMC * 128), np.float32)
    wia[:, :8512] = f32("L1_attn_a_in_proj_w").T
    bia = np.zeros(MMC * 128, np.float32)
    bia[:8512] = f32("L1_attn_a_in_proj_b")
    wib = np.zeros((D, MMC * 128), np.float32)
    wib[:, :8512] = f32("L1_attn_b_in_proj_w")
    pad128 = lambda v: np.concatenate([v, np.zeros(64, np.float32)])
    pad128hi = lambda v: np.concatenate([np.zeros(64, np.float32), v])
    common = dict(
        ident=ident, eqc=host_eq(),
        wq_a=_wlay(f32("L0_attn_a_Wq"), KC, KC),
        wk_a=_wlay(f32("L0_attn_a_Wk"), KC, 4),
        wv_a=_wlay(f32("L0_attn_a_Wv"), KC, 4),
        wo_a=_wlay(f32("L0_attn_a_Wo"), KC, KC),
        gwt=_wlay(_u8_view(np.asarray(inp["L0_ffn_gate_w"])).T, KC, 64),
        uwt=_wlay(_u8_view(np.asarray(inp["L0_ffn_up_w"])).T, KC, 64),
        dwt=_wlay(_u8_view(np.asarray(inp["L0_ffn_down_w"])).T, 64, KC),
        gsc=f32("L0_ffn_gate_scale"), usc=f32("L0_ffn_up_scale"),
        dsc=f32("L0_ffn_down_scale"),
        wq_b=_wlay(wqkv[:, 0:2048], KC, KC),
        wk_b=_wlay(wqkv[:, 2048:4096], KC, KC),
        wv_b=_wlay(wqkv[:, 4096:6144], KC, KC),
        wo_b=_wlay(f32("L0_attn_b_w_out"), KC, KC),
        bq_b=np.ascontiguousarray(bqkv[0:2048]),
        bk_b=np.ascontiguousarray(bqkv[2048:4096]),
        bv_b=np.ascontiguousarray(bqkv[4096:6144]),
        bo_b=f32("L0_attn_b_b_out"),
        w0t=_wlay(f32("L1_ffn_W0").T, KC, 64),
        w1t=_wlay(f32("L1_ffn_W1").T, 64, KC),
        b0=f32("L1_ffn_bias0"), b1=f32("L1_ffn_bias1"),
        rall=host_rall(), mtri=host_mtri(), cperm=host_cperm(),
        wi_a=_wlay(wia, KC, MMC),
        cwt_a=np.ascontiguousarray(f32("L1_attn_a_conv_w").T),
        alog_a=pad128(f32("L1_attn_a_A_log")),
        dtb_a=pad128hi(f32("L1_attn_a_dt_bias")),
        dvec_a=f32("L1_attn_a_D"), nw_a=f32("L1_attn_a_norm_w"),
        wom_a=_wlay(f32("L1_attn_a_out_proj_w").T, NIC, KC),
        bin_a=bia, bom_a=f32("L1_attn_a_out_proj_b"),
        wi_b=_wlay(wib, KC, MMC),
        cwt_b=np.ascontiguousarray(f32("L1_attn_b_conv_w").T),
        alog_b=pad128(f32("L1_attn_b_A_log")),
        dtb_b=pad128hi(f32("L1_attn_b_dt_bias")),
        dvec_b=f32("L1_attn_b_D"), nw_b=f32("L1_attn_b_norm_weight"),
        wom_b=_wlay(f32("L1_attn_b_out_proj_w"), NIC, KC),
        cb_b=f32("L1_attn_b_conv_b"))
    nc = build_mega()
    in_maps = [dict(rin=shards[c], masks=host_masks(c), hsel=host_hsel(c),
                    pmb=host_pmb(c), zmk=host_zmk(c),
                    cos=np.ascontiguousarray(cos_t[core_tokens(c)].T),
                    sinm=np.ascontiguousarray(sinm_t[core_tokens(c)].T),
                    **common) for c in range(NCORE)]
    res = run_bass_kernel_spmd(nc, in_maps, core_ids=list(range(NCORE)))
    return unshard_resid([res.results[c]["rout"] for c in range(NCORE)])


# version 06 (final)
# speedup vs baseline: 1.0479x; 1.0479x over previous
import math
from contextlib import ExitStack

import numpy as np
import ml_dtypes

import concourse.bass as bass
import concourse.mybir as mybir
from concourse.alu_op_type import AluOpType
from concourse.bass_utils import run_bass_kernel_spmd

F32 = mybir.dt.float32
BF16 = mybir.dt.bfloat16
U8 = mybir.dt.uint8
AF = mybir.ActivationFunctionType
OP = AluOpType
BF = ml_dtypes.bfloat16

B, S, D = 2, 4096, 2048
KC = D // 128
BT = 256
NBLK = S // BT
TOK = 1024
HALO = 3
BW = HALO + BT
XW = 4 * BW
NCORE, GROUP = 8, 4
EPS = 1e-5

def blocks_of(g):
    return sorted([g, 7 - g, 8 + g, 15 - g])

BLK_OWNER = {}
for _g in range(GROUP):
    for _s, _b in enumerate(blocks_of(_g)):
        BLK_OWNER[_b] = (_g, _s)


def _u8_view(w_int32):
    return np.ascontiguousarray(
        w_int32.view(np.uint8).reshape(w_int32.shape + (4,))[..., 0])

def _wlay(w, kcn, mcn):
    return np.ascontiguousarray(
        np.asarray(w).reshape(kcn, 128, mcn, 128)
        .transpose(1, 2, 0, 3).reshape(128, -1))

def _rope_tables():
    dh = 128
    inv = 1.0 / (10000.0 ** (np.arange(0, dh, 2, dtype=np.float64) / dh))
    fr = np.outer(np.arange(S, dtype=np.float64), inv)
    cos = np.concatenate([np.cos(fr), np.cos(fr)], axis=1)
    sin = np.concatenate([np.sin(fr), np.sin(fr)], axis=1)
    sinm = sin.copy()
    sinm[:, :64] *= -1.0
    return cos.astype(np.float32), sinm.astype(np.float32)

def core_tokens(core):
    g = core % GROUP
    return np.concatenate([np.arange(b * BT, (b + 1) * BT) for b in blocks_of(g)])

def shard_resid(x):
    return [np.ascontiguousarray(x[c // GROUP][core_tokens(c)].T) for c in range(NCORE)]

def unshard_resid(shards):
    x = np.empty((B, S, D), np.float32)
    for c in range(NCORE):
        x[c // GROUP][core_tokens(c)] = np.asarray(shards[c], np.float32).T
    return x


class Sem:
    __slots__ = ("h", "n", "w")
    def __init__(self, h):
        self.h, self.n, self.w = h, 0, {}

class Ctx:
    SEMS = ("ld", "ld2", "mm", "ep", "va", "vb", "sa", "sb", "ga", "cc",
            "lda", "ldb", "ldc", "ldd", "lde", "ldf", "ste", "stf",
            "bt", "bv", "bs", "bg", "by")

    def __init__(self, nc, block, es):
        self.nc, self.block, self.es = nc, block, es
        self.S = {n: Sem(es.enter_context(nc.semaphore(n))) for n in self.SEMS}
        self._scopes = []
        self._i = 0

    def push(self):
        st = ExitStack()
        self._scopes.append(st)

    def pop(self):
        self._scopes.pop().close()

    def _stack(self):
        return self._scopes[-1] if self._scopes else self.es

    def sbuf(self, name, shape, dt):
        self._i += 1
        return self._stack().enter_context(
            self.nc.sbuf_tensor(f"{name}_{self._i}", shape, dt))

    def psum(self, name, shape, dt=F32):
        self._i += 1
        return self._stack().enter_context(
            self.nc.psum_tensor(f"{name}_{self._i}", shape, dt))

    def dram(self, name, shape, dt):
        self._i += 1
        return self.nc.dram_tensor(f"{name}_{self._i}", shape, dt)

def inc(instr, sem, k=1):
    instr.then_inc(sem.h, k)
    sem.n += k
    return instr

def dinc(instr, sem):
    instr.then_inc(sem.h, 16)
    sem.n += 16
    return instr

def wait(eng, sem, n=None):
    v = sem.n if n is None else n
    k = eng.engine
    if sem.w.get(k, -1) >= v:
        return
    eng.wait_ge(sem.h, v)
    sem.w[k] = v

def xc_plain(kc, t):
    return kc * XW + t

def xc_mamba(kc, slot, t):
    return kc * XW + slot * BW + HALO + t

def emit_consts(ctx, ident_dram):
    ctx.ones_col = ctx.sbuf("ones_col", [128, 1], F32)
    ctx.ones_row = ctx.sbuf("ones_row", [1, 128], F32)
    ctx.ones_row_bf = ctx.sbuf("ones_row_bf", [1, 128], BF16)
    ctx.ident = ctx.sbuf("ident", [128, 128], F32)
    ctx.ident_bf = ctx.sbuf("ident_bf", [128, 128], BF16)
    ctx.ones_col_bf = ctx.sbuf("ones_col_bf", [128, 1], BF16)
    ctx.neg128_row = ctx.sbuf("neg128_row", [1, 128], BF16)
    ld = ctx.S["ld"]
    ldb = ld.n

    @ctx.block.sync
    def _(sync):
        wait(sync, ld)
        dinc(sync.dma_start(out=ctx.ident[:], in_=ident_dram), ld)

    @ctx.block.vector
    def _(vector):
        vector.memset(ctx.ones_col[:], 1.0)
        vector.memset(ctx.ones_row[:], 1.0)
        vector.memset(ctx.ones_col_bf[:], 1.0)
        vector.memset(ctx.neg128_row[:], -128.0)
        wait(vector, ld, ldb + 16)
        vector.tensor_copy(ctx.ident_bf[:], ctx.ident[:])
        inc(vector.memset(ctx.ones_row_bf[:], 1.0), ctx.S["va"], 1)
    ctx.consts_va = ctx.S["va"].n


def emit_convert(ctx, src, dst, kcn, mcn, src_dt):
    CW = 1024
    N = kcn * mcn * 128
    assert N % CW == 0
    n = N // CW
    lda, ldbp = ctx.S["lda"], ctx.S["ldb"]
    va, ga = ctx.S["va"], ctx.S["ga"]
    ste, stf = ctx.S["ste"], ctx.S["stf"]
    lab, lbb, vab, gab, seb, sfb = lda.n, ldbp.n, va.n, ga.n, ste.n, stf.n
    ctx.push()
    bufs = [ctx.sbuf("cvt_in", [128, CW], src_dt) for _ in range(2)]
    obufs = [ctx.sbuf("cvt_out", [128, CW], BF16) for _ in range(2)]

    def conv_done(u):
        return (va, vab + u // 2 + 1) if u % 2 == 0 else (ga, gab + (u + 1) // 2)

    @ctx.block.sync
    def _(sync):
        for u in range(n):
            if u < 2:
                wait(sync, va, vab)
                wait(sync, ga, gab)
            else:
                s_, c_ = conv_done(u - 2)
                wait(sync, s_, c_)
            dinc(sync.dma_start(out=bufs[u % 2][:],
                                in_=src[:, u * CW:(u + 1) * CW]),
                 lda if u % 2 == 0 else ldbp)

    @ctx.block.scalar
    def _(scalar):
        for u in range(n):
            s_, c_ = conv_done(u)
            wait(scalar, s_, c_)
            dinc(scalar.dma_start(out=dst[:, u * CW:(u + 1) * CW],
                                  in_=obufs[u % 2][:]),
                 ste if u % 2 == 0 else stf)

    @ctx.block.vector
    def _(vector):
        for u in range(0, n, 2):
            wait(vector, lda, lab + 16 * (u // 2 + 1))
            wait(vector, ste, seb + 16 * (u // 2) if u >= 2 else seb)
            inc(vector.tensor_copy(obufs[0][:], bufs[0][:]), va, 1)

    @ctx.block.gpsimd
    def _(gpsimd):
        for u in range(1, n, 2):
            wait(gpsimd, ldbp, lbb + 16 * (u // 2 + 1))
            wait(gpsimd, stf, sfb + 16 * (u // 2) if u >= 2 else sfb)
            inc(gpsimd.tensor_copy(obufs[1][:], bufs[1][:]), ga, 1)

    ctx.pop()
    ne = (n + 1) // 2
    nf = n // 2
    return [(ste, seb + 16 * ne), (stf, sfb + 16 * nf)]


def emit_ln(ctx, resid, xhat, mode, dep=()):
    blk = ctx.block
    mm, va, sa, sb, ga = (ctx.S[k] for k in ("mm", "va", "sa", "sb", "ga"))
    mmb, vab, sab, sbb, gab = mm.n, va.n, sa.n, sb.n, ga.n
    if not hasattr(ctx, "_ln"):
        ctx._ln = dict(
            sq=[ctx.sbuf("ln_sq", [128, 256], F32) for _ in range(2)],
            stat=ctx.sbuf("ln_stat", [1, 1024], F32),
            coef=ctx.sbuf("ln_coef", [128, 512], F32),
            tmp=ctx.sbuf("ln_tmp", [128, 2048], F32))
    L = ctx._ln
    sq, stat, coef, tmp = L["sq"], L["stat"], L["coef"], L["tmp"]
    ctx.push()
    ms = [ctx.psum("ln_ms", [1, 256])]
    ss = [ctx.psum("ln_ss", [1, 256])]
    pab = [ctx.psum("ln_ab", [128, 512])]

    def rc(kc, s):
        return kc * 1024 + s * 256

    def oc(kc, s):
        return xc_plain(kc, s * 256) if mode == "plain" else xc_mamba(kc, s, 0)

    @blk.scalar
    def _(scalar):
        for sem, v in dep:
            wait(scalar, sem, v)
        for s in range(4):
            for kc in range(KC):
                u = s * KC + kc
                if u >= 2:
                    s2, k2 = divmod(u - 2, KC)
                    wait(scalar, mm, mmb + s2 * 18 + k2 + 1)
                inc(scalar.activation(sq[u % 2][:],
                                      resid[:, rc(kc, s):rc(kc, s) + 256],
                                      AF.Square), sa, 1)
            wait(scalar, va, vab + 8 * s + 4)
            inc(scalar.activation(stat[:, 768:1024],
                                  stat[:, 512:768],
                                  AF.Sqrt, bias=np.float32(EPS)), sb, 1)
            wait(scalar, mm, mmb + s * 18 + 18)
            if s >= 1:
                wait(scalar, ga, gab + 2 * s)
            inc(scalar.copy(coef[:, 0:512],
                            pab[0][:]), sb, 1)

    @blk.tensor
    def _(tensor):
        for sem, v in dep:
            wait(tensor, sem, v)
        wait(tensor, ctx.S["va"], ctx.consts_va)
        wait(tensor, sb, sbb)
        for s in range(4):
            if s >= 1:
                wait(tensor, va, vab + 8 * (s - 1) + 2)
            for kc in range(KC):
                u = s * KC + kc
                tensor.matmul(ms[0][:], ctx.ones_col[:],
                              resid[:, rc(kc, s):rc(kc, s) + 256],
                              start=(kc == 0), stop=(kc == KC - 1))
                wait(tensor, sa, sab + u + 1)
                inc(tensor.matmul(ss[0][:], ctx.ones_col[:], sq[u % 2][:],
                                  start=(kc == 0), stop=(kc == KC - 1)), mm, 1)
            wait(tensor, va, vab + 8 * s + 6)
            if s >= 1:
                wait(tensor, sb, sbb + 2 * (s - 1) + 2)
            inc(tensor.matmul(pab[0][:, 0:256], ctx.ones_row[:],
                              stat[:, 768:1024],
                              start=True, stop=True), mm, 1)
            inc(tensor.matmul(pab[0][:, 256:512], ctx.ones_row[:],
                              stat[:, 0:256],
                              start=True, stop=True), mm, 1)

    @blk.vector
    def _(vector):
        for sem, v in dep:
            wait(vector, sem, v)
        for s in range(4):
            o = 0
            wait(vector, mm, mmb + s * 18 + 16)
            if s >= 1:
                wait(vector, mm, mmb + (s - 1) * 18 + 18)
            inc(vector.tensor_scalar(stat[:, o:o + 256], ms[0][:],
                                     1.0 / D, None, OP.mult), va, 1)
            inc(vector.tensor_scalar(stat[:, o + 256:o + 512], ss[0][:],
                                     1.0 / D, None, OP.mult), va, 1)
            wait(vector, va, vab + 8 * s + 2)
            inc(vector.scalar_tensor_tensor(stat[:, o + 512:o + 768],
                                            stat[:, o:o + 256], -1.0,
                                            stat[:, o:o + 256],
                                            OP.mult, OP.mult), va, 1)
            wait(vector, va, vab + 8 * s + 3)
            inc(vector.tensor_tensor(stat[:, o + 512:o + 768],
                                     stat[:, o + 256:o + 512],
                                     stat[:, o + 512:o + 768], OP.add), va, 1)
            wait(vector, sb, sbb + 2 * s + 1)
            inc(vector.reciprocal(stat[:, o + 768:o + 1024],
                                  stat[:, o + 768:o + 1024]), va, 1)
            wait(vector, va, vab + 8 * s + 5)
            inc(vector.scalar_tensor_tensor(stat[:, o:o + 256],
                                            stat[:, o:o + 256], -1.0,
                                            stat[:, o + 768:o + 1024],
                                            OP.mult, OP.mult), va, 1)
            wait(vector, sb, sbb + 2 * s + 2)
            a = coef[:, 0:256]
            for h in range(2):
                wait(vector, ga, gab + 2 * s + h)
                for kc in range(h * 8, h * 8 + 8):
                    ins = vector.tensor_tensor(
                        tmp[:, (kc - h * 8) * 256:(kc - h * 8) * 256 + 256],
                        resid[:, rc(kc, s):rc(kc, s) + 256], a, OP.mult)
                inc(ins, va, 1)

    @blk.gpsimd
    def _(gpsimd):
        for sem, v in dep:
            wait(gpsimd, sem, v)
        for s in range(4):
            b = coef[:, 256:512]
            for h in range(2):
                wait(gpsimd, va, vab + 8 * s + 7 + h)
                for kc in range(h * 8, h * 8 + 8):
                    ins = gpsimd.tensor_tensor(
                        xhat[:, oc(kc, s):oc(kc, s) + 256],
                        tmp[:, (kc - h * 8) * 256:(kc - h * 8) * 256 + 256],
                        b, OP.add)
                inc(ins, ga, 1)

    ctx.pop()
    return ga.n


def emit_ffn16(ctx, resid, xhat, w0t, w1t, b0_sb, b1_sb, dep=()):
    blk = ctx.block
    mm, ep, vb = ctx.S["mm"], ctx.S["ep"], ctx.S["vb"]
    l1 = [ctx.S["lda"], ctx.S["ldb"]]
    l2 = [ctx.S["ldc"], ctx.S["ldd"]]
    l1b = [x.n for x in l1]
    l2b = [x.n for x in l2]
    mmb, epb, vbb = mm.n, ep.n, vb.n
    sl1 = [ctx.sbuf("f16_sl1", [128, KC * 128], BF16) for _ in range(2)]
    sl2 = [ctx.sbuf("f16_sl2", [128, 64 * 128], BF16) for _ in range(2)]
    pA = [ctx.psum("f16_pA", [128, 256]) for _ in range(2)]
    pB = [ctx.psum("f16_pB", [128, 256]) for _ in range(2)]
    hbuf = ctx.sbuf("f16_h", [128, 64 * 256], BF16)
    w0_4 = w0t.rearrange("k (m kc c) -> m k kc c", kc=KC, c=128)
    w1_4 = w1t.rearrange("k (m kc c) -> m k kc c", kc=64, c=128)
    NT = 4

    def mma(ti, m):
        return mmb + ti * 80 + m + 1

    def mmbh(ti, mo):
        return mmb + ti * 80 + 64 + mo + 1

    def epa(ti, m):
        return epb + ti * 64 + m + 1

    def epbh(ti, mo):
        return vbb + ti * 16 + mo + 1

    @blk.sync
    def _(sync):
        for sem, v in dep:
            wait(sync, sem, v)
        nl = 0
        for ti in range(NT):
            for m in range(64):
                ua = ti * 64 + m
                if ua < 2:
                    wait(sync, l1[ua % 2])
                else:
                    t2, m2 = divmod(ua - 2, 64)
                    wait(sync, mm, mma(t2, m2))
                dinc(sync.dma_start(
                    out=sl1[ua % 2][:].rearrange("k (kc c) -> k kc c", c=128),
                    in_=w0_4[m]), l1[ua % 2])
            for mo in range(KC):
                ub = ti * KC + mo
                if ub < 2:
                    wait(sync, l2[ub % 2])
                else:
                    t2, m2 = divmod(ub - 2, KC)
                    wait(sync, mm, mmbh(t2, m2))
                dinc(sync.dma_start(
                    out=sl2[ub % 2][:].rearrange("k (kc c) -> k kc c", c=128),
                    in_=w1_4[mo]), l2[ub % 2])

    @blk.tensor
    def _(tensor):
        for sem, v in dep:
            wait(tensor, sem, v)
        for ti in range(NT):
            t0 = ti * 256
            for m in range(64):
                ua = ti * 64 + m
                wait(tensor, l1[ua % 2], l1b[ua % 2] + 16 * (ua // 2 + 1))
                if ua >= 2:
                    t2, m2 = divmod(ua - 2, 64)
                    wait(tensor, ep, epa(t2, m2))
                sl = sl1[ua % 2]
                ps = pA[ua % 2]
                for kc in range(KC):
                    last = tensor.matmul(ps[:], sl[:, kc * 128:(kc + 1) * 128],
                                         xhat[:, xc_plain(kc, t0):xc_plain(kc, t0) + 256],
                                         start=(kc == 0), stop=(kc == KC - 1))
                inc(last, mm, 1)
            wait(tensor, ep, epa(ti, 63))
            for mo in range(KC):
                ub = ti * KC + mo
                wait(tensor, l2[ub % 2], l2b[ub % 2] + 16 * (ub // 2 + 1))
                if ub >= 2:
                    t2, m2 = divmod(ub - 2, KC)
                    wait(tensor, vb, epbh(t2, m2))
                sl = sl2[ub % 2]
                ps = pB[ub % 2]
                for hc in range(64):
                    last = tensor.matmul(ps[:], sl[:, hc * 128:(hc + 1) * 128],
                                         hbuf[:, hc * 256:(hc + 1) * 256],
                                         start=(hc == 0), stop=(hc == 63))
                inc(last, mm, 1)

    @blk.scalar
    def _(scalar):
        for ti in range(NT):
            for m in range(64):
                wait(scalar, mm, mma(ti, m))
                if ti >= 1 and m == 0:
                    wait(scalar, mm, mmbh(ti - 1, KC - 1))
                inc(scalar.activation(hbuf[:, m * 256:(m + 1) * 256],
                                      pA[(ti * 64 + m) % 2][:], AF.Gelu,
                                      bias=b0_sb[:, m:m + 1]), ep, 1)

    @blk.vector
    def _(vector):
        for ti in range(NT):
            t0 = ti * 256
            for mo in range(KC):
                wait(vector, mm, mmbh(ti, mo))
                r = resid[:, mo * 1024 + t0:mo * 1024 + t0 + 256]
                inc(vector.scalar_tensor_tensor(
                    r, pB[(ti * KC + mo) % 2][:], b1_sb[:, mo:mo + 1], r,
                    OP.add, OP.add), vb, 1)

    return dict(mm_end=mmb + NT * 80, done=(ctx.S["vb"], vbb + NT * KC))


def emit_load_colvec(ctx, src_dram, nchunk, name="bias"):
    ld, mm, sb = ctx.S["ld"], ctx.S["mm"], ctx.S["sb"]
    out = ctx.sbuf(name, [128, nchunk], F32)
    stg = ctx.sbuf(name + "_stg", [nchunk, 128], F32)
    ctx.push()
    pt = ctx.psum(name + "_pt", [128, nchunk])
    ldb = ld.n
    sbb0 = ctx.S["sb"].n

    @ctx.block.sync
    def _(sync):
        wait(sync, ld)
        dinc(sync.dma_start(out=stg[:], in_=src_dram.rearrange("(m k) -> m k", k=128)),
             ld)

    @ctx.block.tensor
    def _(tensor):
        wait(tensor, ld, ldb + 16)
        wait(tensor, ctx.S["va"], ctx.consts_va)
        wait(tensor, ctx.S["sb"], sbb0)
        inc(tensor.transpose(pt[:], stg[:], ctx.ident[:nchunk, :nchunk]), mm, 1)

    mmv = ctx.S["mm"].n

    @ctx.block.scalar
    def _(scalar):
        wait(scalar, mm, mmv)
        inc(scalar.copy(out[:], pt[:]), sb, 1)

    ctx.pop()
    return out, (ctx.S["sb"], ctx.S["sb"].n)


class Launch:
    def __init__(self):
        self.nc = bass.Bass("TRN2", target_bir_lowering=False)
        self.es = ExitStack()
        self.block = self.es.enter_context(self.nc.Block())
        self.ctx = Ctx(self.nc, self.block, self.es)

    def input(self, name, shape, dt):
        return self.nc.dram_tensor(name, shape, dt, kind="ExternalInput").ap()

    def output(self, name, shape, dt):
        return self.nc.dram_tensor(name, shape, dt, kind="ExternalOutput").ap()

    def finish(self):
        self.es.close()
        return self.nc

def emit_resid_in(ctx, resid_dram):
    resid = ctx.sbuf("resid", [128, KC * 1024], F32)
    ld = ctx.S["ld"]
    dinc(ctx.block.sync_section_dma(resid, resid_dram), ld) if False else None

    @ctx.block.sync
    def _(sync):
        wait(sync, ctx.S["ld"])
        dinc(sync.dma_start(out=resid[:].rearrange("k (kc t) -> k kc t", t=1024),
                            in_=resid_dram.rearrange("(kc k) t -> k kc t", k=128)),
             ctx.S["ld"])
    return resid, (ctx.S["ld"], ctx.S["ld"].n)

def emit_resid_out(ctx, resid, resid_out_dram, dep):
    so = ctx.S["by"]

    @ctx.block.sync
    def _(sync):
        for sem, v in dep:
            wait(sync, sem, v)
        dinc(sync.dma_start(out=resid_out_dram.rearrange("(kc k) t -> k kc t", k=128),
                            in_=resid[:].rearrange("k (kc t) -> k kc t", t=1024)),
             so)
        wait(sync, so)


def build_ffn16_launch(l0=False):
    L = Launch()
    ctx = L.ctx
    rin = L.input("rin", [D, 1024], F32)
    w0 = L.input("w0t", [128, 64 * KC * 128], F32)
    w1 = L.input("w1t", [128, KC * 64 * 128], F32)
    b0 = L.input("b0", [8192], F32)
    b1 = L.input("b1", [D], F32)
    rout = L.output("rout", [D, 1024], F32)
    w0scr = ctx.dram("w0scr", [128, 64 * KC * 128], BF16)
    w1scr = ctx.dram("w1scr", [128, KC * 64 * 128], BF16)

    ident = L.input("ident", [128, 128], F32)
    emit_consts(ctx, ident)
    resid, dep_r = emit_resid_in(ctx, rin)
    xhat = ctx.sbuf("xhat", [128, KC * XW], BF16)
    ga_ln = emit_ln(ctx, resid, xhat, "plain", dep=[dep_r])
    b0sb, _ = emit_load_colvec(ctx, b0, 64, "b0")
    b1sb, dep_b = emit_load_colvec(ctx, b1, KC, "b1")
    cv1 = emit_convert(ctx, w0, w0scr.ap(), KC, 64, F32)
    cv2 = emit_convert(ctx, w1, w1scr.ap(), 64, KC, F32)
    ffn = emit_ffn16(ctx, resid, xhat, w0scr.ap(), w1scr.ap(), b0sb, b1sb,
                     dep=[(ctx.S["ga"], ga_ln)] + cv1 + cv2 + [dep_b])
    emit_resid_out(ctx, resid, rout, dep=[ffn["done"]])
    return L.finish()


def emit_ffnq(ctx, resid, xhat, gscr, uscr, dscr, gs_sb, us_sb, ds_sb, dep=()):
    blk = ctx.block
    mm, ep, vb = ctx.S["mm"], ctx.S["ep"], ctx.S["vb"]
    lg = [ctx.S["lda"], ctx.S["ldb"]]
    lu = [ctx.S["ldc"], ctx.S["ldd"]]
    ldn = [ctx.S["lde"], ctx.S["ldf"]]
    lgb = [x.n for x in lg]
    lub = [x.n for x in lu]
    ldb_ = [x.n for x in ldn]
    mmb, epb, vbb = mm.n, ep.n, vb.n
    slg = [ctx.sbuf("fq_slg", [128, KC * 128], BF16) for _ in range(2)]
    slu = [ctx.sbuf("fq_slu", [128, KC * 128], BF16) for _ in range(2)]
    sld = [ctx.sbuf("fq_sld", [128, 64 * 128], BF16) for _ in range(2)]
    gss = [ctx.sbuf("fq_gs", [128, 256], BF16) for _ in range(2)]
    uss = [ctx.sbuf("fq_us", [128, 256], BF16) for _ in range(2)]
    dss = [ctx.sbuf("fq_ds", [128, 256], F32) for _ in range(2)]
    hbuf = ctx.sbuf("fq_h", [128, 64 * 256], BF16)
    rsrow = ctx.sbuf("fq_rs", [1, 512], BF16)
    pG = [ctx.psum("fq_pG", [128, 256]) for _ in range(2)]
    pU = [ctx.psum("fq_pU", [128, 256]) for _ in range(2)]
    pD = [ctx.psum("fq_pD", [128, 256]) for _ in range(2)]
    rs = ctx.psum("fq_rsp", [1, 512])
    g4 = gscr.rearrange("k (m kc c) -> m k kc c", kc=KC, c=128)
    u4 = uscr.rearrange("k (m kc c) -> m k kc c", kc=KC, c=128)
    d4 = dscr.rearrange("k (m kc c) -> m k kc c", kc=64, c=128)
    NT = 4

    def mm_rsx(ti):
        return mmb + ti * 82 + 1

    def mm_m(ti, m):
        return mmb + ti * 82 + 1 + m + 1

    def mm_rsh(ti):
        return mmb + ti * 82 + 66

    def mm_mo(ti, mo):
        return mmb + ti * 82 + 66 + mo + 1

    def ep_rsx(ti):
        return epb + ti * 146 + 1

    def ep_m(ti, m):
        return epb + ti * 146 + 1 + 2 * (m + 1)

    def ep_rsh(ti):
        return epb + ti * 146 + 130

    def ep_mo(ti, mo):
        return epb + ti * 146 + 130 + mo + 1

    def vb_m(ti, m):
        return vbb + ti * 80 + m + 1

    def vb_mo(ti, mo):
        return vbb + ti * 80 + 64 + mo + 1

    @blk.sync
    def _(sync):
        for sem, v in dep:
            wait(sync, sem, v)
        for ti in range(NT):
            for m in range(64):
                u = ti * 64 + m
                if u < 2:
                    wait(sync, lg[u % 2])
                    wait(sync, lu[u % 2])
                else:
                    t2, m2 = divmod(u - 2, 64)
                    wait(sync, mm, mm_m(t2, m2))
                dinc(sync.dma_start(
                    out=slg[u % 2][:].rearrange("k (kc c) -> k kc c", c=128),
                    in_=g4[m]), lg[u % 2])
                dinc(sync.dma_start(
                    out=slu[u % 2][:].rearrange("k (kc c) -> k kc c", c=128),
                    in_=u4[m]), lu[u % 2])
            for mo in range(KC):
                u = ti * KC + mo
                if u < 2:
                    wait(sync, ldn[u % 2])
                else:
                    t2, m2 = divmod(u - 2, KC)
                    wait(sync, mm, mm_mo(t2, m2))
                dinc(sync.dma_start(
                    out=sld[u % 2][:].rearrange("k (kc c) -> k kc c", c=128),
                    in_=d4[mo]), ldn[u % 2])

    @blk.tensor
    def _(tensor):
        for sem, v in dep:
            wait(tensor, sem, v)
        for ti in range(NT):
            t0 = ti * 256
            if ti >= 1:
                wait(tensor, ep, ep_rsh(ti - 1))
            for kc in range(KC):
                last = tensor.matmul(rs[:, 0:256], ctx.ones_col_bf[:],
                                     xhat[:, xc_plain(kc, t0):xc_plain(kc, t0) + 256],
                                     start=(kc == 0), stop=(kc == KC - 1))
            inc(last, mm, 1)
            wait(tensor, ep, ep_rsx(ti))
            for m in range(64):
                u = ti * 64 + m
                wait(tensor, lg[u % 2], lgb[u % 2] + 16 * (u // 2 + 1))
                wait(tensor, lu[u % 2], lub[u % 2] + 16 * (u // 2 + 1))
                if u >= 2:
                    t2, m2 = divmod(u - 2, 64)
                    wait(tensor, ep, ep_m(t2, m2))
                for kc in range(KC):
                    tensor.matmul(pG[u % 2][:], slg[u % 2][:, kc * 128:(kc + 1) * 128],
                                  xhat[:, xc_plain(kc, t0):xc_plain(kc, t0) + 256],
                                  start=(kc == 0), stop=False)
                tensor.matmul(pG[u % 2][:], ctx.neg128_row[:], rsrow[:, 0:256],
                              start=False, stop=True)
                for kc in range(KC):
                    tensor.matmul(pU[u % 2][:], slu[u % 2][:, kc * 128:(kc + 1) * 128],
                                  xhat[:, xc_plain(kc, t0):xc_plain(kc, t0) + 256],
                                  start=(kc == 0), stop=False)
                inc(tensor.matmul(pU[u % 2][:], ctx.neg128_row[:], rsrow[:, 0:256],
                                  start=False, stop=True), mm, 1)
            wait(tensor, vb, vb_m(ti, 63))
            for hc in range(64):
                last = tensor.matmul(rs[:, 256:512], ctx.ones_col_bf[:],
                                     hbuf[:, hc * 256:(hc + 1) * 256],
                                     start=(hc == 0), stop=(hc == 63))
            inc(last, mm, 1)
            wait(tensor, ep, ep_rsh(ti))
            for mo in range(KC):
                u = ti * KC + mo
                wait(tensor, ldn[u % 2], ldb_[u % 2] + 16 * (u // 2 + 1))
                if u >= 2:
                    t2, m2 = divmod(u - 2, KC)
                    wait(tensor, vb, vb_mo(t2, m2))
                for hc in range(64):
                    tensor.matmul(pD[u % 2][:], sld[u % 2][:, hc * 128:(hc + 1) * 128],
                                  hbuf[:, hc * 256:(hc + 1) * 256],
                                  start=(hc == 0), stop=False)
                inc(tensor.matmul(pD[u % 2][:], ctx.neg128_row[:], rsrow[:, 256:512],
                                  start=False, stop=True), mm, 1)

    @blk.scalar
    def _(scalar):
        for ti in range(NT):
            wait(scalar, mm, mm_rsx(ti))
            inc(scalar.copy(rsrow[:, 0:256], rs[:, 0:256]), ep, 1)
            for m in range(64):
                u = ti * 64 + m
                wait(scalar, mm, mm_m(ti, m))
                if ti >= 1 and m == 0:
                    wait(scalar, mm, mm_mo(ti - 1, KC - 1))
                if u >= 2:
                    t2, m2 = divmod(u - 2, 64)
                    wait(scalar, vb, vb_m(t2, m2))
                inc(scalar.activation(gss[u % 2][:], pG[u % 2][:], AF.Silu,
                                      scale=gs_sb[:, m:m + 1]), ep, 1)
                inc(scalar.activation(uss[u % 2][:], pU[u % 2][:], AF.Copy,
                                      scale=us_sb[:, m:m + 1]), ep, 1)
            wait(scalar, mm, mm_rsh(ti))
            inc(scalar.copy(rsrow[:, 256:512], rs[:, 256:512]), ep, 1)
            for mo in range(KC):
                u = ti * KC + mo
                wait(scalar, mm, mm_mo(ti, mo))
                if u >= 2:
                    t2, m2 = divmod(u - 2, KC)
                    wait(scalar, vb, vb_mo(t2, m2))
                inc(scalar.activation(dss[u % 2][:], pD[u % 2][:], AF.Copy,
                                      scale=ds_sb[:, mo:mo + 1]), ep, 1)

    @blk.vector
    def _(vector):
        for ti in range(NT):
            t0 = ti * 256
            for m in range(64):
                u = ti * 64 + m
                wait(vector, ep, ep_m(ti, m))
                inc(vector.tensor_tensor(hbuf[:, m * 256:(m + 1) * 256],
                                         gss[u % 2][:], uss[u % 2][:],
                                         OP.mult), vb, 1)
            for mo in range(KC):
                u = ti * KC + mo
                wait(vector, ep, ep_mo(ti, mo))
                r = resid[:, mo * 1024 + t0:mo * 1024 + t0 + 256]
                inc(vector.tensor_tensor(r, r, dss[u % 2][:], OP.add), vb, 1)

    return dict(done=(vb, vbb + NT * 80))

def build_ffnq_launch():
    L = Launch()
    ctx = L.ctx
    rin = L.input("rin", [D, 1024], F32)
    gw = L.input("gwt", [128, 64 * KC * 128], U8)
    uw = L.input("uwt", [128, 64 * KC * 128], U8)
    dw = L.input("dwt", [128, KC * 64 * 128], U8)
    gsc = L.input("gsc", [8192], F32)
    usc = L.input("usc", [8192], F32)
    dsc = L.input("dsc", [D], F32)
    ident = L.input("ident", [128, 128], F32)
    rout = L.output("rout", [D, 1024], F32)
    gscr = ctx.dram("gscr", [128, 64 * KC * 128], BF16)
    uscr = ctx.dram("uscr", [128, 64 * KC * 128], BF16)
    dscr = ctx.dram("dscr", [128, KC * 64 * 128], BF16)

    emit_consts(ctx, ident)
    resid, dep_r = emit_resid_in(ctx, rin)
    xhat = ctx.sbuf("xhat", [128, KC * XW], BF16)
    ga_ln = emit_ln(ctx, resid, xhat, "plain", dep=[dep_r])
    gs_sb, _ = emit_load_colvec(ctx, gsc, 64, "gs")
    us_sb, _ = emit_load_colvec(ctx, usc, 64, "us")
    ds_sb, dep_s = emit_load_colvec(ctx, dsc, KC, "ds")
    cv1 = emit_convert(ctx, gw, gscr.ap(), KC, 64, U8)
    cv2 = emit_convert(ctx, uw, uscr.ap(), KC, 64, U8)
    cv3 = emit_convert(ctx, dw, dscr.ap(), 64, KC, U8)
    q = emit_ffnq(ctx, resid, xhat, gscr.ap(), uscr.ap(), dscr.ap(),
                  gs_sb, us_sb, ds_sb,
                  dep=[(ctx.S["ga"], ga_ln)] + cv1 + cv2 + cv3 + [dep_s])
    emit_resid_out(ctx, resid, rout, dep=[q["done"]])
    return L.finish()


def attn_cfg(variant):
    if variant == "a":
        return dict(nkv=4, kvd=512, KR=516, vD=512, rope=True, bias=False)
    return dict(nkv=16, kvd=2048, KR=2064, vD=2048, rope=False, bias=True)

def emit_rope(ctx, vector, buf, col, tcol, tn, cos, sinm, t1, t2, va):
    wait(vector, va)
    inc(vector.tensor_tensor(t1[:, :tn], buf[:, col:col + tn],
                             sinm[:, tcol:tcol + tn], OP.mult), va, 1)
    wait(vector, va)
    inc(vector.tensor_copy(t2[0:64, :tn], t1[64:128, :tn]), va, 1)
    inc(vector.tensor_copy(t2[64:128, :tn], t1[0:64, :tn]), va, 1)
    wait(vector, va)
    inc(vector.tensor_tensor(t1[:, :tn], buf[:, col:col + tn],
                             cos[:, tcol:tcol + tn], OP.mult), va, 1)
    wait(vector, va)
    inc(vector.tensor_tensor(buf[:, col:col + tn], t1[:, :tn], t2[:, :tn],
                             OP.subtract), va, 1)

def emit_attn_kv(ctx, variant, xhat, wk_scr, wv_scr, bk_sb, bv_row, cos, sinm,
                 kpack, vpack, dep=()):
    cfg = attn_cfg(variant)
    nkv, kvd, vD, KR = cfg["nkv"], cfg["kvd"], cfg["vD"], cfg["KR"]
    nkc = kvd // 128
    blk = ctx.block
    mm, ep, va, sa, ga = (ctx.S[k] for k in ("mm", "ep", "va", "sa", "ga"))
    so = [ctx.S["ste"], ctx.S["stf"]]
    l2 = ctx.S["ld2"]
    lw = [ctx.S["lda"], ctx.S["ldb"]]
    lwb = [x.n for x in lw]
    sob = [x.n for x in so]
    l2b = l2.n
    ctx.push()
    ktb = [ctx.sbuf("akv_kt", [128, 1024], BF16) for _ in range(2)]
    vstg = [ctx.sbuf("akv_vs", [128, 512], BF16) for _ in range(2)]
    knr = ctx.sbuf("akv_knr", [1, 1024], BF16)
    vslab = ctx.sbuf("akv_vsl", [128, KC * 512], BF16)
    slab = [ctx.sbuf("akv_sl", [128, KC * 128], BF16) for _ in range(2)]
    tmp = ctx._ln["tmp"]
    t1, t2 = tmp[:, 0:512], tmp[:, 512:1024]
    sqb = [tmp[:, 1024:1536], tmp[:, 1536:2048]]
    pk = [ctx.psum("akv_pk", [128, 512]) for _ in range(2)]
    pv = [ctx.psum("akv_pv", [128, 512]) for _ in range(2)]
    pn = ctx.psum("akv_pn", [1, 512])
    w4 = wk_scr.rearrange("k (m kc c) -> m k kc c", kc=KC, c=128)
    v4 = wv_scr.rearrange("k (m kc c) -> m kc k c", kc=KC, c=128)

    for m in range(nkc):
        @blk.sync
        def _(sync, m=m):
            for sem, v in dep:
                wait(sync, sem, v)
            if m < 2:
                wait(sync, lw[m % 2])
            else:
                wait(sync, mm)
            dinc(sync.dma_start(
                out=slab[m % 2][:].rearrange("k (kc c) -> k kc c", c=128),
                in_=w4[m]), lw[m % 2])

        for tt in range(2):
            @blk.tensor
            def _(tensor, m=m, tt=tt):
                for sem, v in dep:
                    wait(tensor, sem, v)
                wait(tensor, lw[m % 2], lwb[m % 2] + 16 * (m // 2 + 1))
                if m * 2 + tt >= 2:
                    wait(tensor, ep)
                for kc in range(KC):
                    last = tensor.matmul(
                        pk[tt][:], slab[m % 2][:, kc * 128:(kc + 1) * 128],
                        xhat[:, xc_plain(kc, tt * 512):xc_plain(kc, tt * 512) + 512],
                        start=(kc == 0), stop=(kc == KC - 1))
                inc(last, mm, 1)

            @blk.scalar
            def _(scalar, m=m, tt=tt):
                wait(scalar, mm)
                if m >= 2 and tt == 0:
                    wait(scalar, so[m % 2], sob[m % 2] + 16 * (m // 2 - 1 + 1))
                c0 = tt * 512
                if cfg["bias"]:
                    ins = scalar.activation(ktb[m % 2][:, c0:c0 + 512],
                                            pk[tt][:], AF.Identity,
                                            bias=bk_sb[:, m:m + 1])
                else:
                    ins = scalar.copy(ktb[m % 2][:, c0:c0 + 512], pk[tt][:])
                inc(ins, ep, 1)

        if cfg["rope"]:
            @blk.vector
            def _(vector, m=m):
                wait(vector, ep)
                emit_rope(ctx, vector, ktb[m % 2], 0, 0, 512, cos, sinm, t1, t2, va)
                emit_rope(ctx, vector, ktb[m % 2], 512, 512, 512, cos, sinm,
                          t1, t2, va)

        for tt in range(2):
            @blk.scalar
            def _(scalar, m=m, tt=tt):
                if cfg["rope"]:
                    wait(scalar, va)
                else:
                    wait(scalar, ep)
                inc(scalar.activation(sqb[tt][:], ktb[m % 2][:, tt * 512:tt * 512 + 512],
                                      AF.Square), sa, 1)

            @blk.tensor
            def _(tensor, tt=tt):
                wait(tensor, sa)
                if tt == 0:
                    wait(tensor, ep)
                inc(tensor.matmul(pn[:], ctx.ones_col[:], sqb[tt][:],
                                  start=True, stop=True), mm, 1)

            @blk.scalar
            def _(scalar, m=m, tt=tt):
                wait(scalar, mm)
                if m >= 1 and tt == 0:
                    wait(scalar, l2, l2b + 16 * m)
                inc(scalar.copy(knr[:, tt * 512:tt * 512 + 512], pn[:]), ep, 1)

        @blk.scalar
        def _(scalar, m=m):
            wait(scalar, ep)
            dinc(scalar.dma_start(out=kpack[kvd + m:kvd + m + 1, :], in_=knr[:]),
                 l2)
            dinc(scalar.dma_start(out=kpack[m * 128:(m + 1) * 128, :],
                                  in_=ktb[m % 2][:]), so[m % 2])

    for dv in range(vD // 512):
        @blk.sync
        def _(sync, dv=dv):
            if dv >= 1:
                wait(sync, mm)
            for mi in range(4):
                dinc(sync.dma_start(
                    out=vslab[:].rearrange("k (kc m c) -> k kc m c",
                                           m=4, c=128)[:, :, mi],
                    in_=v4[dv * 4 + mi].rearrange("kc k c -> k kc c")),
                    ctx.S["ldc"])

        for tt in range(8):
            u = dv * 8 + tt

            @blk.tensor
            def _(tensor, u=u, tt=tt, dv=dv):
                if tt == 0:
                    wait(tensor, ctx.S["ldc"])
                if u >= 2:
                    wait(tensor, ep)
                for kc in range(KC):
                    tensor.matmul(
                        pv[u % 2][:],
                        xhat[:, xc_plain(kc, tt * 128):xc_plain(kc, tt * 128) + 128],
                        vslab[:, kc * 512:(kc + 1) * 512],
                        start=(kc == 0), stop=False)
                inc(tensor.matmul(pv[u % 2][:], ctx.ones_row_bf[:],
                                  bv_row[:, dv * 512:(dv + 1) * 512],
                                  start=False, stop=True), mm, 1)

            @blk.scalar
            def _(scalar, u=u, tt=tt, dv=dv):
                wait(scalar, mm)
                if u >= 2:
                    wait(scalar, so[u % 2], sob[u % 2] + 16 * (nkc // 2 + u // 2 - 1 + 1))
                inc(scalar.copy(vstg[u % 2][:], pv[u % 2][:]), ep, 1)
                wait(scalar, ep)
                dinc(scalar.dma_start(
                    out=vpack[tt * 128:(tt + 1) * 128, dv * 512:(dv + 1) * 512],
                    in_=vstg[u % 2][:]), so[u % 2])

    @blk.gpsimd
    def _(gpsimd):
        wait(gpsimd, ep)
        wait(gpsimd, so[0])
        wait(gpsimd, so[1])
        wait(gpsimd, l2)
        inc(gpsimd.engine_nop(), ga, 1)

    ctx.pop()
    return [(ga, ga.n)]

def emit_attn_core(ctx, variant, resid, xhat, wq_scr, wo_scr, bq_sb, bo_sb,
                   cos, sinm, kag, vag, masks, eq, dep=()):
    cfg = attn_cfg(variant)
    nkv, kvd, vD, KR = cfg["nkv"], cfg["kvd"], cfg["vD"], cfg["KR"]
    blk = ctx.block
    mm, ep, va, vb, sa, sb, ga = (ctx.S[k] for k in
                                  ("mm", "ep", "va", "vb", "sa", "sb", "ga"))
    lq = [ctx.S["lda"], ctx.S["ldb"]]
    lk = [ctx.S["lde"], ctx.S["ldf"]]
    lv = [ctx.S["ste"], ctx.S["stf"]]
    lm = ctx.S["ldc"]
    ld = ctx.S["ld"]
    ctx.push()
    qt = ctx.sbuf("at_qt", [128, 16 * 1024], BF16)
    qn2 = ctx.sbuf("at_qn2", [1, 128], F32)
    negBs = ctx.sbuf("at_negB", [128, 32], F32)
    brow = ctx.sbuf("at_brow", [1, 96], F32)
    knl = ctx.sbuf("at_knl", [nkv, 1024], BF16)
    kmx = ctx.sbuf("at_kmx", [nkv, 2], F32)
    PT = ctx.sbuf("at_ptb", [128, 1024], BF16)
    kblk = [ctx.sbuf("at_kb", [128, nkv * 256], BF16) for _ in range(2)]
    vblk = [ctx.sbuf("at_vb", [128, 2 * vD], BF16) for _ in range(2)]
    mbuf = ctx.sbuf("at_mb", [128, 4 * 512], BF16)
    ao = ctx.sbuf("at_ao", [128, 2048], BF16)
    rsv = ctx.sbuf("at_rsv", [16, 128], F32)
    rec = ctx.sbuf("at_rec", [128, 16], F32)
    slab = [ctx.sbuf("at_sl", [128, KC * 128], BF16) for _ in range(2)]
    tmp = ctx._ln["tmp"]
    t1, t2 = tmp[:, 0:512], tmp[:, 512:1024]
    sqv = tmp[:, 1024:1536]
    ctx.push()
    pq = [ctx.psum("at_pq", [128, 512]) for _ in range(2)]
    pn = ctx.psum("at_pn", [1, 512])
    w4q = wq_scr.rearrange("k (m kc c) -> m k kc c", kc=KC, c=128)
    lqb = [x.n for x in lq]

    _sca = ctx.nc.named_scope("qprep"); _sca.__enter__()
    for u in range(32):
        m, tt = divmod(u, 2)
        if tt == 0:
            @blk.sync
            def _(sync, m=m):
                if m < 2:
                    wait(sync, lq[m % 2])
                else:
                    wait(sync, mm)
                dinc(sync.dma_start(
                    out=slab[m % 2][:].rearrange("k (kc c) -> k kc c", c=128),
                    in_=w4q[m]), lq[m % 2])

        @blk.tensor
        def _(tensor, u=u, m=m, tt=tt):
            for sem, v in dep:
                wait(tensor, sem, v)
            wait(tensor, lq[m % 2], lqb[m % 2] + 16 * (m // 2 + 1))
            if u >= 2:
                wait(tensor, ep)
            for kc in range(KC):
                last = tensor.matmul(
                    pq[u % 2][:], slab[m % 2][:, kc * 128:(kc + 1) * 128],
                    xhat[:, xc_plain(kc, tt * 512):xc_plain(kc, tt * 512) + 512],
                    start=(kc == 0), stop=(kc == KC - 1))
            inc(last, mm, 1)

        @blk.scalar
        def _(scalar, u=u, m=m, tt=tt):
            wait(scalar, mm)
            c0 = m * 1024 + tt * 512
            if cfg["bias"]:
                ins = scalar.activation(qt[:, c0:c0 + 512], pq[u % 2][:],
                                        AF.Identity, bias=bq_sb[:, m:m + 1])
            else:
                ins = scalar.copy(qt[:, c0:c0 + 512], pq[u % 2][:])
            inc(ins, ep, 1)

        if cfg["rope"]:
            @blk.vector
            def _(vector, m=m, tt=tt):
                wait(vector, ep)
                emit_rope(ctx, vector, qt, m * 1024 + tt * 512, tt * 512, 512,
                          cos, sinm, t1, t2, va)

        @blk.scalar
        def _(scalar, m=m, tt=tt):
            if cfg["rope"]:
                wait(scalar, va)
            else:
                wait(scalar, ep)
            wait(scalar, vb)
            inc(scalar.activation(sqv[:], qt[:, m * 1024 + tt * 512:
                                             m * 1024 + tt * 512 + 512],
                                  AF.Square), sa, 1)

        @blk.tensor
        def _(tensor):
            wait(tensor, sa)
            inc(tensor.matmul(pn[:], ctx.ones_col[:], sqv[:],
                              start=True, stop=True), mm, 1)

        @blk.vector
        def _(vector, m=m, tt=tt):
            wait(vector, mm)
            qhq, qi = divmod(m, 4)
            for h in range(4):
                st = tt * 4 + h
                inc(vector.reduce_max(
                    qn2[:, qhq * 32 + st * 4 + qi:qhq * 32 + st * 4 + qi + 1],
                    pn[:, h * 128:h * 128 + 128], axis=mybir.AxisListType.X),
                    vb, 1)

    _sca.__exit__(None, None, None)
    _sca = ctx.nc.named_scope("bounds"); _sca.__enter__()
    for r in range(GROUP):
        @blk.sync
        def _(sync, r=r):
            wait(sync, ld)
            if r >= 1:
                wait(sync, vb)
            dinc(sync.dma_start(out=knl[:, :],
                                in_=kag[r * KR + kvd:r * KR + kvd + nkv, :]), ld)

        @blk.vector
        def _(vector, r=r):
            wait(vector, ld)
            inc(vector.reduce_max(kmx[:, 1:2] if r else kmx[:, 0:1], knl[:, :],
                                  axis=mybir.AxisListType.X), vb, 1)
            if r:
                wait(vector, vb)
                inc(vector.tensor_tensor(kmx[:, 0:1], kmx[:, 0:1], kmx[:, 1:2],
                                         OP.max), vb, 1)

    @blk.tensor
    def _(tensor):
        wait(tensor, vb)
        wait(tensor, ep)
        inc(tensor.transpose(pn[:, :nkv], kmx[:, 0:1], ctx.ident[:nkv, :nkv]),
            mm, 1)

    @blk.scalar
    def _(scalar):
        wait(scalar, mm)
        inc(scalar.copy(brow[:, 0:nkv], pn[:, :nkv]), sb, 1)

    @blk.vector
    def _(vector):
        wait(vector, sb)
        if variant == "a":
            inc(vector.tensor_copy(brow[:, 16:20], brow[:, 0:4]), vb, 1)
        else:
            inc(vector.reduce_max(brow[:, 16:20],
                                  brow[:, 0:16].rearrange("o (q i) -> o q i", i=4),
                                  axis=mybir.AxisListType.X), vb, 1)
        for q in range(4):
            wait(vector, vb)
            inc(vector.tensor_scalar(brow[:, 24 + q * 8:24 + q * 8 + 8],
                                     ctx.ones_row[:, 0:8], brow[:, 16 + q:17 + q],
                                     None, OP.mult), vb, 1)
        inc(vector.reduce_max(brow[:, 56:88],
                              qn2[:].rearrange("o (g i) -> o g i", i=4),
                              axis=mybir.AxisListType.X), vb, 1)
        wait(vector, vb)
        inc(vector.tensor_tensor(brow[:, 56:88], brow[:, 56:88], brow[:, 24:56],
                                 OP.mult), vb, 1)

    @blk.scalar
    def _(scalar):
        wait(scalar, vb)
        inc(scalar.activation(brow[:, 56:88], brow[:, 56:88], AF.Sqrt), sb, 1)

    @blk.vector
    def _(vector):
        wait(vector, sb)
        inc(vector.tensor_scalar(brow[:, 56:88], brow[:, 56:88],
                                 float(-1.01 / math.sqrt(128.0)), None,
                                 OP.mult), vb, 1)

    @blk.tensor
    def _(tensor):
        wait(tensor, vb)
        inc(tensor.matmul(pq[0][:, 0:32], ctx.ones_row[:], brow[:, 56:88],
                          start=True, stop=True), mm, 1)

    @blk.scalar
    def _(scalar):
        wait(scalar, mm)
        inc(scalar.copy(negBs[:], pq[0][:, 0:32]), sb, 1)

    _sca.__exit__(None, None, None)
    _sca = ctx.nc.named_scope("flash"); _sca.__enter__()
    ctx.pop()
    OT = ctx.psum("at_OT", [128, 2048])
    RSp = ctx.psum("at_RS", [16, 128])
    stq = [ctx.psum("at_stq", [128, 512]) for _ in range(2)]
    ptp = ctx.psum("at_ptp", [128, 128], BF16)
    sc128 = np.float32(1.0 / math.sqrt(128.0))

    def kvh_of(qh):
        return qh // 4 if variant == "a" else qh

    vis = [(s, qh2, j) for s in range(4) for qh2 in range(2)
           for j in range(4 * (s + 1))]

    def kv_dma(sync, s, qh2, j, vi):
        r, sl = BLK_OWNER[j]
        scol = sl * 256
        dinc(sync.dma_start(
            out=kblk[vi % 2][:].rearrange("k (h t) -> k h t", t=256),
            in_=kag[r * KR:r * KR + kvd, scol:scol + 256]
                .rearrange("(h k) t -> k h t", k=128)), lk[vi % 2])
        dinc(sync.dma_start(
            out=vblk[vi % 2][:].rearrange("k (h d) -> k h d", d=vD),
            in_=vag[r * 1024 + scol:r * 1024 + scol + 256, :]
                .rearrange("(h k) d -> k h d", k=128)), lv[vi % 2])

    lkb = [x.n for x in lk]
    lvb = [x.n for x in lv]
    nkv_dma = [0, 0]

    for vi, (s, qh2, j) in enumerate(vis):
        ns = 4 * (s + 1)
        qoff = s * 256 + qh2 * 128
        newtile = (j == 0)

        @blk.sync
        def _(sync, s=s, qh2=qh2, j=j, vi=vi, ns=ns, newtile=newtile):
            if newtile:
                for jj in range(min(2, ns)):
                    if vi + jj < 2:
                        wait(sync, lk[(vi + jj) % 2])
                        wait(sync, lv[(vi + jj) % 2])
                    else:
                        wait(sync, mm)
                    kv_dma(sync, s, qh2, jj, vi + jj)
                wait(sync, mm)
                for jr in range(2):
                    for kh in range(2):
                        dinc(sync.dma_start(
                            out=mbuf[:, (jr * 2 + kh) * 512:(jr * 2 + kh) * 512 + 512],
                            in_=masks[s, qh2, jr, kh]), lm)
            elif j + 1 < ns:
                wait(sync, mm)
                kv_dma(sync, s, qh2, j + 1, vi + 1)
            if j == ns - 2:
                for jr in range(2, 4):
                    for kh in range(2):
                        dinc(sync.dma_start(
                            out=mbuf[:, ((jr - 2) * 2 + kh) * 512:
                                     ((jr - 2) * 2 + kh) * 512 + 512],
                            in_=masks[s, qh2, jr, kh]), lm)

        for quad in range(4):
            @blk.tensor
            def _(tensor, s=s, qh2=qh2, j=j, vi=vi, ns=ns, quad=quad, qoff=qoff):
                if quad == 0:
                    nkv_dma[0] += 1
                    wait(tensor, lk[vi % 2], lkb[vi % 2] + 16 * ((vi // 2) + 1))
                    wait(tensor, lv[vi % 2], lvb[vi % 2] + 16 * ((vi // 2) + 1))
                    if j >= ns - 4:
                        wait(tensor, lm)
                if vi == 0 and quad == 0:
                    wait(tensor, ep)
                    wait(tensor, va)
                    wait(tensor, sb)
                    wait(tensor, vb)
                if quad >= 1 or vi > 0:
                    wait(tensor, sa)
                if j == 0 and quad == 0:
                    wait(tensor, vb)
                last = None
                for kh in range(2):
                    masked = j >= ns - 4
                    for qi in range(4):
                        qh = quad * 4 + qi
                        last = tensor.matmul(
                            stq[kh][:, qi * 128:qi * 128 + 128],
                            kblk[vi % 2][:, kvh_of(qh) * 256 + kh * 128:
                                         kvh_of(qh) * 256 + kh * 128 + 128],
                            qt[:, qh * 1024 + qoff:qh * 1024 + qoff + 128],
                            start=(qi == 0), stop=(qi == 3 and not masked))
                        if masked:
                            jr = (j - (ns - 4)) % 2
                            mc = (jr * 2 + kh) * 512 + qi * 128
                            last = tensor.matmul(
                                stq[kh][:, qi * 128:qi * 128 + 128],
                                ctx.ident_bf[:], mbuf[:, mc:mc + 128],
                                start=False, stop=(qi == 3))
                inc(last, mm, 1)

            @blk.scalar
            def _(scalar, s=s, qh2=qh2, quad=quad, vi=vi):
                wait(scalar, mm)
                if vi == 0 and quad == 0:
                    wait(scalar, sb)
                nb = quad * 8 + s * 2 + qh2
                for kh in range(2):
                    ins = scalar.activation(PT[:, kh * 512:kh * 512 + 512],
                                            stq[kh][:], AF.Exp,
                                            bias=negBs[:, nb:nb + 1],
                                            scale=sc128)
                inc(ins, sa, 1)

            @blk.tensor
            def _(tensor, s=s, qh2=qh2, j=j, vi=vi, quad=quad, ns=ns):
                wait(tensor, sa)
                if j == 0 and quad == 0:
                    wait(tensor, sb)
                first = (j == 0)
                lastv = (j == ns - 1)
                for kh in range(2):
                    for qi in range(4):
                        qh = quad * 4 + qi
                        tensor.matmul(
                            OT[:, qh * 128:qh * 128 + 128],
                            PT[:, kh * 512 + qi * 128:kh * 512 + qi * 128 + 128],
                            vblk[vi % 2][:, kh * vD + kvh_of(qh) * 128:
                                         kh * vD + kvh_of(qh) * 128 + 128],
                            start=(first and kh == 0 and qi == 0),
                            stop=(lastv and kh == 1 and qi == 3))
                        last = tensor.matmul(
                            RSp[:], eq[:, qh * 16:qh * 16 + 16],
                            PT[:, kh * 512 + qi * 128:kh * 512 + qi * 128 + 128],
                            start=(first and kh == 0 and qi == 0 and quad == 0),
                            stop=(lastv and kh == 1 and qi == 3 and quad == 3))
                inc(last, mm, 1)

        if j == ns - 1:
            @blk.scalar
            def _(scalar):
                wait(scalar, mm)
                inc(scalar.copy(rsv[:], RSp[:]), sb, 1)

            @blk.tensor
            def _(tensor):
                wait(tensor, sb)
                wait(tensor, sa)
                inc(tensor.transpose(stq[0][:, 0:16], rsv[:],
                                     ctx.ident[:16, :16]), mm, 1)

            @blk.vector
            def _(vector):
                wait(vector, mm)
                inc(vector.reciprocal(rec[:], stq[0][:, 0:16]), vb, 1)
                wait(vector, vb)
                for dvc in range(16):
                    inc(vector.tensor_scalar(ao[:, dvc * 128:dvc * 128 + 128],
                                             OT[:, dvc * 128:dvc * 128 + 128],
                                             rec[:, dvc:dvc + 1], None,
                                             OP.mult), vb, 1)

            for dvc in range(16):
                @blk.tensor
                def _(tensor, dvc=dvc):
                    wait(tensor, vb)
                    wait(tensor, ep)
                    inc(tensor.transpose(ptp[:], ao[:, dvc * 128:dvc * 128 + 128],
                                         ctx.ident_bf[:]), mm, 1)

                @blk.scalar
                def _(scalar, s=s, qh2=qh2, dvc=dvc):
                    wait(scalar, mm)
                    st = s * 2 + qh2
                    inc(scalar.copy(xhat[:, xc_plain(dvc, st * 128):
                                        xc_plain(dvc, st * 128) + 128],
                                    ptp[:]), ep, 1)

    _sca.__exit__(None, None, None)
    _sca = ctx.nc.named_scope("outproj"); _sca.__enter__()
    w4o = wo_scr.rearrange("k (m kc c) -> m k kc c", kc=KC, c=128)
    for u in range(32):
        m, tt = divmod(u, 2)
        if tt == 0:
            @blk.sync
            def _(sync, m=m):
                wait(sync, mm)
                dinc(sync.dma_start(
                    out=slab[m % 2][:].rearrange("k (kc c) -> k kc c", c=128),
                    in_=w4o[m]), lq[m % 2])

        @blk.tensor
        def _(tensor, u=u, m=m, tt=tt):
            wait(tensor, lq[m % 2])
            if u == 0:
                wait(tensor, ep)
                wait(tensor, vb)
            if u >= 2:
                wait(tensor, vb)
            for kc in range(KC):
                last = tensor.matmul(
                    stq[u % 2][:], slab[m % 2][:, kc * 128:(kc + 1) * 128],
                    xhat[:, xc_plain(kc, tt * 512):xc_plain(kc, tt * 512) + 512],
                    start=(kc == 0), stop=(kc == KC - 1))
            inc(last, mm, 1)

        @blk.vector
        def _(vector, u=u, m=m, tt=tt):
            wait(vector, mm)
            r = resid[:, m * 1024 + tt * 512:m * 1024 + tt * 512 + 512]
            sc = bo_sb[:, m:m + 1] if bo_sb is not None else 0.0
            inc(vector.scalar_tensor_tensor(r, stq[u % 2][:], sc, r,
                                            OP.add, OP.add), vb, 1)

    _sca.__exit__(None, None, None)
    ctx.pop()
    return [(ctx.S["vb"], ctx.S["vb"].n)]

def emit_load_row(ctx, src_dram, n, name="row", dt=BF16):
    ld, va = ctx.S["ld"], ctx.S["va"]
    stg = ctx.sbuf(name + "_stg", [1, n], F32)
    out = ctx.sbuf(name, [1, n], dt)

    @ctx.block.sync
    def _(sync):
        wait(sync, ld)
        dinc(sync.dma_start(out=stg[:], in_=src_dram.rearrange("(o n) -> o n", o=1)),
             ld)

    @ctx.block.vector
    def _(vector):
        wait(vector, ctx.S["ld"])
        inc(vector.tensor_copy(out[:], stg[:]), va, 1)

    return out, (va, va.n)

def emit_load_tile(ctx, src_dram, shape, name, dt):
    t = ctx.sbuf(name, shape, dt)
    ld = ctx.S["ld"]

    @ctx.block.sync
    def _(sync):
        wait(sync, ld)
        dinc(sync.dma_start(out=t[:], in_=src_dram), ld)
    return t, (ld, ld.n)

def host_masks(core):
    g = core % GROUP
    m = np.zeros((4, 2, 4, 2, 128, 512), np.float32)
    for s in range(4):
        qb = blocks_of(g)[s]
        for qh2 in range(2):
            for jr in range(4):
                j = 4 * s + jr
                for kh in range(2):
                    kpos = j * 256 + kh * 128 + np.arange(128)[:, None]
                    qpos = qb * 256 + qh2 * 128 + np.arange(128)[None, :]
                    t = np.where(kpos <= qpos, 0.0, -30000.0)
                    m[s, qh2, jr, kh] = np.tile(t, (1, 4))
    return m.astype(BF)

def host_eq():
    e = np.zeros((128, 256), np.float32)
    for qh in range(16):
        e[:, qh * 16 + qh] = 1.0
    return e.astype(BF)

def build_attn_kv_launch(variant):
    cfg = attn_cfg(variant)
    kvd, vD, KR = cfg["kvd"], cfg["vD"], cfg["KR"]
    L = Launch()
    ctx = L.ctx
    rin = L.input("rin", [D, 1024], F32)
    wk = L.input("wk", [128, (kvd // 128) * KC * 128], F32)
    wv = L.input("wv", [128, (vD // 128) * KC * 128], F32)
    ident = L.input("ident", [128, 128], F32)
    kpack = L.output("kpack", [KR, 1024], BF16)
    vpack = L.output("vpack", [1024, vD], BF16)
    kscr = ctx.dram("kscr", [128, (kvd // 128) * KC * 128], BF16)
    vscr = ctx.dram("vscr", [128, (vD // 128) * KC * 128], BF16)
    emit_consts(ctx, ident)
    resid, dep_r = emit_resid_in(ctx, rin)
    xhat = ctx.sbuf("xhat", [128, KC * XW], BF16)
    ga_ln = emit_ln(ctx, resid, xhat, "plain", dep=[dep_r])
    dep = [(ctx.S["ga"], ga_ln)]
    if cfg["bias"]:
        bkv = L.input("bk", [kvd], F32)
        bvv = L.input("bv", [vD], F32)
        bk_sb, _ = emit_load_colvec(ctx, bkv, kvd // 128, "bk")
        bv_row, dep_bv = emit_load_row(ctx, bvv, vD, "bv")
        dep.append(dep_bv)
        cos = sinm = None
    else:
        bk_sb = None
        bv_row = ctx.sbuf("bv0", [1, vD], BF16)
        cosd = L.input("cos", [128, 1024], F32)
        sind = L.input("sinm", [128, 1024], F32)
        cos, dep_c = emit_load_tile(ctx, cosd, [128, 1024], "cos", F32)
        sinm, dep_s = emit_load_tile(ctx, sind, [128, 1024], "sinm", F32)
        dep += [dep_c, dep_s]

        @ctx.block.vector
        def _(vector):
            inc(vector.memset(bv_row[:], 0.0), ctx.S["va"], 1)
        dep.append((ctx.S["va"], ctx.S["va"].n))
    cv1 = emit_convert(ctx, wk, kscr.ap(), KC, kvd // 128, F32)
    cv2 = emit_convert(ctx, wv, vscr.ap(), KC, vD // 128, F32)
    fin = emit_attn_kv(ctx, variant, xhat, kscr.ap(), vscr.ap(), bk_sb, bv_row,
                       cos, sinm, kpack, vpack, dep=dep + cv1 + cv2)

    @ctx.block.sync
    def _(sync):
        for sem, v in fin:
            wait(sync, sem, v)
    return L.finish()

def build_attn_core_launch(variant):
    cfg = attn_cfg(variant)
    kvd, vD, KR = cfg["kvd"], cfg["vD"], cfg["KR"]
    L = Launch()
    ctx = L.ctx
    rin = L.input("rin", [D, 1024], F32)
    wq = L.input("wq", [128, KC * KC * 128], F32)
    wo = L.input("wo", [128, KC * KC * 128], F32)
    kag = L.input("kag", [4 * KR, 1024], BF16)
    vag = L.input("vag", [4096, vD], BF16)
    masks = L.input("masks", [4, 2, 4, 2, 128, 512], BF16)
    eqc = L.input("eqc", [128, 256], BF16)
    ident = L.input("ident", [128, 128], F32)
    rout = L.output("rout", [D, 1024], F32)
    qscr = ctx.dram("qscr", [128, KC * KC * 128], BF16)
    oscr = ctx.dram("oscr", [128, KC * KC * 128], BF16)
    emit_consts(ctx, ident)
    resid, dep_r = emit_resid_in(ctx, rin)
    xhat = ctx.sbuf("xhat", [128, KC * XW], BF16)
    ga_ln = emit_ln(ctx, resid, xhat, "plain", dep=[dep_r])
    dep = [(ctx.S["ga"], ga_ln)]
    eq, dep_e = emit_load_tile(ctx, eqc, [128, 256], "eq", BF16)
    dep.append(dep_e)
    if cfg["bias"]:
        bqv = L.input("bq", [D], F32)
        bov = L.input("bo", [D], F32)
        bq_sb, _ = emit_load_colvec(ctx, bqv, KC, "bq")
        bo_sb, dep_b = emit_load_colvec(ctx, bov, KC, "bo")
        dep.append(dep_b)
        cos = sinm = None
    else:
        bq_sb = bo_sb = None
        cosd = L.input("cos", [128, 1024], F32)
        sind = L.input("sinm", [128, 1024], F32)
        cos, dep_c = emit_load_tile(ctx, cosd, [128, 1024], "cos", F32)
        sinm, dep_s = emit_load_tile(ctx, sind, [128, 1024], "sinm", F32)
        dep += [dep_c, dep_s]
    cv1 = emit_convert(ctx, wq, qscr.ap(), KC, KC, F32)
    cv2 = emit_convert(ctx, wo, oscr.ap(), KC, KC, F32)
    fin = emit_attn_core(ctx, variant, resid, xhat, qscr.ap(), oscr.ap(),
                         bq_sb, bo_sb, cos, sinm, kag, vag, masks, eq,
                         dep=dep + cv1 + cv2)
    emit_resid_out(ctx, resid, rout, dep=fin)
    return L.finish()


def mamba_cfg(variant):
    return dict(in_bias=(variant == "a"), conv_bias=(variant == "b"),
                out_bias=(variant == "a"), gate_first=(variant == "a"))

MMC = 67
NIC = 32

def emit_mamba_m1(ctx, xhat, hpack, dep=()):
    blk = ctx.block
    mm, ep, sb = ctx.S["mm"], ctx.S["ep"], ctx.S["sb"]
    l2 = ctx.S["ld2"]
    ctx.push()
    hst = ctx.sbuf("m1_hst", [3, 4 * 2048], BF16)
    pt = ctx.psum("m1_pt", [3, 128], BF16)
    for s in range(4):
        for kc in range(KC):
            @blk.tensor
            def _(tensor, s=s, kc=kc):
                for sem, v in dep:
                    wait(tensor, sem, v)
                wait(tensor, ctx.S["va"], ctx.consts_va)
                wait(tensor, ep)
                inc(tensor.transpose(
                    pt[:], xhat[:, kc * XW + s * BW + BW - 3:kc * XW + s * BW + BW],
                    ctx.ident_bf[:]), mm, 1)

            @blk.scalar
            def _(scalar, s=s, kc=kc):
                wait(scalar, mm)
                inc(scalar.copy(hst[0:3, s * 2048 + kc * 128:s * 2048 + (kc + 1) * 128],
                                pt[0:3, :]), ep, 1)

    @blk.scalar
    def _(scalar):
        wait(scalar, ep)
        for s4 in range(4):
            dinc(scalar.dma_start(out=hpack[s4 * 3:(s4 + 1) * 3, :],
                                  in_=hst[:, s4 * 2048:(s4 + 1) * 2048]), l2)
        wait(scalar, l2)
    ctx.pop()
    return [(l2, l2.n)]

def emit_halo_select(ctx, xhat, hgat, hsel, dep=()):
    blk = ctx.block
    mm, ep, va = ctx.S["mm"], ctx.S["ep"], ctx.S["va"]
    ld = ctx.S["ld"]
    ctx.push()
    hg = ctx.sbuf("hs_hg", [48, 2048], BF16)
    hs = ctx.sbuf("hs_sel", [48, 12], BF16)
    htok = ctx.sbuf("hs_tok", [12, 2048], BF16)
    pg = ctx.psum("hs_pg", [12, 512])
    ptr = ctx.psum("hs_ptr", [128, 12], BF16)

    @blk.sync
    def _(sync):
        for sem, v in dep:
            wait(sync, sem, v)
        wait(sync, ld)
        dinc(sync.dma_start(out=hg[:], in_=hgat), ld)
        wait(sync, ld)
        dinc(sync.dma_start(out=hs[:], in_=hsel), ld)

    for t4 in range(4):
        @blk.tensor
        def _(tensor, t4=t4):
            wait(tensor, ld)
            if t4 >= 1:
                wait(tensor, ep)
            inc(tensor.matmul(pg[:], hs[:], hg[:, t4 * 512:(t4 + 1) * 512],
                              start=True, stop=True), mm, 1)

        @blk.scalar
        def _(scalar, t4=t4):
            wait(scalar, mm)
            inc(scalar.copy(htok[:, t4 * 512:(t4 + 1) * 512], pg[:]), ep, 1)

    for kc in range(KC):
        @blk.tensor
        def _(tensor, kc=kc):
            wait(tensor, ep)
            if kc >= 1:
                wait(tensor, ctx.S["sa"])
            inc(tensor.transpose(ptr[:, :12], htok[:, kc * 128:(kc + 1) * 128],
                                 ctx.ident_bf[:12, :12]), mm, 1)

        @blk.scalar
        def _(scalar, kc=kc):
            wait(scalar, mm)
            for s in range(4):
                ins = scalar.copy(xhat[:, kc * XW + s * BW:kc * XW + s * BW + 3],
                                  ptr[:, s * 3:s * 3 + 3])
            inc(ins, ctx.S["sa"], 1)
    ctx.pop()
    return [(ctx.S["sa"], ctx.S["sa"].n)]

def emit_mamba_m2(ctx, variant, xhat, wi_scr, bin_sb, cw_sb, cb_sb, dtb_sb,
                  acol, dbc, rsb, mtri, rfr2, gate_sp, yp_sp, c_sp, sd_sp,
                  spack, ctile, dep=()):
    cfg = mamba_cfg(variant)
    blk = ctx.block
    mm, ep, va, vb, sa, sb, ga = (ctx.S[k] for k in
                                  ("mm", "ep", "va", "vb", "sa", "sb", "ga"))
    lw = [ctx.S["lda"], ctx.S["ldb"]]
    so = [ctx.S["ste"], ctx.S["stf"]]
    l2 = ctx.S["ld2"]
    lwb = [x.n for x in lw]
    l2b = l2.n
    bt = ctx.S["bt"]
    btb = bt.n
    ctx.push()
    slab = [ctx.sbuf("m2_sl", [128, KC * 128], BF16) for _ in range(2)]
    xbc = ctx.sbuf("m2_xbc", [128, 34 * BW], BF16)
    xcs = ctx.sbuf("m2_xcs", [128, 34 * 256], BF16)
    gst = [ctx.sbuf("m2_gst", [128, 256], BF16)]
    cacc = ctx.sbuf("m2_cacc", [128, 256], F32)
    sstg = cacc[:].bitcast(BF16)
    dtf = ctx._ln["tmp"][64:128, 0:BW]
    dw = ctx._ln["tmp"]
    cumt = ctx._ln["coef"][:, 320:448]
    fbf = ctx.sbuf("m2_fbf", [64, 256], BF16)
    ddbf = ctx.sbuf("m2_ddbf", [64, 256], BF16)
    crow = ctx._ln["stat"][:, 0:512]
    ebuf = ctx.sbuf("m2_e", [128, 256], BF16)
    sdt = ctx._ln["coef"][:].bitcast(BF16)[:, 896:1024]
    xdfm = ctx.sbuf("m2_xdfm", [128, 256], BF16)
    xdt = ctx.sbuf("m2_xdt", [128, 2 * 4096], BF16)
    ybuf = ctx.sbuf("m2_yb", [128, 2 * 4096], BF16)
    btk = ctx.sbuf("m2_btk", [128, 2 * 128], BF16)
    gbuf = ctx.sbuf("m2_g", [128, 2 * 256], BF16)
    stg = [ctx._ln["sq"][0][:].bitcast(BF16)]
    ytmp = ctx._ln["coef"][:, 256:320]
    p1 = [ctx.psum("m2_p1", [128, 512]) for _ in range(2)]
    p2 = ctx.psum("m2_p2", [128, 256])
    p3 = ctx.psum("m2_p3", [128, 256])
    pt = ctx.psum("m2_pt", [128, 128], BF16)
    ptf = ctx.psum("m2_ptf", [128, 128])
    p4 = ctx.psum("m2_p4", [128, 512])
    w4 = wi_scr.rearrange("k (m kc c) -> m k kc c", kc=KC, c=128)

    def dsl(i):
        return dw[0:64, i * 256:(i + 1) * 256]

    ct_done = {}

    for s in range(4):
        xo = s * BW
        _sc = ctx.nc.named_scope(f"inproj{s}"); _sc.__enter__()
        epb_s = ep.n
        for mc in range(MMC):
            u2 = s * MMC + mc

            @blk.sync
            def _(sync, mc=mc, u2=u2):
                for sem, v in dep:
                    wait(sync, sem, v)
                if u2 < 2:
                    wait(sync, lw[u2 % 2])
                else:
                    wait(sync, bt, btb + u2 - 1)
                dinc(sync.dma_start(
                    out=slab[u2 % 2][:].rearrange("k (kc c) -> k kc c", c=128),
                    in_=w4[mc]), lw[u2 % 2])

            @blk.tensor
            def _(tensor, mc=mc, u2=u2, xo=xo, epb_s=epb_s):
                for sem, v in dep:
                    wait(tensor, sem, v)
                wait(tensor, lw[u2 % 2],
                     lwb[u2 % 2] + 16 * (u2 // 2 + 1))
                if mc >= 2:
                    wait(tensor, ep, epb_s + mc - 1)
                else:
                    wait(tensor, ep)
                    wait(tensor, sa)
                    wait(tensor, sb)
                for kc in range(KC):
                    last = tensor.matmul(
                        p1[u2 % 2][:, :BW], slab[u2 % 2][:, kc * 128:(kc + 1) * 128],
                        xhat[:, kc * XW + xo:kc * XW + xo + BW],
                        start=(kc == 0), stop=(kc == KC - 1))
                inc(last, bt, 1)

            @blk.scalar
            def _(scalar, s=s, mc=mc, u2=u2):
                wait(scalar, bt, btb + u2 + 1)
                wait(scalar, va)
                ps = p1[u2 % 2]
                if mc < 32:
                    wait(scalar, so[0])
                    if cfg["in_bias"]:
                        ins = scalar.activation(gst[0][:], ps[:, 3:BW],
                                                AF.Identity, bias=bin_sb[:, mc:mc + 1])
                    else:
                        ins = scalar.copy(gst[0][:], ps[:, 3:BW])
                    inc(ins, ep, 1)
                    wait(scalar, ep)
                    dinc(scalar.dma_start(
                        out=gate_sp[mc * 128:(mc + 1) * 128, s * 256:(s + 1) * 256],
                        in_=gst[0][:]), so[0])
                elif mc < 66:
                    c0 = (mc - 32) * BW
                    if cfg["in_bias"]:
                        ins = scalar.activation(xbc[:, c0:c0 + BW], ps[:, :BW],
                                                AF.Identity, bias=bin_sb[:, mc:mc + 1])
                    else:
                        ins = scalar.copy(xbc[:, c0:c0 + BW], ps[:, :BW])
                    inc(ins, ep, 1)
                else:
                    if cfg["in_bias"]:
                        ins = scalar.activation(dtf, ps[0:64, :BW], AF.Identity,
                                                bias=bin_sb[0:64, mc:mc + 1])
                    else:
                        ins = scalar.copy(dtf, ps[0:64, :BW])
                    inc(ins, ep, 1)

        _sc.__exit__(None, None, None)
        _sc = ctx.nc.named_scope(f"conv{s}"); _sc.__enter__()
        for kc in range(34):
            @blk.vector
            def _(vector, kc=kc):
                wait(vector, ep)
                wait(vector, sa)
                if kc == 0:
                    wait(vector, mm)
                inc(vector.tensor_scalar(cacc[:], xbc[:, kc * BW:kc * BW + 256],
                                         cw_sb[:, kc * 4:kc * 4 + 1], None,
                                         OP.mult), va, 1)
                for k in range(1, 4):
                    wait(vector, va)
                    inc(vector.scalar_tensor_tensor(
                        cacc[:], xbc[:, kc * BW + k:kc * BW + k + 256],
                        cw_sb[:, kc * 4 + k:kc * 4 + k + 1], cacc[:],
                        OP.mult, OP.add), va, 1)

            @blk.scalar
            def _(scalar, kc=kc):
                wait(scalar, va)
                wait(scalar, vb)
                wait(scalar, l2)
                if cfg["conv_bias"]:
                    ins = scalar.activation(xcs[:, kc * 256:(kc + 1) * 256],
                                            cacc[:], AF.Silu,
                                            bias=cb_sb[:, kc:kc + 1])
                else:
                    ins = scalar.activation(xcs[:, kc * 256:(kc + 1) * 256],
                                            cacc[:], AF.Silu)
                inc(ins, sa, 1)

        _sc.__exit__(None, None, None)
        _sc = ctx.nc.named_scope(f"mid{s}"); _sc.__enter__()
        @blk.vector
        def _(vector):
            wait(vector, ep)
            wait(vector, sa)
            inc(vector.tensor_scalar(dsl(0), dtf[:, 3:BW], dtb_sb[64:128, 0:1],
                                     None, OP.add), va, 1)

        @blk.scalar
        def _(scalar):
            wait(scalar, va)
            inc(scalar.activation(dsl(1), dsl(0), AF.Abs), sa, 1)
            wait(scalar, sa)
            inc(scalar.activation(dsl(2), dsl(1), AF.Exp, scale=np.float32(-1.0)),
                sa, 1)
            wait(scalar, sa)
            inc(scalar.activation(dsl(3), dsl(2), AF.Ln, bias=np.float32(1.0)),
                sa, 1)

        @blk.vector
        def _(vector):
            wait(vector, sa)
            inc(vector.tensor_scalar(dsl(4), dsl(0), 0.0, None, OP.max), va, 1)
            wait(vector, va)
            inc(vector.tensor_tensor(dsl(5), dsl(4), dsl(3), OP.add), va, 1)
            wait(vector, va)
            inc(vector.tensor_scalar(dsl(6), dsl(5), acol[0:64, 0:1], None,
                                     OP.mult), va, 1)
            wait(vector, va)
            inc(vector.tensor_tensor_scan(dsl(7), dsl(6), dsl(6), 0.0,
                                          OP.add, OP.bypass), va, 1)

        for jh in range(2):
            @blk.tensor
            def _(tensor, jh=jh):
                wait(tensor, va)
                wait(tensor, sb)
                inc(tensor.transpose(ptf[:, 0:64], dsl(7)[:, jh * 128:jh * 128 + 128],
                                     ctx.ident[:64, :64]), mm, 1)

            @blk.scalar
            def _(scalar, jh=jh):
                wait(scalar, mm)
                inc(scalar.mul(cumt[:, jh * 64:jh * 64 + 64], ptf[:, 0:64], -1.0),
                    sb, 1)


        @blk.scalar
        def _(scalar, s=s):
            wait(scalar, va)
            wait(scalar, l2)
            dinc(scalar.dma_start(out=ctile, in_=dsl(7)), l2)
            ct_done[s] = l2.n
            inc(scalar.activation(dsl(1), dsl(7), AF.Exp, scale=np.float32(-1.0),
                                  bias=dsl(7)[:, 255:256]), sa, 1)
            wait(scalar, sa)
            inc(scalar.activation(dsl(3), dsl(7), AF.Exp), sa, 1)

        @blk.vector
        def _(vector):
            wait(vector, sa)
            inc(vector.tensor_copy(fbf[:], dsl(5)), va, 1)
            wait(vector, va)
            inc(vector.tensor_copy(ddbf[:], dsl(1)), va, 1)

        for jh in range(2):
            @blk.tensor
            def _(tensor, jh=jh):
                wait(tensor, va)
                wait(tensor, ep)
                inc(tensor.transpose(ptf[:, 0:64], dsl(3)[:, jh * 128:jh * 128 + 128],
                                     ctx.ident[:64, :64]), mm, 1)

            @blk.scalar
            def _(scalar, jh=jh):
                wait(scalar, mm)
                inc(scalar.copy(sdt[:, jh * 64:jh * 64 + 64], ptf[:, 0:64]), ep, 1)
        @blk.scalar
        def _(scalar, s=s):
            wait(scalar, ep)
            for jh in range(2):
                wait(scalar, l2)
                dinc(scalar.dma_start(
                    out=sd_sp[s * 256 + jh * 128:s * 256 + jh * 128 + 128, :],
                    in_=sdt[:, jh * 64:jh * 64 + 64]), l2)

        for ci in range(32):
            @blk.tensor
            def _(tensor, ci=ci):
                wait(tensor, va)
                wait(tensor, sa)
                if ci >= 1:
                    wait(tensor, vb)
                inc(tensor.matmul(p2[:], rsb[:, ci * 128:(ci + 1) * 128], fbf[:],
                                  start=True, stop=True), mm, 1)

            @blk.vector
            def _(vector, ci=ci):
                wait(vector, mm)
                inc(vector.tensor_tensor(xdfm[:], xcs[:, ci * 256:(ci + 1) * 256],
                                         p2[:], OP.mult), vb, 1)

            for jh in range(2):
                @blk.tensor
                def _(tensor, ci=ci, jh=jh):
                    wait(tensor, vb)
                    wait(tensor, ep)
                    inc(tensor.transpose(pt[:], xdfm[:, jh * 128:jh * 128 + 128],
                                         ctx.ident_bf[:]), mm, 1)

                @blk.scalar
                def _(scalar, ci=ci, jh=jh):
                    wait(scalar, mm)
                    inc(scalar.copy(xdt[:, jh * 4096 + ci * 128:jh * 4096 + ci * 128 + 128],
                                    pt[:]), ep, 1)

        for jh in range(2):
            @blk.tensor
            def _(tensor, jh=jh):
                wait(tensor, ep)
                wait(tensor, sa)
                inc(tensor.transpose(pt[:], xcs[:, 32 * 256 + jh * 128:32 * 256 + jh * 128 + 128],
                                     ctx.ident_bf[:]), mm, 1)

            @blk.scalar
            def _(scalar, jh=jh):
                wait(scalar, mm)
                inc(scalar.copy(btk[:, jh * 128:(jh + 1) * 128], pt[:]), ep, 1)

        @blk.scalar
        def _(scalar, s=s):
            wait(scalar, sa)
            wait(scalar, l2)
            dinc(scalar.dma_start(out=c_sp[s * 128:(s + 1) * 128, :],
                                  in_=xcs[:, 33 * 256:34 * 256]), l2)

        for jh in range(2):
            @blk.tensor
            def _(tensor, jh=jh):
                wait(tensor, sa)
                if jh >= 1:
                    wait(tensor, ep)
                inc(tensor.matmul(p3[:], xcs[:, 32 * 256 + jh * 128:32 * 256 + jh * 128 + 128],
                                  xcs[:, 33 * 256:34 * 256],
                                  start=True, stop=True), mm, 1)

            @blk.scalar
            def _(scalar, jh=jh):
                wait(scalar, mm)
                inc(scalar.copy(gbuf[:, jh * 256:(jh + 1) * 256], p3[:]), ep, 1)

        for ti in range(8):
            for jh in range(2):
                @blk.tensor
                def _(tensor, ti=ti, jh=jh):
                    wait(tensor, ep)
                    wait(tensor, vb)
                    inc(tensor.matmul(
                        p4[:], ddbf[:, jh * 128:(jh + 1) * 128],
                        rfr2[:, ti * 512:(ti + 1) * 512],
                        start=True, stop=True), mm, 1)

                @blk.vector
                def _(vector, ti=ti, jh=jh):
                    wait(vector, mm)
                    if ti + jh > 0:
                        wait(vector, ctx.S["mm"])
                    inc(vector.tensor_tensor(
                        sstg, xdt[:, jh * 4096 + ti * 512:jh * 4096 + (ti + 1) * 512],
                        p4[:], OP.mult), vb, 1)

                @blk.tensor
                def _(tensor, ti=ti, jh=jh, s=s):
                    wait(tensor, vb)
                    if ti >= 2 and jh == 0:
                        wait(tensor, sb)
                    inc(tensor.matmul(
                        p1[ti % 2][:], btk[:, jh * 128:(jh + 1) * 128],
                        sstg, start=(jh == 0), stop=(jh == 1)), mm, 1)

            @blk.scalar
            def _(scalar, ti=ti, s=s):
                wait(scalar, mm)
                wait(scalar, so[0])
                inc(scalar.copy(stg[0][:], p1[ti % 2][:]), sb, 1)
                wait(scalar, sb)
                dinc(scalar.dma_start(
                    out=spack[s * 129:s * 129 + 128, ti * 512:(ti + 1) * 512],
                    in_=stg[0][:]), so[0])

        @blk.tensor
        def _(tensor):
            wait(tensor, sb)
            inc(tensor.transpose(ptf[0:1, 0:64], dsl(7)[:, 255:256],
                                 ctx.ident[:64, :64]), mm, 1)

        @blk.scalar
        def _(scalar, s=s):
            wait(scalar, mm)
            wait(scalar, so[0])
            inc(scalar.copy(stg[0][0:1, 0:64].bitcast(BF16) if False
                            else stg[0][0:1, 0:64], ptf[0:1, 0:64]), sb, 1)
            wait(scalar, sb)
            wait(scalar, l2)
            dinc(scalar.dma_start(out=spack[s * 129 + 128:s * 129 + 129, 0:64],
                                  in_=stg[0][0:1, 0:64]), l2)

        _sc.__exit__(None, None, None)
        _sc = ctx.nc.named_scope(f"heads{s}"); _sc.__enter__()
        for h in range(64):
            if h % 2 == 0:
                @blk.scalar
                def _(scalar, s=s, h=h):
                    if s + h > 0:
                        wait(scalar, mm)
                    wait(scalar, l2)
                    wait(scalar, l2, ct_done[s])
                    dinc(scalar.dma_start(
                        out=crow[:],
                        in_=ctile.rearrange("a b -> (a b)")[h * 256:(h + 2) * 256]
                            .rearrange("(o n) -> o n", o=1)), l2)

            for jh in range(2):
                @blk.tensor
                def _(tensor, h=h, jh=jh):
                    wait(tensor, l2)
                    wait(tensor, vb)
                    wait(tensor, sa)
                    tensor.matmul(p2[:] if jh == 0 else p3[:], ctx.ones_row[:],
                                  crow[:, (h % 2) * 256:(h % 2) * 256 + 256],
                                  start=True, stop=False)
                    inc(tensor.matmul(p2[:] if jh == 0 else p3[:], ctx.ident_bf[:],
                                      mtri[:, jh * 256:(jh + 1) * 256],
                                      start=False, stop=True), mm, 1)

                @blk.scalar
                def _(scalar, h=h, jh=jh):
                    wait(scalar, mm)
                    wait(scalar, vb)
                    inc(scalar.activation(ebuf[:],
                                          p2[:] if jh == 0 else p3[:], AF.Exp,
                                          bias=cumt[:, jh * 64 + h:jh * 64 + h + 1]),
                        sa, 1)

                @blk.vector
                def _(vector, h=h, jh=jh):
                    wait(vector, sa)
                    inc(vector.tensor_tensor(ebuf[:],
                                             gbuf[:, jh * 256:(jh + 1) * 256],
                                             ebuf[:], OP.mult), vb, 1)

                @blk.tensor
                def _(tensor, h=h, jh=jh):
                    wait(tensor, vb)
                    wait(tensor, sa)
                    if jh == 0 and h == 0:
                        wait(tensor, sb)
                    for ih in range(2):
                        last = tensor.matmul(
                            p1[0][:, ih * 64:ih * 64 + 64],
                            ebuf[:, ih * 128:ih * 128 + 128],
                            xdt[:, jh * 4096 + h * 64:jh * 4096 + h * 64 + 64],
                            start=(jh == 0 and ih == 0),
                            stop=(jh == 1 and ih == 1))
                    inc(last, mm, 1)

            for ih in range(2):
                @blk.tensor
                def _(tensor, h=h, ih=ih):
                    wait(tensor, vb)
                    ci, half = divmod(h, 2)
                    inc(tensor.transpose(
                        pt[:, 0:64],
                        xcs[half * 64:half * 64 + 64,
                            ci * 256 + ih * 128:ci * 256 + ih * 128 + 128],
                        ctx.ident_bf[half * 64:half * 64 + 64,
                                     half * 64:half * 64 + 64]), mm, 1)

                @blk.scalar
                def _(scalar, h=h, ih=ih):
                    wait(scalar, mm)
                    wait(scalar, vb)
                    inc(scalar.copy(ytmp[:], p1[0][:, ih * 64:ih * 64 + 64]), sa, 1)

                @blk.vector
                def _(vector, h=h, ih=ih, s=s):
                    wait(vector, mm)
                    wait(vector, sa)
                    if h == 0 and ih == 0:
                        wait(vector, so[0])
                        wait(vector, so[1])
                    inc(vector.scalar_tensor_tensor(
                        ybuf[:, ih * 4096 + h * 64:ih * 4096 + h * 64 + 64],
                        pt[:, 0:64], dbc[:, h:h + 1], ytmp[:],
                        OP.mult, OP.add), vb, 1)

        @blk.scalar
        def _(scalar, s=s):
            wait(scalar, vb)
            for ih in range(2):
                dinc(scalar.dma_start(
                    out=yp_sp[s * 256 + ih * 128:s * 256 + ih * 128 + 128, :],
                    in_=ybuf[:, ih * 4096:(ih + 1) * 4096]), so[ih])
        _sc.__exit__(None, None, None)

    @blk.gpsimd
    def _(gpsimd):
        wait(gpsimd, ep)
        wait(gpsimd, so[0])
        wait(gpsimd, so[1])
        wait(gpsimd, l2)
        inc(gpsimd.engine_nop(), ctx.S["cc"], 1)

    ctx.pop()
    return [(ctx.S["cc"], ctx.S["cc"].n)]

def emit_mamba_m3(ctx, variant, resid, sag, pmb, zmk, rfree, cperm, gate_sp,
                  yp_sp, c_sp, sd_sp, wo_scr, nw_sb, bo_sb, dep=(), xscr=None):
    cfg = mamba_cfg(variant)
    blk = ctx.block
    mm, ep, va, vb, sa, sb, ga = (ctx.S[k] for k in
                                  ("mm", "ep", "va", "vb", "sa", "sb", "ga"))
    lw = [ctx.S["lda"], ctx.S["ldb"]]
    l2 = ctx.S["ld2"]
    ld = ctx.S["ld"]
    ctx.push()
    alm = ctx.sbuf("m3_alm", [16, 64], BF16)
    alt = ctx.sbuf("m3_alt", [64, 40], F32)
    wrep = ctx.sbuf("m3_wrep", [64, 128], BF16)
    o64 = ctx.sbuf("m3_o64", [64, 128], F32)
    wbc = ctx.sbuf("m3_wbc", [128, 4096], BF16)
    sjb = [ctx.sbuf("m3_sj", [128, 4096], BF16) for _ in range(2)]
    P = ctx.sbuf("m3_P", [128, 4096], F32)
    Pb = ctx.sbuf("m3_Pb", [128, 4096], BF16)
    tmb = ctx.sbuf("m3_tmb", [128, 4096], F32)
    ytk = [ctx.sbuf("m3_ytk", [128, 4096], BF16) for _ in range(2)]
    sdc = ctx.sbuf("m3_sdc", [128, 2 * 64], BF16)
    cfm = ctx.sbuf("m3_cfm", [128, 256], BF16)
    ycf = ctx.sbuf("m3_ycf", [128, 256], BF16)
    gfm = ctx.sbuf("m3_gfm", [128, 256], BF16)
    gsil = ctx.sbuf("m3_gsil", [128, 256], BF16)
    sqf = ctx.sbuf("m3_sqf", [128, 256], F32)
    ynrm = (ctx.sbuf("m3_ynrm", [128, 32 * 256], BF16) if xscr is None
            else xscr[:, 0:32 * 256])
    rowst = ctx.sbuf("m3_rows", [1, 512], F32)
    rbc = ctx.sbuf("m3_rbc", [128, 256], F32)
    slab = [(xscr[:, 8192:12288] if xscr is not None
             else ctx.sbuf("m3_sl", [128, 32 * 128], BF16)),
            ctx.sbuf("m3_sl", [128, 32 * 128], BF16)]
    pA = ctx.psum("m3_pA", [128, 512])
    pB = [ctx.psum("m3_pB", [128, 256]) for _ in range(2)]
    pt = ctx.psum("m3_pt", [128, 128], BF16)
    ptf = ctx.psum("m3_ptf", [128, 128])
    pr = ctx.psum("m3_pr", [1, 512])
    lwb = [x.n for x in lw]
    w4o = wo_scr.rearrange("k (m kc c) -> m k kc c", kc=NIC, c=128)
    _sc3 = ctx.nc.named_scope("m3pre"); _sc3.__enter__()

    @blk.sync
    def _(sync):
        for sem, v in dep:
            wait(sync, sem, v)
        wait(sync, ld)
        dinc(sync.dma_start(
            out=alm[:],
            in_=sag.rearrange("(rc n) d -> rc n d", n=129)[:, 128, 0:64]), ld)

    @blk.tensor
    def _(tensor):
        for sem, v in dep:
            wait(tensor, sem, v)
        wait(tensor, ld)
        wait(tensor, ctx.S["va"], ctx.consts_va)
        inc(tensor.matmul(ptf[0:16, 0:64], cperm[:], alm[:],
                          start=True, stop=True), mm, 1)

    @blk.scalar
    def _(scalar):
        wait(scalar, mm)
        inc(scalar.copy(wrep[0:16, 0:64].bitcast(F32) if False
                        else wrep[0:16, 0:64], ptf[0:16, 0:64]), sb, 1)

    @blk.tensor
    def _(tensor):
        wait(tensor, sb)
        inc(tensor.transpose(pt[0:64, 0:16], wrep[0:16, 0:64],
                             ctx.ident_bf[:16, :16]), mm, 1)

    @blk.scalar
    def _(scalar):
        wait(scalar, mm)
        inc(scalar.copy(alt[:, 0:16], pt[0:64, 0:16]), sb, 1)

    @blk.vector
    def _(vector):
        wait(vector, sb)
        inc(vector.tensor_tensor_scan(alt[:, 16:32], alt[:, 0:16], alt[:, 0:16],
                                      0.0, OP.add, OP.bypass), va, 1)
        wait(vector, va)
        inc(vector.memset(o64[:], 1.0), va, 1)

    for s in range(4):
        nj = 4 * (s + 1)

        @blk.vector
        def _(vector, s=s):
            wait(vector, va)
            if s >= 1:
                wait(vector, vb)
            inc(vector.tensor_tensor(alt[:, 32 + 0:32 + 16] if False
                                     else tmb[0:64, 0:16],
                                     alt[:, 16:32], zmk[:, s * 16:(s + 1) * 16],
                                     OP.mult), va, 1)
            wait(vector, va)
            inc(vector.tensor_reduce(alt[:, 32:33], tmb[0:64, 0:16],
                                     axis=mybir.AxisListType.X, op=OP.add), va, 1)
            wait(vector, va)
            inc(vector.memset(P[:], 0.0), va, 1)

        for m in range(nj):
            @blk.sync
            def _(sync, s=s, m=m):
                if s + m > 0:
                    wait(sync, vb)
                else:
                    wait(sync, lw[m % 2])
                r, sl = BLK_OWNER[m]
                dinc(sync.dma_start(out=sjb[m % 2][:],
                                    in_=sag[r * 516 + sl * 129:r * 516 + sl * 129 + 128, :]),
                     lw[m % 2])

            @blk.vector
            def _(vector, s=s, m=m):
                wait(vector, va)
                if m >= 1:
                    wait(vector, vb)
                inc(vector.tensor_tensor(alt[:, 33:34], alt[:, 32:33],
                                         alt[:, 16 + m:17 + m], OP.subtract),
                    vb, 1)

            @blk.scalar
            def _(scalar, s=s, m=m):
                wait(scalar, vb)
                inc(scalar.activation(alt[:, 34:35], alt[:, 33:34], AF.Exp,
                                      bias=pmb[:, s * 16 + m:s * 16 + m + 1]),
                    sa, 1)

            @blk.vector
            def _(vector, s=s, m=m):
                wait(vector, sa)
                inc(vector.tensor_scalar(wrep[:], o64[:], alt[:, 34:35], None,
                                         OP.mult), vb, 1)

            for t8 in range(8):
                @blk.tensor
                def _(tensor, t8=t8):
                    wait(tensor, vb)
                    if t8 >= 1:
                        wait(tensor, ep)
                    inc(tensor.matmul(pA[:], wrep[:],
                                      rfree[:, t8 * 512:(t8 + 1) * 512],
                                      start=True, stop=True), mm, 1)

                @blk.scalar
                def _(scalar, t8=t8):
                    wait(scalar, mm)
                    inc(scalar.copy(wbc[:, t8 * 512:(t8 + 1) * 512], pA[:]), ep, 1)

            @blk.vector
            def _(vector, s=s, m=m):
                wait(vector, ep)
                wait(vector, lw[m % 2], lwb[m % 2] + 16 * ((s * (s + 1) * 2 + m) // 2 + 1)
                     if False else None)
                wait(vector, lw[m % 2])
                inc(vector.tensor_tensor(tmb[:], sjb[m % 2][:], wbc[:], OP.mult),
                    vb, 1)
                wait(vector, vb)
                inc(vector.tensor_tensor(P[:], P[:], tmb[:], OP.add), vb, 1)

        @blk.vector
        def _(vector, s=s):
            wait(vector, vb)
            inc(vector.tensor_copy(Pb[:], P[:]), vb, 1)

        @blk.sync
        def _(sync, s=s):
            wait(sync, vb)
            if s >= 1:
                wait(sync, ga)
            for ih in range(2):
                wait(sync, ld)
                dinc(sync.dma_start(
                    out=ytk[ih][:],
                    in_=yp_sp[s * 256 + ih * 128:s * 256 + ih * 128 + 128, :]), ld)
                wait(sync, ld)
                dinc(sync.dma_start(
                    out=sdc[:, ih * 64:(ih + 1) * 64],
                    in_=sd_sp[s * 256 + ih * 128:s * 256 + ih * 128 + 128, :]), ld)
            wait(sync, ld)
            dinc(sync.dma_start(out=cfm[:], in_=c_sp[s * 128:(s + 1) * 128, :]), ld)

        for ih in range(2):
            for h in range(64):
                @blk.tensor
                def _(tensor, s=s, ih=ih, h=h):
                    wait(tensor, ld)
                    wait(tensor, vb)
                    if ih + h > 0:
                        wait(tensor, ga)
                    inc(tensor.matmul(pB[(ih * 64 + h) % 2][:, 0:64],
                                      cfm[:, ih * 128:(ih + 1) * 128],
                                      Pb[:, h * 64:(h + 1) * 64],
                                      start=True, stop=True), mm, 1)

                @blk.vector
                def _(vector, s=s, ih=ih, h=h):
                    wait(vector, mm)
                    inc(vector.scalar_tensor_tensor(
                        ytk[ih][:, h * 64:(h + 1) * 64],
                        pB[(ih * 64 + h) % 2][:, 0:64], sdc[:, ih * 64 + h:ih * 64 + h + 1],
                        ytk[ih][:, h * 64:(h + 1) * 64], OP.mult, OP.add), ga, 1)

        _sc3.__exit__(None, None, None)
        _sc3 = ctx.nc.named_scope("m3norm"); _sc3.__enter__()
        for ci in range(NIC):
            @blk.sync
            def _(sync, s=s, ci=ci):
                wait(sync, ga)
                wait(sync, mm)
                if ci >= 1:
                    wait(sync, sa)
                dinc(sync.dma_start(
                    out=gfm[:],
                    in_=gate_sp[ci * 128:(ci + 1) * 128, s * 256:(s + 1) * 256]),
                    l2)

            for ih in range(2):
                @blk.tensor
                def _(tensor, s=s, ci=ci, ih=ih):
                    wait(tensor, ga)
                    wait(tensor, ep)
                    inc(tensor.transpose(pt[:], ytk[ih][:, ci * 128:(ci + 1) * 128],
                                         ctx.ident_bf[:]), mm, 1)

                @blk.scalar
                def _(scalar, ci=ci, ih=ih):
                    wait(scalar, mm)
                    if ih == 0:
                        wait(scalar, vb)
                    inc(scalar.copy(ycf[:, ih * 128:(ih + 1) * 128], pt[:]), ep, 1)

            @blk.scalar
            def _(scalar, s=s, ci=ci):
                wait(scalar, l2)
                inc(scalar.activation(gsil[:], gfm[:], AF.Silu), sa, 1)

            @blk.vector
            def _(vector, s=s, ci=ci):
                wait(vector, ep)
                wait(vector, sa)
                if cfg["gate_first"]:
                    inc(vector.tensor_tensor(ynrm[:, ci * 256:(ci + 1) * 256],
                                             ycf[:], gsil[:], OP.mult), vb, 1)
                else:
                    inc(vector.tensor_copy(ynrm[:, ci * 256:(ci + 1) * 256],
                                           ycf[:]), vb, 1)
                wait(vector, vb)
                inc(vector.tensor_tensor(sqf[:], ynrm[:, ci * 256:(ci + 1) * 256],
                                         ynrm[:, ci * 256:(ci + 1) * 256],
                                         OP.mult), vb, 1)

            @blk.tensor
            def _(tensor, s=s, ci=ci):
                wait(tensor, vb)
                inc(tensor.matmul(pr[:, 0:256], ctx.ones_col[:], sqf[:],
                                  start=(ci == 0), stop=(ci == NIC - 1)), mm, 1)

        @blk.vector
        def _(vector, s=s):
            wait(vector, mm)
            inc(vector.tensor_scalar(rowst[:, 0:256], pr[:, 0:256], 1.0 / 4096.0,
                                     None, OP.mult), vb, 1)

        @blk.scalar
        def _(scalar, s=s):
            wait(scalar, vb)
            inc(scalar.activation(rowst[:, 256:512], rowst[:, 0:256], AF.Sqrt,
                                  bias=np.float32(EPS)), sa, 1)

        @blk.vector
        def _(vector, s=s):
            wait(vector, sa)
            inc(vector.reciprocal(rowst[:, 256:512], rowst[:, 256:512]), vb, 1)

        @blk.tensor
        def _(tensor, s=s):
            wait(tensor, vb)
            inc(tensor.matmul(pA[:, 0:256], ctx.ones_row[:], rowst[:, 256:512],
                              start=True, stop=True), mm, 1)

        @blk.scalar
        def _(scalar, s=s):
            wait(scalar, mm)
            inc(scalar.copy(rbc[:], pA[:, 0:256]), ep, 1)

        for ci in range(NIC):
            if not cfg["gate_first"]:
                @blk.sync
                def _(sync, s=s, ci=ci):
                    wait(sync, ep)
                    wait(sync, sa)
                    dinc(sync.dma_start(
                        out=gfm[:],
                        in_=gate_sp[ci * 128:(ci + 1) * 128, s * 256:(s + 1) * 256]),
                        l2)

                @blk.scalar
                def _(scalar, s=s, ci=ci):
                    wait(scalar, l2)
                    wait(scalar, vb)
                    inc(scalar.activation(gsil[:], gfm[:], AF.Silu), sa, 1)

            @blk.vector
            def _(vector, s=s, ci=ci):
                wait(vector, ep)
                c0 = ci * 256
                if cfg["gate_first"]:
                    inc(vector.scalar_tensor_tensor(
                        ynrm[:, c0:c0 + 256], ynrm[:, c0:c0 + 256],
                        nw_sb[:, ci:ci + 1], rbc[:], OP.mult, OP.mult), vb, 1)
                else:
                    wait(vector, sa)
                    inc(vector.scalar_tensor_tensor(
                        ynrm[:, c0:c0 + 256], ynrm[:, c0:c0 + 256],
                        nw_sb[:, ci:ci + 1], rbc[:], OP.mult, OP.mult), vb, 1)
                    wait(vector, vb)
                    inc(vector.tensor_tensor(ynrm[:, c0:c0 + 256],
                                             ynrm[:, c0:c0 + 256], gsil[:],
                                             OP.mult), vb, 1)

        _sc3.__exit__(None, None, None)
        _sc3 = ctx.nc.named_scope("m3out"); _sc3.__enter__()
        for mo in range(KC):
            u3 = s * KC + mo

            @blk.sync
            def _(sync, s=s, mo=mo, u3=u3):
                wait(sync, vb)
                if u3 < 2:
                    wait(sync, lw[u3 % 2])
                else:
                    wait(sync, mm)
                dinc(sync.dma_start(
                    out=slab[u3 % 2][:].rearrange("k (kc c) -> k kc c", c=128),
                    in_=w4o[mo]), lw[u3 % 2])

            @blk.tensor
            def _(tensor, s=s, mo=mo, u3=u3):
                wait(tensor, lw[u3 % 2])
                wait(tensor, vb)
                if u3 >= 2:
                    wait(tensor, ga)
                for kc in range(NIC):
                    last = tensor.matmul(
                        pB[u3 % 2][:], slab[u3 % 2][:, kc * 128:(kc + 1) * 128],
                        ynrm[:, kc * 256:(kc + 1) * 256],
                        start=(kc == 0), stop=(kc == NIC - 1))
                inc(last, mm, 1)

            @blk.vector
            def _(vector, s=s, mo=mo, u3=u3):
                wait(vector, mm)
                r = resid[:, mo * 1024 + s * 256:mo * 1024 + s * 256 + 256]
                sc = bo_sb[:, mo:mo + 1] if bo_sb is not None else 0.0
                inc(vector.scalar_tensor_tensor(r, pB[u3 % 2][:], sc, r,
                                                OP.add, OP.add), ga, 1)

    _sc3.__exit__(None, None, None)
    ctx.pop()
    return [(ctx.S["ga"], ctx.S["ga"].n)]

def host_hsel(core):
    g = core % GROUP
    sel = np.zeros((48, 12), np.float32)
    for s, b in enumerate(blocks_of(g)):
        if b == 0:
            continue
        r, sl = BLK_OWNER[b - 1]
        for i in range(3):
            sel[r * 12 + sl * 3 + i, s * 3 + i] = 1.0
    return sel.astype(BF)

def host_pmb(core):
    g = core % GROUP
    m = np.full((64, 64), -1e30, np.float32)
    for s, b in enumerate(blocks_of(g)):
        for j in range(16):
            if j < b:
                m[:, s * 16 + j] = 0.0
    return m

def host_zmk(core):
    g = core % GROUP
    m = np.zeros((64, 64), np.float32)
    for s, b in enumerate(blocks_of(g)):
        if b >= 1:
            m[:, s * 16 + (b - 1)] = 1.0
    return m

def host_rfree():
    r = np.zeros((64, 4096), np.float32)
    for h in range(64):
        r[h, h * 64:(h + 1) * 64] = 1.0
    return r.astype(BF)

def host_cperm():
    p = np.zeros((16, 16), np.float32)
    for m in range(16):
        r, sl = BLK_OWNER[m]
        p[r * 4 + sl, m] = 1.0
    return p.astype(BF)

def host_rall():
    r = np.zeros((64, 32 * 128), np.float32)
    for ci in range(32):
        for q in range(128):
            r[2 * ci + q // 64, ci * 128 + q] = 1.0
    return r.astype(BF)

def host_mtri():
    m = np.zeros((128, 2 * 256), np.float32)
    for jh in range(2):
        j = jh * 128 + np.arange(128)[:, None]
        i = np.arange(256)[None, :]
        m[:, jh * 256:(jh + 1) * 256] = np.where(j > i, -1e30, 0.0)
    return m.astype(BF)

def build_m1_launch():
    L = Launch()
    ctx = L.ctx
    rin = L.input("rin", [D, 1024], F32)
    ident = L.input("ident", [128, 128], F32)
    hpack = L.output("hpack", [12, D], BF16)
    emit_consts(ctx, ident)
    resid, dep_r = emit_resid_in(ctx, rin)
    xhat = ctx.sbuf("xhat", [128, KC * XW], BF16)
    ga_ln = emit_ln(ctx, resid, xhat, "mamba", dep=[dep_r])
    fin = emit_mamba_m1(ctx, xhat, hpack, dep=[(ctx.S["ga"], ga_ln)])

    @ctx.block.sync
    def _(sync):
        for sem, v in fin:
            wait(sync, sem, v)
    return L.finish()

def build_m2_launch(variant):
    cfg = mamba_cfg(variant)
    L = Launch()
    ctx = L.ctx
    rin = L.input("rin", [D, 1024], F32)
    wi = L.input("wi", [128, MMC * KC * 128], F32)
    cw = L.input("cwt", [4, 4352], F32)
    alog = L.input("alog", [128], F32)
    dtb = L.input("dtb", [128], F32)
    dvec = L.input("dvec", [64], F32)
    hgat = L.input("hgat", [48, D], BF16)
    hsel = L.input("hsel", [48, 12], BF16)
    rall = L.input("rall", [64, 32 * 128], BF16)
    mtri = L.input("mtri", [128, 512], BF16)
    ident = L.input("ident", [128, 128], F32)
    gate_sp = L.output("gate_sp", [4096, 1024], BF16)
    yp_sp = L.output("yp_sp", [1024, 4096], BF16)
    c_sp = L.output("c_sp", [512, 256], BF16)
    sd_sp = L.output("sd_sp", [1024, 64], BF16)
    spack = L.output("spack", [4 * 129, 4096], BF16)
    wiscr = ctx.dram("wiscr", [128, MMC * KC * 128], BF16)
    ctile = ctx.dram("ctile", [64, 256], F32)
    emit_consts(ctx, ident)
    resid, dep_r = emit_resid_in(ctx, rin)
    xhat = ctx.sbuf("xhat", [128, KC * XW], BF16)
    ga_ln = emit_ln(ctx, resid, xhat, "mamba", dep=[dep_r])
    dep = [(ctx.S["ga"], ga_ln)]
    if cfg["in_bias"]:
        binv = L.input("bin", [MMC * 128], F32)
        bin_sb, _ = emit_load_colvec(ctx, binv, MMC, "bin")
    else:
        bin_sb = None
    cwl = []
    for k in range(4):
        c_sb, dep_c = emit_load_colvec(ctx, cw[k], 34, f"cw{k}")
        cwl.append(c_sb)
    cb_sb = None
    if cfg["conv_bias"]:
        cbv = L.input("cb", [4352], F32)
        cb_sb, _ = emit_load_colvec(ctx, cbv, 34, "cb")
    acol_in, _ = emit_load_colvec(ctx, alog, 1, "alogc")
    dtb_sb, _ = emit_load_colvec(ctx, dtb, 1, "dtbc")
    drow, dep_d = emit_load_row(ctx, dvec, 64, "drow", F32)
    rsb, dep_rs = emit_load_tile(ctx, rall, [64, 32 * 128], "rsb", BF16)
    rfr2 = rsb
    mtr = ctx._ln["coef"][:].bitcast(BF16)[:, 0:512]

    @ctx.block.sync
    def _(sync):
        wait(sync, ctx.S["ga"], ga_ln)
        wait(sync, ctx.S["ld"])
        dinc(sync.dma_start(out=mtr, in_=mtri), ctx.S["ld"])
    dep_mt = (ctx.S["ld"], ctx.S["ld"].n)
    dep += [dep_d, dep_rs, dep_mt]
    acol = ctx.sbuf("acol", [128, 1], F32)
    dbc = ctx.sbuf("dbc", [128, 64], F32)
    pdb = ctx.psum("pdb", [128, 64])

    @ctx.block.scalar
    def _(scalar):
        wait(scalar, ctx.S["sb"])
        inc(scalar.activation(acol[:], acol_in[:, 0:1], AF.Exp,
                              scale=np.float32(1.0)), ctx.S["sa"], 1)

    @ctx.block.vector
    def _(vector):
        wait(vector, ctx.S["sa"])
        inc(vector.tensor_scalar(acol[:], acol[:], -1.0, None, OP.mult),
            ctx.S["va"], 1)

    @ctx.block.tensor
    def _(tensor):
        for sem, v in [dep_d]:
            wait(tensor, sem, v)
        wait(tensor, ctx.S["va"], ctx.consts_va)
        wait(tensor, ctx.S["sb"])
        inc(tensor.matmul(pdb[:], ctx.ones_row[:], drow[:],
                          start=True, stop=True), ctx.S["mm"], 1)

    @ctx.block.scalar
    def _(scalar):
        wait(scalar, ctx.S["mm"])
        inc(scalar.copy(dbc[:], pdb[:]), ctx.S["sb"], 1)
    dep += [(ctx.S["va"], ctx.S["va"].n), (ctx.S["sb"], ctx.S["sb"].n)]

    cwp = ctx.sbuf("cwp", [128, 136], F32)

    @ctx.block.vector
    def _(vector):
        wait(vector, ctx.S["sb"])
        for k in range(4):
            ins = vector.tensor_copy(
                cwp[:].rearrange("p (kc t) -> p kc t", t=4)[:, :, k], cwl[k][:, 0:34])
        inc(ins, ctx.S["va"], 1)
    dep += [(ctx.S["va"], ctx.S["va"].n)]

    cv = emit_convert(ctx, wi, wiscr.ap(), KC, MMC, F32)
    dep += emit_halo_select(ctx, xhat, hgat, hsel, dep=list(dep) + cv + [(ctx.S['va'], ctx.S['va'].n), (ctx.S['ga'], ctx.S['ga'].n)])
    fin = emit_mamba_m2(ctx, variant, xhat, wiscr.ap(), bin_sb, cwp, cb_sb,
                        dtb_sb, acol, dbc, rsb, mtr, rfr2, gate_sp, yp_sp, c_sp,
                        sd_sp, spack, ctile.ap(), dep=dep + cv)

    @ctx.block.sync
    def _(sync):
        for sem, v in fin:
            wait(sync, sem, v)
    return L.finish()

def build_m3_launch(variant):
    cfg = mamba_cfg(variant)
    L = Launch()
    ctx = L.ctx
    rin = L.input("rin", [D, 1024], F32)
    wo = L.input("wo", [128, KC * NIC * 128], F32)
    nw = L.input("nw", [4096], F32)
    sag = L.input("sag", [4 * 516, 4096], BF16)
    gate_sp = L.input("gate_sp", [4096, 1024], BF16)
    yp_sp = L.input("yp_sp", [1024, 4096], BF16)
    c_sp = L.input("c_sp", [512, 256], BF16)
    sd_sp = L.input("sd_sp", [1024, 64], BF16)
    pmbi = L.input("pmb", [64, 64], F32)
    zmki = L.input("zmk", [64, 64], F32)
    rfri = L.input("rfree", [64, 4096], BF16)
    ident = L.input("ident", [128, 128], F32)
    rout = L.output("rout", [D, 1024], F32)
    woscr = ctx.dram("woscr", [128, KC * NIC * 128], BF16)
    emit_consts(ctx, ident)
    resid, dep_r = emit_resid_in(ctx, rin)
    dep = [dep_r]
    nw_sb, _ = emit_load_colvec(ctx, nw, NIC, "nw")
    if cfg["out_bias"]:
        bov = L.input("bo", [D], F32)
        bo_sb, dep_b = emit_load_colvec(ctx, bov, KC, "bo")
        dep.append(dep_b)
    else:
        bo_sb = None
    cpi = L.input("cperm", [16, 16], BF16)
    cpe, dep0 = emit_load_tile(ctx, cpi, [16, 16], "cpes", BF16)
    dep.append(dep0)
    pmb, dep1 = emit_load_tile(ctx, pmbi, [64, 64], "pmbs", F32)
    zmk, dep2 = emit_load_tile(ctx, zmki, [64, 64], "zmks", F32)
    rfr, dep3 = emit_load_tile(ctx, rfri, [64, 4096], "rfrs", BF16)
    dep += [dep1, dep2, dep3]
    cv = emit_convert(ctx, wo, woscr.ap(), NIC, KC, F32)
    fin = emit_mamba_m3(ctx, variant, resid, sag, pmb, zmk, rfr, cpe, gate_sp,
                        yp_sp, c_sp, sd_sp, woscr.ap(), nw_sb, bo_sb,
                        dep=dep + cv)
    emit_resid_out(ctx, resid, rout, dep=fin)
    return L.finish()


def _barrier(ctx, deps):
    deps = [(sem, v) for sem, v in deps]

    for name in ("sync", "tensor", "scalar", "vector", "gpsimd"):
        @getattr(ctx.block, name)
        def _(eng):
            for sem, v in deps:
                wait(eng, sem, v)

def _emit_ag(ctx, src, dst, deps):
    cc = ctx.S["cc"]

    @ctx.block.gpsimd
    def _(g):
        for sem, v in deps:
            wait(g, sem, v)
        g.collective_compute("AllGather", AluOpType.bypass,
                             replica_groups=[[0, 1, 2, 3], [4, 5, 6, 7]],
                             ins=[src.ap().opt()],
                             outs=[dst.ap().opt()]).then_inc(cc.h)
        cc.n += 1
    return (cc, cc.n)

def build_mega():
    L = Launch()
    ctx = L.ctx
    rin = L.input("rin", [D, 1024], F32)
    ident = L.input("ident", [128, 128], F32)
    cosd = L.input("cos", [128, 1024], F32)
    sind = L.input("sinm", [128, 1024], F32)
    masks = L.input("masks", [4, 2, 4, 2, 128, 512], BF16)
    eqc = L.input("eqc", [128, 256], BF16)
    wq_a = L.input("wq_a", [128, KC * KC * 128], F32)
    wk_a = L.input("wk_a", [128, 4 * KC * 128], F32)
    wv_a = L.input("wv_a", [128, 4 * KC * 128], F32)
    wo_a = L.input("wo_a", [128, KC * KC * 128], F32)
    gw = L.input("gwt", [128, 64 * KC * 128], U8)
    uw = L.input("uwt", [128, 64 * KC * 128], U8)
    dwq = L.input("dwt", [128, KC * 64 * 128], U8)
    gsc = L.input("gsc", [8192], F32)
    usc = L.input("usc", [8192], F32)
    dsc = L.input("dsc", [D], F32)
    wq_b = L.input("wq_b", [128, KC * KC * 128], F32)
    wk_b = L.input("wk_b", [128, KC * KC * 128], F32)
    wv_b = L.input("wv_b", [128, KC * KC * 128], F32)
    wo_b = L.input("wo_b", [128, KC * KC * 128], F32)
    bq_b = L.input("bq_b", [D], F32)
    bk_b = L.input("bk_b", [2048], F32)
    bv_b = L.input("bv_b", [2048], F32)
    bo_b = L.input("bo_b", [D], F32)
    w0 = L.input("w0t", [128, 64 * KC * 128], F32)
    w1 = L.input("w1t", [128, KC * 64 * 128], F32)
    b0 = L.input("b0", [8192], F32)
    b1 = L.input("b1", [D], F32)
    hseli = L.input("hsel", [48, 12], BF16)
    ralli = L.input("rall", [64, 32 * 128], BF16)
    mtrii = L.input("mtri", [128, 512], BF16)
    pmbi = L.input("pmb", [64, 64], F32)
    zmki = L.input("zmk", [64, 64], F32)
    cpi = L.input("cperm", [16, 16], BF16)
    mam_in = {}
    for v in ("a", "b"):
        mam_in[v] = dict(
            wi=L.input(f"wi_{v}", [128, MMC * KC * 128], F32),
            cw=L.input(f"cwt_{v}", [4, 4352], F32),
            alog=L.input(f"alog_{v}", [128], F32),
            dtb=L.input(f"dtb_{v}", [128], F32),
            dvec=L.input(f"dvec_{v}", [64], F32),
            nw=L.input(f"nw_{v}", [4096], F32),
            wo=L.input(f"wom_{v}", [128, KC * NIC * 128], F32))
    mam_in["a"]["bin"] = L.input("bin_a", [MMC * 128], F32)
    mam_in["a"]["bo"] = L.input("bom_a", [D], F32)
    mam_in["b"]["cb"] = L.input("cb_b", [4352], F32)
    rout = L.output("rout", [D, 1024], F32)

    emit_consts(ctx, ident)
    resid, dep_r = emit_resid_in(ctx, rin)
    xhat = ctx.sbuf("xhat", [128, KC * XW], BF16)

    def _ln_alloc():
        ctx._ln = dict(
            sq=[ctx.sbuf("ln_sq", [128, 256], F32) for _ in range(2)],
            stat=ctx.sbuf("ln_stat", [1, 1024], F32),
            coef=ctx.sbuf("ln_coef", [128, 512], F32),
            tmp=ctx.sbuf("ln_tmp", [128, 2048], F32))

    def _ln_free():
        del ctx._ln
    _barrier(ctx, [dep_r])

    def attn_phase(variant, wq, wk, wv, wo, bq=None, bk=None, bv=None, bo=None):
        cfg = attn_cfg(variant)
        kvd, vD, KR = cfg["kvd"], cfg["vD"], cfg["KR"]
        kpack_d = ctx.dram(f"kp_{variant}", [KR, 1024], BF16)
        vpack_d = ctx.dram(f"vp_{variant}", [1024, vD], BF16)
        kag_d = ctx.dram(f"kag_{variant}", [4 * KR, 1024], BF16)
        vag_d = ctx.dram(f"vag_{variant}", [4096, vD], BF16)
        kscr = ctx.dram(f"kscr_{variant}", [128, (kvd // 128) * KC * 128], BF16)
        vscr = ctx.dram(f"vscr_{variant}", [128, (vD // 128) * KC * 128], BF16)
        qscr = ctx.dram(f"qscr_{variant}", [128, KC * KC * 128], BF16)
        oscr = ctx.dram(f"oscr_{variant}", [128, KC * KC * 128], BF16)
        ctx.push()
        _ln_alloc()
        _scp = ctx.nc.named_scope("kv"); _scp.__enter__()
        ctx.push()
        ga_ln = emit_ln(ctx, resid, xhat, "plain", dep=[])
        dep = [(ctx.S["ga"], ga_ln)]
        if cfg["bias"]:
            bk_sb, _ = emit_load_colvec(ctx, bk, kvd // 128, "bk")
            bv_row, dep_bv = emit_load_row(ctx, bv, vD, "bv")
            dep.append(dep_bv)
            cos1 = sinm1 = None
        else:
            bk_sb = None
            bv_row = ctx.sbuf("bv0", [1, vD], BF16)
            cos1, dep_c = emit_load_tile(ctx, cosd, [128, 1024], "cos", F32)
            sinm1, dep_s = emit_load_tile(ctx, sind, [128, 1024], "sinm", F32)
            dep += [dep_c, dep_s]

            @ctx.block.vector
            def _(vector):
                inc(vector.memset(bv_row[:], 0.0), ctx.S["va"], 1)
            dep.append((ctx.S["va"], ctx.S["va"].n))
        cv1 = emit_convert(ctx, wk, kscr.ap(), KC, kvd // 128, F32)
        cv2 = emit_convert(ctx, wv, vscr.ap(), KC, vD // 128, F32)
        fin_kv = emit_attn_kv(ctx, variant, xhat, kscr.ap(), vscr.ap(), bk_sb,
                              bv_row, cos1, sinm1, kpack_d.ap(), vpack_d.ap(),
                              dep=dep + cv1 + cv2)
        agk = _emit_ag(ctx, kpack_d, kag_d, fin_kv)
        agv = _emit_ag(ctx, vpack_d, vag_d, [])
        _barrier(ctx, list(fin_kv) + [agk, agv, (ctx.S["ga"], ga_ln)])
        ctx.pop()
        _scp.__exit__(None, None, None)
        ctx.push()
        dep2 = []
        eq, dep_e = emit_load_tile(ctx, eqc, [128, 256], "eq", BF16)
        dep2.append(dep_e)
        if cfg["bias"]:
            bq_sb, _ = emit_load_colvec(ctx, bq, KC, "bq")
            bo_sb, dep_b = emit_load_colvec(ctx, bo, KC, "bo")
            dep2.append(dep_b)
            cos2 = sinm2 = None
        else:
            bq_sb = bo_sb = None
            cos2, dep_c2 = emit_load_tile(ctx, cosd, [128, 1024], "cos", F32)
            sinm2, dep_s2 = emit_load_tile(ctx, sind, [128, 1024], "sinm", F32)
            dep2 += [dep_c2, dep_s2]
        cv3 = emit_convert(ctx, wq, qscr.ap(), KC, KC, F32)
        cv4 = emit_convert(ctx, wo, oscr.ap(), KC, KC, F32)
        fin_core = emit_attn_core(ctx, variant, resid, xhat, qscr.ap(),
                                  oscr.ap(), bq_sb, bo_sb, cos2, sinm2,
                                  kag_d.ap(), vag_d.ap(), masks, eq,
                                  dep=dep2 + cv3 + cv4)
        _barrier(ctx, fin_core)
        ctx.pop()
        ctx.pop()
        _ln_free()

    gscr = ctx.dram("gscr", [128, 64 * KC * 128], BF16)
    uscr = ctx.dram("uscr", [128, 64 * KC * 128], BF16)
    dscr = ctx.dram("dscr", [128, KC * 64 * 128], BF16)

    def ffnq_phase(first):
        ctx.push()
        _ln_alloc()
        ga_ln = emit_ln(ctx, resid, xhat, "plain", dep=[])
        dep = [(ctx.S["ga"], ga_ln)]
        gs_sb, _ = emit_load_colvec(ctx, gsc, 64, "gs")
        us_sb, _ = emit_load_colvec(ctx, usc, 64, "us")
        ds_sb, dep_s = emit_load_colvec(ctx, dsc, KC, "ds")
        dep.append(dep_s)
        cvs = []
        if first:
            cvs += emit_convert(ctx, gw, gscr.ap(), KC, 64, U8)
            cvs += emit_convert(ctx, uw, uscr.ap(), KC, 64, U8)
            cvs += emit_convert(ctx, dwq, dscr.ap(), 64, KC, U8)
        q = emit_ffnq(ctx, resid, xhat, gscr.ap(), uscr.ap(), dscr.ap(),
                      gs_sb, us_sb, ds_sb, dep=dep + cvs)
        _barrier(ctx, [q["done"]])
        ctx.pop()
        _ln_free()

    w0scr = ctx.dram("w0scr", [128, 64 * KC * 128], BF16)
    w1scr = ctx.dram("w1scr", [128, KC * 64 * 128], BF16)

    def ffn16_phase(first):
        ctx.push()
        _ln_alloc()
        ga_ln = emit_ln(ctx, resid, xhat, "plain", dep=[])
        dep = [(ctx.S["ga"], ga_ln)]
        b0sb, _ = emit_load_colvec(ctx, b0, 64, "b0")
        b1sb, dep_b = emit_load_colvec(ctx, b1, KC, "b1")
        dep.append(dep_b)
        cvs = []
        if first:
            cvs += emit_convert(ctx, w0, w0scr.ap(), KC, 64, F32)
            cvs += emit_convert(ctx, w1, w1scr.ap(), 64, KC, F32)
        ffn = emit_ffn16(ctx, resid, xhat, w0scr.ap(), w1scr.ap(), b0sb, b1sb,
                         dep=dep + cvs)
        _barrier(ctx, [ffn["done"]])
        ctx.pop()
        _ln_free()
        return ffn

    def mamba_phase(variant):
        cfgm = mamba_cfg(variant)
        iv = mam_in[variant]
        hpack_d = ctx.dram(f"hp_{variant}", [12, D], BF16)
        hgat_d = ctx.dram(f"hg_{variant}", [48, D], BF16)
        spack_d = ctx.dram(f"spk_{variant}", [4 * 129, 4096], BF16)
        sag_d = ctx.dram(f"sag_{variant}", [4 * 516, 4096], BF16)
        gate_d = ctx.dram(f"gate_{variant}", [4096, 1024], BF16)
        yp_d = ctx.dram(f"yp_{variant}", [1024, 4096], BF16)
        csp_d = ctx.dram(f"csp_{variant}", [512, 256], BF16)
        sd_d = ctx.dram(f"sd_{variant}", [1024, 64], BF16)
        wiscr = ctx.dram(f"wiscr_{variant}", [128, MMC * KC * 128], BF16)
        ctile = ctx.dram(f"ctile_{variant}", [64, 256], F32)
        woscr = ctx.dram(f"woscr_{variant}", [128, KC * NIC * 128], BF16)
        ctx.push()
        _ln_alloc()
        _scp = ctx.nc.named_scope("m1"); _scp.__enter__()
        ctx.push()
        ga_ln = emit_ln(ctx, resid, xhat, "mamba", dep=[])
        fin1 = emit_mamba_m1(ctx, xhat, hpack_d.ap(),
                             dep=[(ctx.S["ga"], ga_ln)])
        ag1 = _emit_ag(ctx, hpack_d, hgat_d, fin1)
        _barrier(ctx, list(fin1) + [ag1, (ctx.S["ga"], ga_ln)])
        ctx.pop()
        _scp.__exit__(None, None, None)
        _scp = ctx.nc.named_scope("m2"); _scp.__enter__()
        ctx.push()
        dep = []
        if cfgm["in_bias"]:
            bin_sb, _ = emit_load_colvec(ctx, iv["bin"], MMC, "bin")
        else:
            bin_sb = None
        cwl = []
        for k in range(4):
            c_sb, _ = emit_load_colvec(ctx, iv["cw"][k], 34, f"cw{k}")
            cwl.append(c_sb)
        cb_sb = None
        if cfgm["conv_bias"]:
            cb_sb, _ = emit_load_colvec(ctx, iv["cb"], 34, "cb")
        acol_in, _ = emit_load_colvec(ctx, iv["alog"], 1, "alogc")
        dtb_sb, _ = emit_load_colvec(ctx, iv["dtb"], 1, "dtbc")
        drow, dep_d = emit_load_row(ctx, iv["dvec"], 64, "drow", F32)
        rsb, dep_rs = emit_load_tile(ctx, ralli, [64, 32 * 128], "rsb", BF16)
        rfr2 = rsb
        mtr = ctx._ln["coef"][:].bitcast(BF16)[:, 0:512]

        @ctx.block.sync
        def _(sync):
            wait(sync, ctx.S["ga"], ga_ln)
            wait(sync, ctx.S["ld"])
            dinc(sync.dma_start(out=mtr, in_=mtrii), ctx.S["ld"])
        dep_mt = (ctx.S["ld"], ctx.S["ld"].n)
        dep += [dep_d, dep_rs, dep_mt]
        acol = ctx.sbuf("acol", [128, 1], F32)
        dbc = ctx.sbuf("dbc", [128, 64], F32)
        pdb = ctx.psum("pdb", [128, 64])

        @ctx.block.scalar
        def _(scalar):
            wait(scalar, ctx.S["sb"])
            inc(scalar.activation(acol[:], acol_in[:, 0:1], AF.Exp,
                                  scale=np.float32(1.0)), ctx.S["sa"], 1)

        @ctx.block.vector
        def _(vector):
            wait(vector, ctx.S["sa"])
            inc(vector.tensor_scalar(acol[:], acol[:], -1.0, None, OP.mult),
                ctx.S["va"], 1)

        @ctx.block.tensor
        def _(tensor):
            for sem, v in [dep_d]:
                wait(tensor, sem, v)
            wait(tensor, ctx.S["va"], ctx.consts_va)
            wait(tensor, ctx.S["sb"])
            inc(tensor.matmul(pdb[:], ctx.ones_row[:], drow[:],
                              start=True, stop=True), ctx.S["mm"], 1)

        @ctx.block.scalar
        def _(scalar):
            wait(scalar, ctx.S["mm"])
            inc(scalar.copy(dbc[:], pdb[:]), ctx.S["sb"], 1)
        dep += [(ctx.S["va"], ctx.S["va"].n), (ctx.S["sb"], ctx.S["sb"].n)]

        cwp = ctx.sbuf("cwp", [128, 136], F32)

        @ctx.block.vector
        def _(vector):
            wait(vector, ctx.S["sb"])
            for k in range(4):
                ins = vector.tensor_copy(
                    cwp[:].rearrange("p (kc t) -> p kc t", t=4)[:, :, k],
                    cwl[k][:, 0:34])
            inc(ins, ctx.S["va"], 1)
        dep += [(ctx.S["va"], ctx.S["va"].n)]

        cv = emit_convert(ctx, iv["wi"], wiscr.ap(), KC, MMC, F32)
        dep += emit_halo_select(ctx, xhat, hgat_d.ap(), hseli,
                                dep=list(dep) + cv +
                                [(ctx.S["va"], ctx.S["va"].n),
                                 (ctx.S["ga"], ctx.S["ga"].n)])
        fin2 = emit_mamba_m2(ctx, variant, xhat, wiscr.ap(), bin_sb, cwp,
                             cb_sb, dtb_sb, acol, dbc, rsb, mtr, rfr2,
                             gate_d.ap(), yp_d.ap(), csp_d.ap(), sd_d.ap(),
                             spack_d.ap(), ctile.ap(), dep=dep + cv)
        ag2 = _emit_ag(ctx, spack_d, sag_d, fin2)
        _barrier(ctx, list(fin2) + [ag2])
        ctx.pop()
        ctx.pop()
        _ln_free()
        _scp.__exit__(None, None, None)
        ctx.push()
        dep3 = []
        nw_sb, _ = emit_load_colvec(ctx, iv["nw"], NIC, "nw")
        if cfgm["out_bias"]:
            bo_sb, dep_b = emit_load_colvec(ctx, iv["bo"], KC, "bo")
            dep3.append(dep_b)
        else:
            bo_sb = None
        cpe, dep0 = emit_load_tile(ctx, cpi, [16, 16], "cpes", BF16)
        pmb, dep1 = emit_load_tile(ctx, pmbi, [64, 64], "pmbs", F32)
        zmk, dep2_ = emit_load_tile(ctx, zmki, [64, 64], "zmks", F32)
        rfr, dep3_ = emit_load_tile(ctx, ralli, [64, 4096], "rfrs", BF16)
        dep3 += [dep0, dep1, dep2_, dep3_]
        cv2 = emit_convert(ctx, iv["wo"], woscr.ap(), NIC, KC, F32)
        fin3 = emit_mamba_m3(ctx, variant, resid, sag_d.ap(), pmb, zmk, rfr,
                             cpe, gate_d.ap(), yp_d.ap(), csp_d.ap(),
                             sd_d.ap(), woscr.ap(), nw_sb, bo_sb,
                             dep=dep3 + cv2, xscr=xhat)
        _barrier(ctx, fin3)
        ctx.pop()
        return fin3

    with L.nc.named_scope("attn_a"):
        attn_phase("a", wq_a, wk_a, wv_a, wo_a)
    with L.nc.named_scope("ffnq1"):
        ffnq_phase(True)
    with L.nc.named_scope("attn_b"):
        attn_phase("b", wq_b, wk_b, wv_b, wo_b, bq=bq_b, bk=bk_b, bv=bv_b,
                   bo=bo_b)
    with L.nc.named_scope("ffnq2"):
        ffnq_phase(False)
    with L.nc.named_scope("mamba_a"):
        mamba_phase("a")
    with L.nc.named_scope("ffn16_1"):
        f6 = ffn16_phase(True)
    with L.nc.named_scope("mamba_b"):
        mamba_phase("b")
    with L.nc.named_scope("ffn16_2"):
        f8 = ffn16_phase(False)
    emit_resid_out(ctx, resid, rout, dep=[f8["done"]])
    return L.finish()


def _run(nc, in_maps):
    res = run_bass_kernel_spmd(nc, in_maps, core_ids=list(range(NCORE)))
    return res.results

def kernel_unfused(**inp):
    ident = np.eye(128, dtype=np.float32)
    cos_t, sinm_t = _rope_tables()
    eqc = host_eq()
    x = np.asarray(inp["L0_attn_a_x"], np.float32)
    shards = shard_resid(x)
    percore = [dict(
        masks=host_masks(c), hsel=host_hsel(c), pmb=host_pmb(c),
        zmk=host_zmk(c),
        cos=np.ascontiguousarray(cos_t[core_tokens(c)].T),
        sinm=np.ascontiguousarray(sinm_t[core_tokens(c)].T),
    ) for c in range(NCORE)]
    rall, mtri, cperm = host_rall(), host_mtri(), host_cperm()

    def attn(variant, wq, wk, wv, wo, bq=None, bk=None, bv=None, bo=None):
        nonlocal shards
        nc1 = build_attn_kv_launch(variant)
        m1 = []
        for c in range(NCORE):
            d = dict(rin=shards[c], wk=wk, wv=wv, ident=ident)
            if variant == "b":
                d.update(bk=bk, bv=bv)
            else:
                d.update(cos=percore[c]["cos"], sinm=percore[c]["sinm"])
            m1.append(d)
        r1 = _run(nc1, m1)
        nc2 = build_attn_core_launch(variant)
        m2 = []
        for c in range(NCORE):
            g0 = (c // GROUP) * GROUP
            d = dict(rin=shards[c], wq=wq, wo=wo,
                     kag=np.concatenate([r1[i]["kpack"] for i in range(g0, g0 + 4)], 0),
                     vag=np.concatenate([r1[i]["vpack"] for i in range(g0, g0 + 4)], 0),
                     masks=percore[c]["masks"], eqc=eqc, ident=ident)
            if variant == "b":
                d.update(bq=bq, bo=bo)
            else:
                d.update(cos=percore[c]["cos"], sinm=percore[c]["sinm"])
            m2.append(d)
        r2 = _run(nc2, m2)
        shards = [r2[c]["rout"] for c in range(NCORE)]

    def ffnq(gw, uw, dw, gs, us, ds):
        nonlocal shards
        nc = build_ffnq_launch()
        m = [dict(rin=shards[c], gwt=gw, uwt=uw, dwt=dw, gsc=gs, usc=us,
                  dsc=ds, ident=ident) for c in range(NCORE)]
        r = _run(nc, m)
        shards = [r[c]["rout"] for c in range(NCORE)]

    def ffn16(w0t, w1t, b0, b1):
        nonlocal shards
        nc = build_ffn16_launch()
        m = [dict(rin=shards[c], w0t=w0t, w1t=w1t, b0=b0, b1=b1, ident=ident)
             for c in range(NCORE)]
        r = _run(nc, m)
        shards = [r[c]["rout"] for c in range(NCORE)]

    def mamba(variant, wi, cwt, dtb, alog, dvec, wo, nw, bin_=None, cb=None,
              bo=None):
        nonlocal shards
        nc1 = build_m1_launch()
        r1 = _run(nc1, [dict(rin=shards[c], ident=ident) for c in range(NCORE)])
        nc2 = build_m2_launch(variant)
        m2 = []
        for c in range(NCORE):
            g0 = (c // GROUP) * GROUP
            d = dict(rin=shards[c], wi=wi, cwt=cwt, alog=alog, dtb=dtb,
                     dvec=dvec,
                     hgat=np.concatenate([r1[i]["hpack"] for i in range(g0, g0 + 4)], 0),
                     hsel=percore[c]["hsel"], rall=rall, mtri=mtri, ident=ident)
            if variant == "a":
                d["bin"] = bin_
            else:
                d["cb"] = cb
            m2.append(d)
        r2 = _run(nc2, m2)
        nc3 = build_m3_launch(variant)
        m3 = []
        for c in range(NCORE):
            g0 = (c // GROUP) * GROUP
            d = dict(rin=shards[c], wo=wo, nw=nw,
                     sag=np.concatenate([r2[i]["spack"] for i in range(g0, g0 + 4)], 0),
                     gate_sp=r2[c]["gate_sp"], yp_sp=r2[c]["yp_sp"],
                     c_sp=r2[c]["c_sp"], sd_sp=r2[c]["sd_sp"],
                     pmb=percore[c]["pmb"], zmk=percore[c]["zmk"],
                     rfree=rall, cperm=cperm, ident=ident)
            if variant == "a":
                d["bo"] = bo
            m3.append(d)
        r3 = _run(nc3, m3)
        shards = [r3[c]["rout"] for c in range(NCORE)]

    f32 = lambda k: np.asarray(inp[k], np.float32)
    attn("a", _wlay(f32("L0_attn_a_Wq"), KC, KC),
         _wlay(f32("L0_attn_a_Wk"), KC, 4), _wlay(f32("L0_attn_a_Wv"), KC, 4),
         _wlay(f32("L0_attn_a_Wo"), KC, KC))
    gw = _wlay(_u8_view(np.asarray(inp["L0_ffn_gate_w"])).T, KC, 64)
    uw = _wlay(_u8_view(np.asarray(inp["L0_ffn_up_w"])).T, KC, 64)
    dw = _wlay(_u8_view(np.asarray(inp["L0_ffn_down_w"])).T, 64, KC)
    ffnq(gw, uw, dw, f32("L0_ffn_gate_scale"), f32("L0_ffn_up_scale"),
         f32("L0_ffn_down_scale"))
    wqkv = f32("L0_attn_b_w_qkv")
    bqkv = f32("L0_attn_b_b_qkv")
    attn("b", _wlay(wqkv[:, 0:2048], KC, KC),
         _wlay(wqkv[:, 2048:4096], KC, KC),
         _wlay(wqkv[:, 4096:6144], KC, KC),
         _wlay(f32("L0_attn_b_w_out"), KC, KC),
         bq=np.ascontiguousarray(bqkv[0:2048]),
         bk=np.ascontiguousarray(bqkv[2048:4096]),
         bv=np.ascontiguousarray(bqkv[4096:6144]), bo=f32("L0_attn_b_b_out"))
    ffnq(gw, uw, dw, f32("L0_ffn_gate_scale"), f32("L0_ffn_up_scale"),
         f32("L0_ffn_down_scale"))
    wia = np.zeros((D, MMC * 128), np.float32)
    wia[:, :8512] = f32("L1_attn_a_in_proj_w").T
    bia = np.zeros(MMC * 128, np.float32)
    bia[:8512] = f32("L1_attn_a_in_proj_b")
    pad128 = lambda v: np.concatenate([v, np.zeros(64, np.float32)])
    pad128hi = lambda v: np.concatenate([np.zeros(64, np.float32), v])
    mamba("a", _wlay(wia, KC, MMC),
          np.ascontiguousarray(f32("L1_attn_a_conv_w").T),
          pad128hi(f32("L1_attn_a_dt_bias")), pad128(f32("L1_attn_a_A_log")),
          f32("L1_attn_a_D"), _wlay(f32("L1_attn_a_out_proj_w").T, NIC, KC),
          f32("L1_attn_a_norm_w"), bin_=bia, bo=f32("L1_attn_a_out_proj_b"))
    w0t = _wlay(f32("L1_ffn_W0").T, KC, 64)
    w1t = _wlay(f32("L1_ffn_W1").T, 64, KC)
    ffn16(w0t, w1t, f32("L1_ffn_bias0"), f32("L1_ffn_bias1"))
    wib = np.zeros((D, MMC * 128), np.float32)
    wib[:, :8512] = f32("L1_attn_b_in_proj_w")
    mamba("b", _wlay(wib, KC, MMC),
          np.ascontiguousarray(f32("L1_attn_b_conv_w").T),
          pad128hi(f32("L1_attn_b_dt_bias")), pad128(f32("L1_attn_b_A_log")),
          f32("L1_attn_b_D"), _wlay(f32("L1_attn_b_out_proj_w"), NIC, KC),
          f32("L1_attn_b_norm_weight"), cb=f32("L1_attn_b_conv_b"))
    ffn16(w0t, w1t, f32("L1_ffn_bias0"), f32("L1_ffn_bias1"))
    return unshard_resid(shards)

def kernel(**inp):
    f32 = lambda k: np.asarray(inp[k], np.float32)
    ident = np.eye(128, dtype=np.float32)
    cos_t, sinm_t = _rope_tables()
    x = f32("L0_attn_a_x")
    shards = shard_resid(x)
    wqkv = f32("L0_attn_b_w_qkv")
    bqkv = f32("L0_attn_b_b_qkv")
    wia = np.zeros((D, MMC * 128), np.float32)
    wia[:, :8512] = f32("L1_attn_a_in_proj_w").T
    bia = np.zeros(MMC * 128, np.float32)
    bia[:8512] = f32("L1_attn_a_in_proj_b")
    wib = np.zeros((D, MMC * 128), np.float32)
    wib[:, :8512] = f32("L1_attn_b_in_proj_w")
    pad128 = lambda v: np.concatenate([v, np.zeros(64, np.float32)])
    pad128hi = lambda v: np.concatenate([np.zeros(64, np.float32), v])
    common = dict(
        ident=ident, eqc=host_eq(),
        wq_a=_wlay(f32("L0_attn_a_Wq"), KC, KC),
        wk_a=_wlay(f32("L0_attn_a_Wk"), KC, 4),
        wv_a=_wlay(f32("L0_attn_a_Wv"), KC, 4),
        wo_a=_wlay(f32("L0_attn_a_Wo"), KC, KC),
        gwt=_wlay(_u8_view(np.asarray(inp["L0_ffn_gate_w"])).T, KC, 64),
        uwt=_wlay(_u8_view(np.asarray(inp["L0_ffn_up_w"])).T, KC, 64),
        dwt=_wlay(_u8_view(np.asarray(inp["L0_ffn_down_w"])).T, 64, KC),
        gsc=f32("L0_ffn_gate_scale"), usc=f32("L0_ffn_up_scale"),
        dsc=f32("L0_ffn_down_scale"),
        wq_b=_wlay(wqkv[:, 0:2048], KC, KC),
        wk_b=_wlay(wqkv[:, 2048:4096], KC, KC),
        wv_b=_wlay(wqkv[:, 4096:6144], KC, KC),
        wo_b=_wlay(f32("L0_attn_b_w_out"), KC, KC),
        bq_b=np.ascontiguousarray(bqkv[0:2048]),
        bk_b=np.ascontiguousarray(bqkv[2048:4096]),
        bv_b=np.ascontiguousarray(bqkv[4096:6144]),
        bo_b=f32("L0_attn_b_b_out"),
        w0t=_wlay(f32("L1_ffn_W0").T, KC, 64),
        w1t=_wlay(f32("L1_ffn_W1").T, 64, KC),
        b0=f32("L1_ffn_bias0"), b1=f32("L1_ffn_bias1"),
        rall=host_rall(), mtri=host_mtri(), cperm=host_cperm(),
        wi_a=_wlay(wia, KC, MMC),
        cwt_a=np.ascontiguousarray(f32("L1_attn_a_conv_w").T),
        alog_a=pad128(f32("L1_attn_a_A_log")),
        dtb_a=pad128hi(f32("L1_attn_a_dt_bias")),
        dvec_a=f32("L1_attn_a_D"), nw_a=f32("L1_attn_a_norm_w"),
        wom_a=_wlay(f32("L1_attn_a_out_proj_w").T, NIC, KC),
        bin_a=bia, bom_a=f32("L1_attn_a_out_proj_b"),
        wi_b=_wlay(wib, KC, MMC),
        cwt_b=np.ascontiguousarray(f32("L1_attn_b_conv_w").T),
        alog_b=pad128(f32("L1_attn_b_A_log")),
        dtb_b=pad128hi(f32("L1_attn_b_dt_bias")),
        dvec_b=f32("L1_attn_b_D"), nw_b=f32("L1_attn_b_norm_weight"),
        wom_b=_wlay(f32("L1_attn_b_out_proj_w"), NIC, KC),
        cb_b=f32("L1_attn_b_conv_b"))
    nc = build_mega()
    in_maps = [dict(rin=shards[c], masks=host_masks(c), hsel=host_hsel(c),
                    pmb=host_pmb(c), zmk=host_zmk(c),
                    cos=np.ascontiguousarray(cos_t[core_tokens(c)].T),
                    sinm=np.ascontiguousarray(sinm_t[core_tokens(c)].T),
                    **common) for c in range(NCORE)]
    res = run_bass_kernel_spmd(nc, in_maps, core_ids=list(range(NCORE)))
    return unshard_resid([res.results[c]["rout"] for c in range(NCORE)])
